# version 02
# baseline (speedup 1.0000x reference)
import numpy as np
import ml_dtypes

import concourse.bass as bass
import concourse.mybir as mybir
from concourse.bass_utils import run_bass_kernel_spmd

F32 = mybir.dt.float32
BF16 = mybir.dt.bfloat16
I32 = mybir.dt.int32
AF = mybir.ActivationFunctionType
OP = mybir.AluOpType

NCORES = 8
DM = 2048
NT = 2048
HALO = 4
NTH = NT + HALO
DC = DM // 128
TT = 512
TILES_ALL = [(0, TT), (TT, TT), (2 * TT, TT), (3 * TT, TT), (4 * TT, HALO)]
TILES_MAIN = [(HALO + i * TT, TT) for i in range(4)]

D_INNER = 4096
NHEADS = 64
HEADDIM = 64
D_STATE = 128
CHUNK = 64
NCHUNK = NT // CHUNK
INTER = 8192

ENGINES = ("tensor", "scalar", "vector", "gpsimd", "sync")

BF = ml_dtypes.bfloat16


class Prog:
    def __init__(self, nc, ctx):
        self.nc = nc
        self._ctx = ctx
        self.ops = {e: [] for e in ENGINES}
        self.sems = {}
        self.counts = {}
        self.sbufs = {}
        self.psums = {}
        self.bar_tgt = 0

    def sem(self, name):
        if name not in self.sems:
            self.sems[name] = self._ctx.enter_context(self.nc.semaphore(name))
            self.counts[name] = 0
        return self.sems[name]

    def sbuf(self, name, shape, dtype):
        if name not in self.sbufs:
            self.sbufs[name] = self._ctx.enter_context(self.nc.sbuf_tensor(name, shape, dtype))
        return self.sbufs[name]

    def bank(self, i):
        nm = f"pb{i}"
        if nm not in self.psums:
            self.psums[nm] = self._ctx.enter_context(self.nc.psum_tensor(nm, [128, 512], F32))
        return self.psums[nm]

    def bank_bf(self, i):
        return self.bank(i).ap().bitcast(BF16)

    def emit(self, eng, fn, waits=(), inc=None, amt=1, is_dma=False):
        if inc:
            self.sem(inc)
        for s, _ in waits:
            self.sem(s)
        new = None
        if inc is not None:
            self.counts[inc] += amt
            new = self.counts[inc]
        hard = (not is_dma) and eng in ("scalar", "vector", "gpsimd")

        def closure(e, fn=fn, waits=tuple(waits), inc=inc, amt=amt, P=self, hard=hard):
            for s, v in waits:
                if v > 0:
                    e.wait_ge(P.sems[s], v)
            inst = fn(e)
            if inc is not None:
                if inst is None:
                    e.sem_inc(P.sems[inc], amt)
                elif hard:
                    e.drain()
                    e.sem_inc(P.sems[inc], amt)
                else:
                    inst.then_inc(P.sems[inc], amt)

        self.ops[eng].append(closure)
        return new

    def dma(self, q, out, in_, sem, waits=(), **kw):
        return self.emit(q, lambda e: e.dma_start(out=out, in_=in_, **kw), waits, sem, 16,
                         is_dma=True)

    def now(self, sem):
        return (sem, self.counts.get(sem, 0))

    def barrier(self):
        self.sem("bar")
        dma_waits = tuple((s, self.counts[s]) for s in self.counts if s.startswith("D_"))
        self.counts["bar"] += len(ENGINES)
        tgt = self.counts["bar"]
        for e in ENGINES:
            def closure(eng, dw=dma_waits, tgt=tgt, P=self, e=e):
                for s, v in dw:
                    if v > 0:
                        eng.wait_ge(P.sems[s], v)
                if e != "sync":
                    eng.drain()
                eng.sem_inc(P.sems["bar"], 1)
                eng.wait_ge(P.sems["bar"], tgt)
            self.ops[e].append(closure)

    def finish(self):
        nc = self.nc
        with nc.Block() as block:
            @block.tensor
            def _(t):
                for f in self.ops["tensor"]:
                    f(t)

            @block.scalar
            def _(s):
                for f in self.ops["scalar"]:
                    f(s)

            @block.vector
            def _(v):
                for f in self.ops["vector"]:
                    f(v)

            @block.gpsimd
            def _(g):
                for f in self.ops["gpsimd"]:
                    f(g)

            @block.sync
            def _(s):
                for f in self.ops["sync"]:
                    f(s)


class Carve:

    def __init__(self, handle, elems):
        self.h = handle
        self.off = 0
        self.n = elems

    def take(self, elems, dtype=BF16):
        e2 = elems if dtype == BF16 else 2 * elems
        off = (self.off + 1) & ~1
        self.off = off + e2
        assert self.off <= self.n, f"arena overflow {self.off} > {self.n}"
        v = self.h[:, off:off + e2]
        if dtype != BF16:
            v = v.bitcast(dtype)
        return v


def mm_group(P, out_ap, pairs, waits=(), inc="pe"):
    n = len(pairs)
    last = None
    for j, (l, r) in enumerate(pairs):
        last = P.emit(
            "tensor",
            lambda t, l=l, r=r, s=(j == 0), st=(j == n - 1), o=out_ap: t.matmul(o, l, r, start=s, stop=st),
            waits=(waits if j == 0 else ()),
            inc=(inc if j == n - 1 else None),
        )
    return last


CONST_SPECS = [
    ("had128", [128, 128], BF16), ("id128b", [128, 128], BF16),
    ("negid128b", [128, 128], BF16), ("rowsel", [64, 64 * 64], BF16),
    ("onescol_b", [128, 1], BF16),
    ("onesrow_f", [1, 128], F32), ("id128f", [128, 128], F32),
    ("triu64", [64, 64], F32), ("oddcol", [128, 1], F32),
    ("fconst", [128, 4], F32),
]


def emit_consts(P, t):
    for nm, shape, dt in CONST_SPECS:
        dst = P.sbuf("c_" + nm, shape, dt)
        P.dma("gpsimd", dst[:, :], t[nm].ap(), "D_init")


def build_colset(P, name, vec_dram, n, post_scale=None):
    k = n // 128
    assert k <= 128
    stage = P.sbuf("cs_stage", [128, 128], F32)
    out = P.sbuf(f"col_{name}", [128, k], F32)
    ptr = P.bank(7)
    id128f = P.sbufs["c_id128f"]
    vap = vec_dram if isinstance(vec_dram, bass.AP) else vec_dram.ap()
    ld = P.dma("gpsimd", stage[0:k, :], vap.rearrange("(k p) -> k p", p=128),
               "D_cs", waits=[P.now("pe")])
    P.emit("tensor", lambda t: t.transpose(ptr[0:128, 0:k], stage[0:k, :], id128f[0:k, 0:k]),
           waits=[("D_cs", ld), P.now("dve")], inc="pe")
    sc = 1.0 if post_scale is None else float(post_scale)
    P.emit("vector", lambda v: v.tensor_scalar(out[:, :], ptr.ap()[0:128, 0:k], sc, None, OP.mult),
           waits=[P.now("pe")], inc="dve")
    return out


def emit_ln(P, xin, hln_write, su_col=None, tiles=TILES_ALL, tag="ln", ar1=None, ar2=None,
            tile_done=None, extra_waits=None):
    xb = [ar1.take(DC * TT), ar2.take(DC * TT)]
    sq = [ar1.take(DC * TT), ar2.take(DC * TT)]
    rows = ar2.take(4 * TT, F32)[0:1, :]
    sub_s = ar2.take(TT, F32)
    onescol_b = P.sbufs["c_onescol_b"]
    onesrow_f = P.sbufs["c_onesrow_f"]
    sx = [P.bank(0).ap()[0:1, :], P.bank(1).ap()[0:1, :]]
    sxx = [P.bank(2).ap()[0:1, :], P.bank(3).ap()[0:1, :]]
    mubc = [P.bank(4).ap(), P.bank(5).ap()]
    rbc = [P.bank(6).ap(), P.bank(7).ap()]

    for ti, (t0, W) in enumerate(tiles):
        par = ti % 2
        dsem = f"D_x{par}"
        ld = P.dma("gpsimd", xb[par][:, 0:DC * W].rearrange("p (c w) -> p c w", c=DC),
                   xin.ap()[:, t0:t0 + W].rearrange("(c p) w -> p c w", p=128),
                   dsem, waits=[P.now("dve")])
        sq_done = P.emit("scalar", lambda s, par=par, W=W: s.activation(
            sq[par][:, 0:DC * W], xb[par][:, 0:DC * W], AF.Square),
            waits=[(dsem, ld)], inc="act")
        pe_w = [(dsem, ld), ("act", sq_done), P.now("dve")]
        for c in range(DC):
            P.emit("tensor", lambda t, par=par, c=c, W=W: t.matmul(
                sx[par][0:1, 0:W], onescol_b[:, 0:1], xb[par][:, c * W:c * W + W],
                start=(c == 0), stop=(c == DC - 1)),
                waits=(pe_w if c == 0 else ()), inc=("pe" if c == DC - 1 else None))
        for c in range(DC):
            P.emit("tensor", lambda t, par=par, c=c, W=W: t.matmul(
                sxx[par][0:1, 0:W], onescol_b[:, 0:1], sq[par][:, c * W:c * W + W],
                start=(c == 0), stop=(c == DC - 1)),
                inc=("pe" if c == DC - 1 else None))
        pe_stats = P.counts["pe"]
        P.emit("vector", lambda v, par=par, W=W: v.tensor_scalar(
            rows[0:1, 0:W], sx[par][0:1, 0:W], 1.0 / DM, None, OP.mult),
            waits=[("pe", pe_stats)], inc="dve")
        P.emit("vector", lambda v, par=par, W=W: v.tensor_scalar(
            rows[0:1, TT:TT + W], sxx[par][0:1, 0:W], 1.0 / DM, None, OP.mult), inc="dve")
        P.emit("vector", lambda v, W=W: v.tensor_mul(
            rows[0:1, 2 * TT:2 * TT + W], rows[0:1, 0:W], rows[0:1, 0:W]), inc="dve")
        dve_fin = P.emit("vector", lambda v, W=W: v.tensor_sub(
            rows[0:1, 2 * TT:2 * TT + W], rows[0:1, TT:TT + W], rows[0:1, 2 * TT:2 * TT + W]),
            inc="dve")
        fc = P.sbufs["c_fconst"]
        P.emit("scalar", lambda s, W=W, fc=fc: s.activation(
            rows[0:1, TT:TT + W], rows[0:1, 2 * TT:2 * TT + W], AF.Ln, bias=fc[0:1, 1:2]),
            waits=[("dve", dve_fin)], inc="act")
        act_r = P.emit("scalar", lambda s, W=W: s.activation(
            rows[0:1, 2 * TT:2 * TT + W], rows[0:1, TT:TT + W], AF.Exp, scale=-0.5), inc="act")
        P.emit("tensor", lambda t, par=par, W=W: t.matmul(
            mubc[par][:, 0:W], onesrow_f[0:1, :], rows[0:1, 0:W], start=True, stop=True),
            waits=[("act", act_r), P.now("dve")], inc="pe")
        pe_bc = P.emit("tensor", lambda t, par=par, W=W: t.matmul(
            rbc[par][:, 0:W], onesrow_f[0:1, :], rows[0:1, 2 * TT:2 * TT + W],
            start=True, stop=True), inc="pe")
        ew = list(extra_waits(ti)) if extra_waits else []
        for c in range(DC):
            P.emit("vector", lambda v, par=par, c=c, W=W: v.tensor_sub(
                sub_s[:, 0:W], xb[par][:, c * W:c * W + W], mubc[par][:, 0:W]),
                waits=(([("pe", pe_bc)] + ew) if c == 0 else ()), inc="dve")
            dst = hln_write(c, t0, W)
            if su_col is None:
                P.emit("vector", lambda v, dst=dst, par=par, W=W: v.tensor_mul(
                    dst, sub_s[:, 0:W], rbc[par][:, 0:W]), inc="dve")
            else:
                P.emit("vector", lambda v, dst=dst, par=par, c=c, W=W: v.scalar_tensor_tensor(
                    dst, sub_s[:, 0:W], su_col[:, c:c + 1], rbc[par][:, 0:W],
                    OP.mult, OP.mult), inc="dve")
        if tile_done is not None:
            tile_done(ti, t0, W)


def emit_ffn0(P, t, xin, xout, tag, tiles=TILES_ALL):
    A = Carve(P.sbufs["arA"], ARENA_A)
    B_ = Carve(P.sbufs["arB"], ARENA_B)
    H = Carve(P.sbufs["arH"], ARENA_H)
    gsc, usc, dsc = P.sbufs["col_l0g"], P.sbufs["col_l0u"], P.sbufs["col_l0d"]

    hln = Carve(P.sbufs["arH"], ARENA_H).take(DC * NTH)

    def hw(c, t0, W):
        return hln[:, c * NTH + t0: c * NTH + t0 + W]

    emit_ln(P, xin, hw, tiles=tiles, tag=f"{tag}a", ar1=A, ar2=B_)
    P.barrier()

    A = Carve(P.sbufs["arA"], ARENA_A)
    B_ = Carve(P.sbufs["arB"], ARENA_B)
    wg = [A.take(DC * TT), B_.take(DC * TT)]
    wu = [A.take(DC * TT), B_.take(DC * TT)]
    sg_sb = [A.take(TT), A.take(TT)]
    h_sb = [A.take(TT) for _ in range(4)]
    hmid = t["hmid"]
    wgT, wuT = t["L0_ffn_gate_wT"], t["L0_ffn_up_wT"]

    bank_free = {}
    slot_free = {}
    hs_free = {}
    sg_free = {}
    for ob in range(16):
        wpar = ob % 2
        dsem = f"D_w{wpar}"
        for (wt, dst) in ((wgT, wg[wpar]), (wuT, wu[wpar])):
            ld = P.dma("gpsimd", dst[:, :].rearrange("p (c w) -> p c w", c=DC),
                       wt.ap()[:, ob * TT:(ob + 1) * TT].rearrange("(c p) w -> p c w", p=128),
                       dsem, waits=[("pe", slot_free.get(wpar, 0))])
        sh = None
        fc = P.sbufs["c_fconst"]
        for dst in (wg[wpar], wu[wpar]):
            sh = P.emit("scalar", lambda sc, d=dst, fc=fc: sc.activation(
                d[:, :], d[:, :], AF.Identity, bias=fc[:, 0:1]),
                waits=[(dsem, ld)], inc="act")
        for ti, (t0, W) in enumerate(tiles):
            for oc in range(4):
                bsel = (ti * 4 + oc) % 2
                bg, bu = P.bank(bsel).ap(), P.bank(2 + bsel).ap()
                og = ob * 4 + oc
                gw = wg[wpar][:, :].rearrange("p (c w) -> p c w", c=DC)
                uw = wu[wpar][:, :].rearrange("p (c w) -> p c w", c=DC)
                mm_group(P, bg[:, 0:W],
                         [(gw[:, c, oc * 128:(oc + 1) * 128], hw(c, t0, W)) for c in range(DC)],
                         waits=[("act", sh), bank_free.get(bsel, ("pe", 0))])
                pe_done = mm_group(P, bu[:, 0:W],
                                   [(uw[:, c, oc * 128:(oc + 1) * 128], hw(c, t0, W)) for c in range(DC)],
                                   waits=[bank_free.get(2 + bsel, ("pe", 0))])
                sgp = (ti * 4 + oc) % 2
                a = P.emit("scalar", lambda s, bg=bg, og=og, W=W, sgp=sgp: s.activation(
                    sg_sb[sgp][:, 0:W], bg[:, 0:W], AF.Silu, scale=gsc[:, og:og + 1]),
                    waits=[("pe", pe_done), ("dve", sg_free.get(sgp, 0))], inc="act")
                bank_free[bsel] = ("act", a)
                hp = (ti * 4 + oc) % 4
                d = P.emit("vector", lambda v, bu=bu, og=og, W=W, sgp=sgp, hp=hp: v.scalar_tensor_tensor(
                    h_sb[hp][:, 0:W], sg_sb[sgp][:, 0:W], usc[:, og:og + 1], bu[:, 0:W],
                    OP.mult, OP.mult),
                    waits=[("act", a), (f"D_h{hp}", hs_free.get(hp, 0))], inc="dve")
                bank_free[2 + bsel] = ("dve", d)
                sg_free[sgp] = d
                st = P.dma("sync", hmid.ap()[og * 128:(og + 1) * 128, t0:t0 + W],
                           h_sb[hp][:, 0:W], f"D_h{hp}", waits=[("dve", d)])
                hs_free[hp] = st
        slot_free[wpar] = P.counts["pe"]
    P.barrier()

    A = Carve(P.sbufs["arA"], ARENA_A)
    B_ = Carve(P.sbufs["arB"], ARENA_B)
    H = Carve(P.sbufs["arH"], ARENA_H)
    OBW = 256
    wd = [A.take(64 * OBW), B_.take(64 * OBW)]
    hq = [H.take(DC * TT), H.take(DC * TT)]
    xi = [H.take(2 * TT, F32), H.take(2 * TT, F32)]
    xo = [H.take(2 * TT, F32), H.take(2 * TT, F32)]
    wdT = t["L0_ffn_down_wT"]

    bank_free = {}
    slot_free = {}
    xo_free = {}
    xi_free = {}
    hq_free = {}
    for ob in range(8):
        wpar = ob % 2
        dsem = f"D_wd{wpar}"
        ld = P.dma("gpsimd", wd[wpar][:, :].rearrange("p (c w) -> p c w", c=64),
                   wdT.ap()[:, ob * OBW:(ob + 1) * OBW].rearrange("(c p) w -> p c w", p=128),
                   dsem, waits=[("pe", slot_free.get(wpar, 0))])
        fc = P.sbufs["c_fconst"]
        sh = P.emit("scalar", lambda sc, d=wd[wpar], fc=fc: sc.activation(
            d[:, :], d[:, :], AF.Identity, bias=fc[:, 0:1]),
            waits=[(dsem, ld)], inc="act")
        for ti, (t0, W) in enumerate(tiles):
            tb = (ti % 2) * 2
            hpar = ti % 2
            xsem = f"D_xi{hpar}"
            xl = P.dma("gpsimd", xi[hpar][:, 0:2 * W].rearrange("p (c w) -> p c w", c=2),
                       xin.ap()[ob * OBW:(ob + 1) * OBW, t0:t0 + W].rearrange("(c p) w -> p c w", p=128),
                       xsem, waits=[("dve", xi_free.get(hpar, 0))])
            hsem = f"D_hq{hpar}"
            wdv = wd[wpar][:, :].rearrange("p (c w) -> p c w", c=64)
            for kq in range(4):
                hl = P.dma("gpsimd", hq[hpar][:, 0:DC * W].rearrange("p (c w) -> p c w", c=DC),
                           hmid.ap()[kq * 2048:(kq + 1) * 2048, t0:t0 + W].rearrange("(c p) w -> p c w", p=128),
                           hsem, waits=[("pe", hq_free.get(hpar, 0) if kq == 0 else P.counts["pe"])])
                hqv = hq[hpar][:, 0:DC * W].rearrange("p (c w) -> p c w", c=DC)
                for oc in range(2):
                    wts = [(hsem, hl)]
                    if kq == 0:
                        wts += [("act", sh), bank_free.get(tb + oc, ("pe", 0))]
                    for c in range(DC):
                        gk = kq * DC + c
                        P.emit("tensor", lambda t_, gk=gk, oc=oc, c=c, W=W, wdv=wdv, hqv=hqv, tb=tb:
                               t_.matmul(P.bank(tb + oc).ap()[:, 0:W],
                                         wdv[:, gk, oc * 128:(oc + 1) * 128], hqv[:, c, :],
                                         start=(gk == 0), stop=(gk == 63)),
                               waits=(wts if c == 0 else ()),
                               inc=("pe" if c == DC - 1 else None))
            hq_free[hpar] = P.counts["pe"]
            pe_done = P.counts["pe"]
            for oc in range(2):
                og = ob * 2 + oc
                d = P.emit("vector", lambda v, oc=oc, og=og, W=W, hpar=hpar, tb=tb: v.scalar_tensor_tensor(
                    xo[hpar][:, oc * W:oc * W + W], P.bank(tb + oc).ap()[:, 0:W], dsc[:, og:og + 1],
                    xi[hpar][:, oc * W:oc * W + W], OP.mult, OP.add),
                    waits=[("pe", pe_done), (xsem, xl), (f"D_xo{hpar}", xo_free.get(hpar, 0))],
                    inc="dve")
                bank_free[tb + oc] = ("dve", d)
            xi_free[hpar] = P.counts["dve"]
            st = P.dma("sync", xout.ap()[ob * OBW:(ob + 1) * OBW, t0:t0 + W].rearrange("(c p) w -> p c w", p=128),
                       xo[hpar][:, 0:2 * W].rearrange("p (c w) -> p c w", c=2),
                       f"D_xo{hpar}", waits=[P.now("dve")])
            xo_free[hpar] = st
        slot_free[wpar] = P.counts["pe"]
    P.barrier()


ARENA_A = 24576
ARENA_B = 24576
ARENA_H = 45184


def alloc_arenas(P):
    P.sbuf("arA", [128, ARENA_A], BF16)
    P.sbuf("arB", [128, ARENA_B], BF16)
    P.sbuf("arH", [128, ARENA_H], BF16)


def _hadamard(n):
    H = np.array([[1.0]], dtype=np.float64)
    while H.shape[0] < n:
        H = np.block([[H, H], [H, -H]])
    return H


def host_constants():
    c = {}
    c["had128"] = _hadamard(128).astype(BF)
    c["id128b"] = np.eye(128, dtype=np.float32).astype(BF)
    c["negid128b"] = (-np.eye(128, dtype=np.float32)).astype(BF)
    rs = np.zeros((64, 64 * 64), dtype=np.float32)
    for h in range(64):
        rs[h, h * 64:(h + 1) * 64] = 1.0
    c["rowsel"] = rs.astype(BF)
    c["onescol_b"] = np.ones((128, 1), dtype=np.float32).astype(BF)
    c["onesrow_f"] = np.ones((1, 128), dtype=np.float32)
    c["id128f"] = np.eye(128, dtype=np.float32)
    c["triu64"] = np.triu(np.ones((64, 64), dtype=np.float32))
    fc = np.zeros((128, 4), np.float32)
    fc[:, 0] = -128.0
    fc[:, 1] = 1e-5
    fc[:, 2] = 1.0
    fc[:, 3] = 0.5
    c["fconst"] = fc
    return c


def shard_x(x):
    shards = []
    for k in range(NCORES):
        b, hf = k // 2, k % 2
        main = x[b, hf * NT:(hf + 1) * NT]
        if hf == 0:
            halo = np.zeros((HALO, DM), x.dtype)
        else:
            halo = x[b, NT - HALO:NT]
        xt = np.concatenate([halo, main], axis=0)
        shards.append(np.ascontiguousarray(xt.T))
    return shards


def host_prep_common(inputs):
    d = {}
    d.update(host_constants())
    d["oddcol"] = [np.full((128, 1), float(k % 2), np.float32) for k in range(NCORES)]
    d["L0_ffn_gate_wT"] = np.ascontiguousarray(inputs["L0_ffn_gate_w"].T)
    d["L0_ffn_up_wT"] = np.ascontiguousarray(inputs["L0_ffn_up_w"].T)
    d["L0_ffn_down_wT"] = np.ascontiguousarray(inputs["L0_ffn_down_w"].T)
    d["L0_ffn_gate_scale"] = inputs["L0_ffn_gate_scale"]
    d["L0_ffn_up_scale"] = inputs["L0_ffn_up_scale"]
    d["L0_ffn_down_scale"] = inputs["L0_ffn_down_scale"]
    return d


def declare_ffn0_tensors(nc, kind_x="ExternalInput", kind_y="ExternalOutput"):
    t = {}
    for nm, shape, dt in CONST_SPECS:
        t[nm] = nc.dram_tensor(nm, shape, dt, kind="ExternalInput")
    t["L0_ffn_gate_wT"] = nc.dram_tensor("L0_ffn_gate_wT", [DM, INTER], I32, kind="ExternalInput")
    t["L0_ffn_up_wT"] = nc.dram_tensor("L0_ffn_up_wT", [DM, INTER], I32, kind="ExternalInput")
    t["L0_ffn_down_wT"] = nc.dram_tensor("L0_ffn_down_wT", [INTER, DM], I32, kind="ExternalInput")
    t["L0_ffn_gate_scale"] = nc.dram_tensor("L0_ffn_gate_scale", [INTER], F32, kind="ExternalInput")
    t["L0_ffn_up_scale"] = nc.dram_tensor("L0_ffn_up_scale", [INTER], F32, kind="ExternalInput")
    t["L0_ffn_down_scale"] = nc.dram_tensor("L0_ffn_down_scale", [DM], F32, kind="ExternalInput")
    t["xin"] = nc.dram_tensor("xin", [DM, NTH], F32, kind=kind_x)
    t["xout"] = nc.dram_tensor("xout", [DM, NTH], F32, kind=kind_y)
    t["hmid"] = nc.dram_tensor("hmid", [INTER, NTH], F32 if False else BF16)
    return t


def build_ffn0_program():
    from contextlib import ExitStack
    nc = bass.Bass("TRN2", target_bir_lowering=False)
    ctx = ExitStack()
    t = declare_ffn0_tensors(nc)
    P = Prog(nc, ctx)
    with ctx:
        alloc_arenas(P)
        emit_consts(P, t)
        P.barrier()
        build_colset(P, "l0g", t["L0_ffn_gate_scale"], INTER)
        build_colset(P, "l0u", t["L0_ffn_up_scale"], INTER)
        build_colset(P, "l0d", t["L0_ffn_down_scale"], DM, post_scale=0.5)
        P.barrier()
        emit_ffn0(P, t, t["xin"], t["xout"], "f0")
        P.finish()
    return nc


def ffn0_in_map(prep, xshard, core):
    m = {nm: np.asarray(prep[nm]) for nm, _, _ in CONST_SPECS if nm != "oddcol"}
    m["oddcol"] = prep["oddcol"][core]
    for nm in ["L0_ffn_gate_wT", "L0_ffn_up_wT", "L0_ffn_down_wT",
               "L0_ffn_gate_scale", "L0_ffn_up_scale", "L0_ffn_down_scale"]:
        m[nm] = prep[nm]
    m["xin"] = xshard
    return m


def _gather(res):
    out = np.empty((4, 4096, 2048), np.float32)
    for k in range(NCORES):
        b, hf = k // 2, k % 2
        out[b, hf * NT:(hf + 1) * NT] = res.results[k]["xout"][:, HALO:].T
    return out


def kernel_unfused(**inputs):
    x = inputs["L0_attn_x"]
    prep = host_prep_common(inputs)
    prepm = host_prep_mamba(inputs)
    prepf1 = host_prep_ffn1(inputs)
    prepa1 = host_prep_attn1(inputs)

    nc_ffn0 = build_ffn0_program()
    nc_mamba = build_mamba_program()
    nc_ffn1 = build_ffn1_program(fold=True)
    nc_attn1 = build_attn1_program()

    shards = shard_x(x)
    res = run_bass_kernel_spmd(nc_ffn0, [ffn0_in_map(prep, shards[k], k) for k in range(NCORES)],
                               core_ids=list(range(NCORES)))
    x = _gather(res)
    shards = shard_x(x)
    res = run_bass_kernel_spmd(nc_mamba, [mamba_in_map(prep, prepm, shards[k], k) for k in range(NCORES)],
                               core_ids=list(range(NCORES)))
    x = _gather(res)
    shards = shard_x(x)
    res = run_bass_kernel_spmd(nc_ffn0, [ffn0_in_map(prep, shards[k], k) for k in range(NCORES)],
                               core_ids=list(range(NCORES)))
    x = _gather(res)
    shards = shard_x(x)
    res = run_bass_kernel_spmd(nc_ffn1, [generic_in_map(prep, prepf1, shards[k], k) for k in range(NCORES)],
                               core_ids=list(range(NCORES)))
    x = _gather(res)
    shards = shard_x(x)
    res = run_bass_kernel_spmd(nc_attn1, [generic_in_map(prep, prepa1, shards[k], k) for k in range(NCORES)],
                               core_ids=list(range(NCORES)))
    x = _gather(res)
    shards = shard_x(x)
    res = run_bass_kernel_spmd(nc_ffn1, [generic_in_map(prep, prepf1, shards[k], k) for k in range(NCORES)],
                               core_ids=list(range(NCORES)))
    return _gather(res)


def emit_mamba(P, t, xin, xout, tag="m", upto="m7"):
    A = Carve(P.sbufs["arA"], ARENA_A)
    B_ = Carve(P.sbufs["arB"], ARENA_B)
    H = Carve(P.sbufs["arH"], ARENA_H)
    hst = [H.take(DC * TT), H.take(DC * TT)]
    hst_free = {}
    hln2 = t["hln2"]

    def hw0(c, t0, W):
        par = (t0 // TT) % 2
        return hst[par][:, c * W: c * W + W]

    def flush0(ti, t0, W):
        par = ti % 2
        st = P.dma("sync", hln2.ap()[:, t0:t0 + W].rearrange("(c p) w -> p c w", p=128),
                   hst[par][:, 0:DC * W].rearrange("p (c w) -> p c w", c=DC),
                   f"D_hst{par}", waits=[P.now("dve")])
        hst_free[par] = st

    emit_ln(P, xin, hw0, tiles=TILES_ALL, tag=f"{tag}0", ar1=A, ar2=B_, tile_done=flush0,
            extra_waits=lambda ti: [(f"D_hst{ti % 2}", hst_free.get(ti % 2, 0))])
    P.barrier()
    if upto == "m0":
        return

    A = Carve(P.sbufs["arA"], ARENA_A)
    B_ = Carve(P.sbufs["arB"], ARENA_B)
    H = Carve(P.sbufs["arH"], ARENA_H)
    w_sl = [A.take(DC * TT), A.take(DC * TT)]
    hti = [B_.take(DC * TT), B_.take(DC * TT)]
    rb = [H.take(4 * NTH), H.take(4 * NTH)]
    vt = H.take(NT, F32)
    ut = [H.take(NT), H.take(NT)]
    zst = [B_.take(TT), B_.take(TT)]
    wT = t["L0_attn_in_proj_wT"]
    cw = [P.sbufs[f"col_cw{k}"] for k in range(4)]
    cb = P.sbufs["col_cb"]
    hln_free = {}
    wslot_free = {}
    rb_free = {}
    zst_free = {}
    ut_free = {}
    bank_free = {}
    for ob in range(16):
        wpar = ob % 2
        dsem = f"D_w{wpar}"
        ld = P.dma("gpsimd", w_sl[wpar][:, :].rearrange("p (c w) -> p c w", c=DC),
                   wT.ap()[:, ob * TT:(ob + 1) * TT].rearrange("(c p) w -> p c w", p=128),
                   dsem, waits=[("pe", wslot_free.get(wpar, 0))])
        is_x = ob < 8
        tiles = TILES_ALL if is_x else TILES_MAIN
        for ti, (t0, W) in enumerate(tiles):
            hpar = ti % 2
            hsem = f"D_h{hpar}"
            hl = P.dma("sync", hti[hpar][:, 0:DC * W].rearrange("p (c w) -> p c w", c=DC),
                       hln2.ap()[:, t0:t0 + W].rearrange("(c p) w -> p c w", p=128),
                       hsem, waits=[("pe", hln_free.get(hpar, 0))])
            wv = w_sl[wpar][:, :].rearrange("p (c w) -> p c w", c=DC)
            hv = hti[hpar][:, 0:DC * W].rearrange("p (c w) -> p c w", c=DC)
            for oc in range(4):
                bk = (ti * 4 + oc) % 2 if is_x else 2 + (ti * 4 + oc) % 2
                wts = [(dsem, ld), (hsem, hl), bank_free.get(bk, ("dve", 0))]
                pe_done = mm_group(P, P.bank(bk).ap()[:, 0:W],
                                   [(wv[:, c, oc * 128:(oc + 1) * 128], hv[:, c, :]) for c in range(DC)],
                                   waits=wts)
                if is_x:
                    d = P.emit("vector", lambda v, bk=bk, oc=oc, t0=t0, W=W, wpar=wpar: v.tensor_copy(
                        rb[wpar][:, oc * NTH + t0: oc * NTH + t0 + W], P.bank(bk).ap()[:, 0:W]),
                        waits=[("pe", pe_done)], inc="dve")
                else:
                    zp = (ti * 4 + oc) % 2
                    d = P.emit("vector", lambda v, bk=bk, zp=zp, W=W: v.tensor_copy(
                        zst[zp][:, 0:W], P.bank(bk).ap()[:, 0:W]),
                        waits=[("pe", pe_done), (f"D_z{zp}", zst_free.get(zp, 0))], inc="dve")
                    gz = (ob - 8) * 4 + oc
                    st = P.dma("sync", t["z_s"].ap()[gz * 128:(gz + 1) * 128, t0 - HALO:t0 - HALO + W],
                               zst[zp][:, 0:W], f"D_z{zp}", waits=[("dve", d)])
                    zst_free[zp] = st
                bank_free[bk] = ("dve", d)
            hln_free[hpar] = P.counts["pe"]
        if is_x:
            dve_rb = P.counts["dve"]
            for oc in range(4):
                gc = ob * 4 + oc
                up = gc % 2
                r = rb[wpar]
                P.emit("vector", lambda v, r=r, oc=oc, gc=gc: v.tensor_scalar(
                    vt[:, :], r[:, oc * NTH + 1: oc * NTH + 1 + NT], cw[0][:, gc:gc + 1], None, OP.mult),
                    waits=([("dve", dve_rb)] if oc == 0 else []) + [P.now("act")], inc="dve")
                for k in (1, 2):
                    P.emit("vector", lambda v, r=r, oc=oc, gc=gc, k=k: v.scalar_tensor_tensor(
                        vt[:, :], r[:, oc * NTH + 1 + k: oc * NTH + 1 + k + NT], cw[k][:, gc:gc + 1],
                        vt[:, :], OP.mult, OP.add), inc="dve")
                d3 = P.emit("vector", lambda v, r=r, oc=oc, gc=gc: v.scalar_tensor_tensor(
                    vt[:, :], r[:, oc * NTH + 4: oc * NTH + 4 + NT], cw[3][:, gc:gc + 1],
                    vt[:, :], OP.mult, OP.add), inc="dve")
                a = P.emit("scalar", lambda s, gc=gc, up=up: s.activation(
                    ut[up][:, :], vt[:, :], AF.Silu, bias=cb[:, gc:gc + 1]),
                    waits=[("dve", d3), (f"D_u{up}", ut_free.get(up, 0))], inc="act")
                st = P.dma("sync", t["u_s"].ap()[gc * 128:(gc + 1) * 128, :],
                           ut[up][:, :], f"D_u{up}", waits=[("act", a)])
                ut_free[up] = st
        wslot_free[wpar] = P.counts["pe"]
    P.barrier()
    if upto == "m1":
        return

    A = Carve(P.sbufs["arA"], ARENA_A)
    B_ = Carve(P.sbufs["arB"], ARENA_B)
    H = Carve(P.sbufs["arH"], ARENA_H)
    xpT = t["L0_attn_x_proj_wT"]
    wsl = A.take(32 * TT)
    w_sl = [wsl, wsl]
    uc = [B_.take(32 * 128), B_.take(32 * 128)]
    xst = [B_.take(TT), B_.take(TT)]
    wslot_free = {}
    uc_free = {}
    xst_free = {}
    bank_free = {}
    for on in range(8):
        wpar = 0
        dsem = f"D_xw{wpar}"
        ld = P.dma("gpsimd", w_sl[wpar][:, :].rearrange("p (c w) -> p c w", c=32),
                   xpT.ap()[:, on * TT:(on + 1) * TT].rearrange("(c p) w -> p c w", p=128),
                   dsem, waits=[("pe", P.counts["pe"])])
        for tg in range(16):
            up = tg % 2
            usem = f"D_uc{up}"
            ul = P.dma("gpsimd", uc[up][:, :].rearrange("p (c w) -> p c w", c=32),
                       t["u_s"].ap()[:, tg * 128:(tg + 1) * 128].rearrange("(c p) w -> p c w", p=128),
                       usem, waits=[("pe", uc_free.get(up, 0))])
            bk = tg % 2
            wv = w_sl[wpar][:, :].rearrange("p (c w) -> p c w", c=32)
            uv = uc[up][:, :].rearrange("p (c w) -> p c w", c=32)
            pe_done = mm_group(P, P.bank(bk).ap()[:, :],
                               [(uv[:, c, :], wv[:, c, :]) for c in range(32)],
                               waits=[(dsem, ld), (usem, ul), bank_free.get(bk, ("dve", 0))])
            uc_free[up] = pe_done
            sp = tg % 2
            d = P.emit("vector", lambda v, bk=bk, sp=sp: v.tensor_copy(
                xst[sp][:, :], P.bank(bk).ap()[:, :]),
                waits=[("pe", pe_done), (f"D_xs{sp}", xst_free.get(sp, 0))], inc="dve")
            bank_free[bk] = ("dve", d)
            st = P.dma("sync", t["xtm_s"].ap()[tg * 128:(tg + 1) * 128, on * TT:(on + 1) * TT],
                       xst[sp][:, :], f"D_xs{sp}", waits=[("dve", d)])
            xst_free[sp] = st
        wslot_free[wpar] = P.counts["pe"]
    P.barrier()
    Hp = Carve(P.sbufs["arH"], ARENA_H)
    sl = {
        "bfm": Hp.take(NT),
        "cfm": Hp.take(NT),
        "dt2r": Hp.take(NT, F32),
        "acs": Hp.take(NT, F32),
        "hi": Hp.take(NT),
        "lo": Hp.take(NT),
        "tm_acs": Hp.take(NT, F32),
        "tm_dt": Hp.take(NT, F32),
        "tm_decay": Hp.take(NT, F32),
        "btm": Hp.take(2 * NT),
        "S": Hp.take(2 * NT, F32),
        "dbc": Hp.take(2 * NT),
    }
    bfm, cfm = sl["bfm"], sl["cfm"]
    B2 = Carve(P.sbufs["arB"], ARENA_B)
    A2 = Carve(P.sbufs["arA"], ARENA_A)
    w2 = B2.take(32 * 320)
    ut2 = A2.take(32 * TT)
    dsem2 = f"D_xw2"
    ld2 = P.dma("gpsimd", w2[:, :].rearrange("p (c w) -> p c w", c=32),
                xpT.ap()[:, D_INNER:D_INNER + 320].rearrange("(c p) w -> p c w", p=128),
                dsem2)
    ut2_free = {}
    for ti, (t0, W) in enumerate(TILES_MAIN):
        usem = f"D_u2"
        ul = P.dma("gpsimd", ut2[:, 0:32 * W].rearrange("p (c w) -> p c w", c=32),
                   t["u_s"].ap()[:, t0 - HALO:t0 - HALO + W].rearrange("(c p) w -> p c w", p=128),
                   usem, waits=[("pe", ut2_free.get(0, 0))])
        wv = w2[:, :].rearrange("p (c w) -> p c w", c=32)
        uv = ut2[:, 0:32 * W].rearrange("p (c w) -> p c w", c=32)
        m0 = t0 - HALO
        for (bk, o0, o1, dstf) in ((4, 0, 64, None), (5, 64, 192, bfm), (6, 192, 320, cfm)):
            pe_done = mm_group(P, P.bank(bk).ap()[0:(o1 - o0), 0:W],
                               [(wv[:, c, o0:o1], uv[:, c, :]) for c in range(32)],
                               waits=[(dsem2, ld2), (usem, ul), bank_free.get(bk, ("dve", 0))])
            if dstf is None:
                d = P.emit("vector", lambda v, m0=m0, W=W: v.tensor_copy(
                    sl["dt2r"][0:64, m0:m0 + W], P.bank(4).ap()[0:64, 0:W]),
                    waits=[("pe", pe_done)], inc="dve")
            else:
                d = P.emit("vector", lambda v, dstf=dstf, bk=bk, m0=m0, W=W: v.tensor_copy(
                    dstf[:, m0:m0 + W], P.bank(bk).ap()[0:128, 0:W]),
                    waits=[("pe", pe_done)], inc="dve")
            bank_free[bk] = ("dve", d)
        ut2_free[0] = P.counts["pe"]
    P.barrier()
    if upto == "m2":
        return
    emit_mamba_ssd(P, t, xin, xout, tag, sl, upto)


def emit_mamba_ssd(P, t, xin, xout, tag, sl, upto="m7"):
    nc = P.nc
    fc = P.sbufs["c_fconst"]
    id128f = P.sbufs["c_id128f"]
    id128b = P.sbufs["c_id128b"]
    onesrow_f = P.sbufs["c_onesrow_f"]
    rowsel = P.sbufs["c_rowsel"]
    triu = P.sbufs["c_triu64"]
    oddcol = P.sbufs["c_oddcol"]
    B3 = Carve(P.sbufs["arB"], ARENA_B)
    sp_v = B3.take(TT, F32)[0:64, :]
    sp_b = B3.take(TT, F32)[0:64, :]
    dt_bf = B3.take(NT)[0:64, :]
    tmpb = B3.take(2 * NT)
    lrow = B3.take(NCHUNK * 64, F32)[0:1, :]
    zt64 = P.sbuf("m_zt64", [64, 64], F32)
    spd = P.sbuf("m_spd", [64, 64], F32)
    spe = P.sbuf("m_spe", [64, 64], F32)
    lbc = P.sbuf("m_lbc", [128, 64], F32)
    dtb_col = P.sbuf("m_dtb", [64, 1], F32)
    negA = P.sbuf("m_negA", [64, 1], F32)
    dtpT = P.sbuf("m_dtpT", [64, 64], BF16)
    drow = P.sbuf("m_drow", [1, 64], F32)
    gtm = P.sbuf("m_gtm", [64, 64], BF16)
    e1s = P.sbuf("m_e1", [64, 64], F32)
    e2s = P.sbuf("m_e2", [64, 64], F32)
    ems = [P.sbuf(f"m_em{i}", [64, 64], BF16) for i in range(2)]
    xds = [P.sbuf(f"m_xd{i}", [64, 64], BF16) for i in range(2)]
    sbf = [P.sbuf(f"m_sbf{i}", [128, 64], BF16) for i in range(2)]
    lrow_s = P.sbuf("m_lrs", [1, 64], F32)

    acs, dt2r = sl["acs"], sl["dt2r"]
    hi, lo = sl["hi"], sl["lo"]
    tm_acs, tm_dt = sl["tm_acs"], sl["tm_dt"]
    tm_decay = sl["tm_decay"]
    tm_eacs = P.sbuf("m_tme", [64, 64], F32)
    btm, S, dbc = sl["btm"], sl["S"], sl["dbc"]
    bfm, cfm = sl["bfm"], sl["cfm"]

    l1 = P.dma("gpsimd", dtpT[:, :], t["L0_attn_dt_proj_wT"].ap(), "D_m3")
    l2 = P.dma("gpsimd", dtb_col[:, :], t["L0_attn_dt_proj_b"].ap().rearrange("(a b) -> a b", b=1), "D_m3")
    l3 = P.dma("gpsimd", negA[:, :], t["L0_attn_A_log"].ap().rearrange("(a b) -> a b", b=1), "D_m3")
    l4 = P.dma("gpsimd", drow[:, :], t["L0_attn_D"].ap().rearrange("(a b) -> b a", b=1), "D_m3")
    a1 = P.emit("scalar", lambda s: s.activation(negA[:, :], negA[:, :], AF.Exp),
                waits=[("D_m3", l4)], inc="act")
    n1 = P.emit("vector", lambda v: v.tensor_scalar(negA[:, :], negA[:, :], -1.0, None, OP.mult),
                waits=[("act", a1)], inc="dve")
    P.emit("vector", lambda v: v.memset(zt64[:, :], 0.0), inc="dve")
    pe1 = P.emit("tensor", lambda tt: tt.matmul(
        P.bank(7).ap()[:, 0:64], onesrow_f[0:1, :], drow[0:1, :],
        start=True, stop=True), waits=[("D_m3", l4)], inc="pe")
    dsm = P.sbuf("m_dsm", [128, 64], F32)
    dd1 = P.emit("vector", lambda v: v.tensor_copy(dsm[:, :], P.bank(7).ap()[:, 0:64]),
                 waits=[("pe", pe1)], inc="dve")
    P.emit("vector", lambda v: v.tensor_copy(
        dbc[:, :].rearrange("p (h q) -> p h q", h=64),
        dsm[:, :].unsqueeze(2).broadcast_to([128, 64, 64])), inc="dve")
    c1 = P.emit("vector", lambda v: v.tensor_copy(dt_bf[:, :], dt2r[0:64, 0:NT]), inc="dve")
    for ti in range(4):
        t0 = ti * TT
        pe_done = P.emit("tensor", lambda tt, t0=t0: tt.matmul(
            P.bank(0).ap()[0:64, :], dtpT[:, :], dt_bf[:, t0:t0 + TT], start=True, stop=True),
            waits=[("D_m3", l2), ("dve", c1), P.now("dve")], inc="pe")
        v1 = P.emit("vector", lambda v, t0=t0: v.tensor_scalar(
            sp_v[:, :], P.bank(0).ap()[0:64, :], dtb_col[:, 0:1], None, OP.add),
            waits=[("pe", pe_done), P.now("act")], inc="dve")
        a2 = P.emit("scalar", lambda s: s.activation(sp_b[:, :], sp_v[:, :], AF.Abs),
                    waits=[("dve", v1)], inc="act")
        a3 = P.emit("scalar", lambda s: s.activation(sp_b[:, :], sp_b[:, :], AF.Exp, scale=-1.0),
                    inc="act")
        a4 = P.emit("scalar", lambda s: s.activation(sp_b[:, :], sp_b[:, :], AF.Ln, bias=fc[0:64, 2:3]),
                    inc="act")
        v2 = P.emit("vector", lambda v: v.tensor_scalar(sp_v[:, :], sp_v[:, :], 0.0, None, OP.max),
                    waits=[("act", a4)], inc="dve")
        v3 = P.emit("vector", lambda v, t0=t0: v.tensor_add(dt2r[0:64, t0:t0 + TT], sp_v[:, :], sp_b[:, :]),
                    inc="dve")
        v4 = P.emit("vector", lambda v, t0=t0: v.tensor_scalar(
            sp_v[:, :], dt2r[0:64, t0:t0 + TT], negA[:, 0:1], None, OP.mult),
            waits=[("dve", n1)], inc="dve")
        for cc in range(TT // CHUNK):
            P.emit("vector", lambda v, t0=t0, cc=cc: v.tensor_tensor_scan(
                acs[0:64, t0 + cc * CHUNK: t0 + (cc + 1) * CHUNK],
                sp_v[:, cc * CHUNK:(cc + 1) * CHUNK], zt64[:, :], 0.0, OP.add, OP.add),
                inc="dve")
    h1 = P.emit("vector", lambda v: v.tensor_copy(hi[0:64, 0:NT], acs[0:64, 0:NT]), inc="dve")
    h2 = P.emit("vector", lambda v: v.tensor_sub(lo[0:64, 0:NT], acs[0:64, 0:NT], hi[0:64, 0:NT]),
                inc="dve")
    for c in range(NCHUNK):
        cs = slice(c * CHUNK, (c + 1) * CHUNK)
        d0 = P.emit("vector", lambda v, cs=cs, c=c: v.tensor_scalar(
            spd[:, :], acs[0:64, cs], acs[0:64, c * CHUNK + 63:c * CHUNK + 64], None, OP.subtract),
            waits=[P.now("pe")], inc="dve")
        a5 = P.emit("scalar", lambda s: s.activation(spd[:, :], spd[:, :], AF.Exp, scale=-1.0),
                    waits=[("dve", d0)], inc="act")
        a6 = P.emit("scalar", lambda s, cs=cs: s.activation(spe[:, :], acs[0:64, cs], AF.Exp),
                    waits=[P.now("pe")], inc="act")
        pw = [("act", a6), ("dve", P.counts["dve"])]
        tps = [(acs[0:64, cs], tm_acs, 2), (dt2r[0:64, cs], tm_dt, 3),
               (spd[:, :], tm_decay, 4)]
        for (src_ap, dst, bk) in tps:
            pe2 = P.emit("tensor", lambda tt, src_ap=src_ap, bk=bk: tt.transpose(
                P.bank(bk).ap()[0:64, 0:64], src_ap, id128f[0:64, 0:64]),
                waits=(pw if bk == 2 else ()), inc="pe")
            P.emit("vector", lambda v, dst=dst, c=c, bk=bk: v.tensor_copy(
                dst[0:64, c * 64:(c + 1) * 64], P.bank(bk).ap()[0:64, 0:64]),
                waits=[("pe", pe2)], inc="dve")
        pe3 = P.emit("tensor", lambda tt: tt.transpose(
            P.bank(6).ap()[0:1, 0:64], spe[:, 63:64], id128f[0:64, 0:64]),
            inc="pe")
        P.emit("vector", lambda v, c=c: v.tensor_copy(
            lrow[0:1, c * 64:(c + 1) * 64], P.bank(6).ap()[0:1, 0:64]),
            waits=[("pe", pe3)], inc="dve")
        pe4 = P.emit("tensor", lambda tt, cs=cs: tt.transpose(
            P.bank_bf(7)[0:64, 0:128], bfm[:, cs], id128b[:, :]),
            waits=[P.now("dve")], inc="pe")
        P.emit("vector", lambda v, c=c: v.tensor_copy(
            btm[0:64, c * 128:(c + 1) * 128], P.bank_bf(7)[0:64, 0:128]),
            waits=[("pe", pe4)], inc="dve")
    P.barrier()
    if upto == "m3":
        return

    A = Carve(P.sbufs["arA"], ARENA_A)
    xt = [A.take(2 * NT), A.take(2 * NT)]
    xs = A.take(2 * NT)
    ysb = [A.take(2 * NT), A.take(2 * NT)]
    tmp = tmpb

    zS = P.emit("vector", lambda v: v.memset(S.bitcast(F32) if False else S[:, :], 0.0), inc="dve")

    def chunk_head_loop(main, cidx):
        c = cidx
        cs = slice(c * CHUNK, (c + 1) * CHUNK)
        xp = c % 2
        xsem = f"D_xt{xp}{int(main)}"
        xl = P.dma("gpsimd", xt[xp][0:64, :], t["xtm_s"].ap()[c * 64:(c + 1) * 64, :],
                   xsem, waits=[("pe", P.cm.get(("xt", xp, main), 0)),
                                ("dve", P.cm.get(("xtv", xp, main), 0))])
        pe_l = P.emit("tensor", lambda tt, c=c: tt.matmul(
            P.bank(0).ap()[:, 0:64], onesrow_f[0:1, :], lrow[0:1, c * 64:(c + 1) * 64],
            start=True, stop=True), waits=[P.now("dve")], inc="pe")
        dl = P.emit("vector", lambda v: v.tensor_copy(lbc[:, :], P.bank(0).ap()[:, 0:64]),
                    waits=[("pe", pe_l)], inc="dve")
        dxs = P.emit("vector", lambda v, c=c, xp=xp: v.tensor_mul(
            xs[0:64, :].rearrange("p (h q) -> p h q", h=64),
            xt[xp][0:64, :].rearrange("p (h q) -> p h q", h=64),
            tm_dt[0:64, c * 64:(c + 1) * 64].unsqueeze(2).broadcast_to([64, 64, 64])),
            waits=[(xsem, xl), ("pe", P.cm.get(("xsr", main), 0))], inc="dve")
        P.cm[("xtv", xp, main)] = dxs
        if main:
            pe_g = P.emit("tensor", lambda tt, cs=cs: tt.matmul(
                P.bank(1).ap()[0:64, 0:64], bfm[:, cs], cfm[:, cs], start=True, stop=True),
                waits=[P.now("dve")], inc="pe")
            dg = P.emit("vector", lambda v: v.tensor_mul(gtm[:, :], P.bank(1).ap()[0:64, 0:64],
                                                         triu[:, :]),
                        waits=[("pe", pe_g)], inc="dve")
            P.cm[("e", 1)] = dg
            ate = P.emit("scalar", lambda s, c=c: s.activation(
                tm_eacs[:, :], tm_acs[0:64, c * 64:(c + 1) * 64], AF.Exp),
                waits=[P.now("dve")], inc="act")
            P.cm[("tme",)] = ate
        for h in range(64):
            hp = h % 2
            if main:
                ebk = 1 + hp
                ew = [("dve", P.cm.get(("e", ebk), dg))]
                P.emit("tensor", lambda tt, h=h, cs=cs, ebk=ebk: tt.matmul(
                    P.bank(ebk).ap()[0:64, 0:64], rowsel[:, h * 64:(h + 1) * 64], hi[0:64, cs],
                    start=True, stop=False), waits=ew, inc=None)
                pe_e = P.emit("tensor", lambda tt, h=h, cs=cs, ebk=ebk: tt.matmul(
                    P.bank(ebk).ap()[0:64, 0:64], rowsel[:, h * 64:(h + 1) * 64], lo[0:64, cs],
                    start=False, stop=True), inc="pe")
                de1 = P.emit("vector", lambda v, c=c, h=h, ebk=ebk: v.tensor_scalar(
                    e1s[:, :], P.bank(ebk).ap()[0:64, 0:64],
                    tm_acs[0:64, c * 64 + h:c * 64 + h + 1], 0.0, OP.subtract, OP.min),
                    waits=[("pe", pe_e)], inc="dve")
                P.cm[("e", ebk)] = de1
                ae = P.emit("scalar", lambda s: s.activation(e2s[:, :], e1s[:, :], AF.Exp),
                            waits=[("dve", de1)], inc="act")
                dem = P.emit("vector", lambda v, hp=hp: v.tensor_mul(ems[hp][:, :], e2s[:, :], gtm[:, :]),
                             waits=[("act", ae), ("pe", P.cm.get(("emr", hp), 0))], inc="dve")
                ybk = 3 + hp
                pe_y = P.emit("tensor", lambda tt, h=h, hp=hp, ybk=ybk: tt.matmul(
                    P.bank(ybk).ap()[0:64, 0:64], ems[hp][:, :], xs[0:64, h * 64:(h + 1) * 64],
                    start=True, stop=True),
                    waits=[("dve", dem), ("dve", P.cm.get(("yb", ybk), 0))], inc="pe")
                P.cm[("emr", hp)] = pe_y
                dsb = P.emit("vector", lambda v, h=h, hp=hp: v.tensor_copy(
                    sbf[hp][:, :], S[:, h * 64:(h + 1) * 64]),
                    waits=[("pe", P.cm.get(("sbr", hp), 0))], inc="dve")
                obk = 5 + (hp if True else 0)
                pe_o = P.emit("tensor", lambda tt, cs=cs, hp=hp, obk=obk: tt.matmul(
                    P.bank(obk).ap()[0:64, 0:64], cfm[:, cs], sbf[hp][:, :],
                    start=True, stop=True),
                    waits=[("dve", dsb), ("dve", P.cm.get(("ob", obk), 0))], inc="pe")
                P.cm[("sbr", hp)] = pe_o
                yw = [("pe", pe_o)]
                if h == 0:
                    yw.append((f"D_y{c % 2}", P.cm.get(("ysb", c % 2), 0)))
                if h == 0:
                    yw.append(("act", P.cm[("tme",)]))
                dy1 = P.emit("vector", lambda v, c=c, h=h, obk=obk: v.tensor_scalar(
                    ysb[c % 2][0:64, h * 64:(h + 1) * 64], P.bank(obk).ap()[0:64, 0:64],
                    tm_eacs[:, h:h + 1], None, OP.mult), waits=yw, inc="dve")
                dy = P.emit("vector", lambda v, c=c, h=h, ybk=ybk: v.tensor_add(
                    ysb[c % 2][0:64, h * 64:(h + 1) * 64],
                    ysb[c % 2][0:64, h * 64:(h + 1) * 64], P.bank(ybk).ap()[0:64, 0:64]),
                    inc="dve")
                P.cm[("yb", ybk)] = dy
                P.cm[("ob", obk)] = dy1
            dxd = P.emit("vector", lambda v, c=c, h=h, hp=hp: v.tensor_scalar(
                xds[hp][:, :], xs[0:64, h * 64:(h + 1) * 64],
                tm_decay[0:64, c * 64 + h:c * 64 + h + 1], None, OP.mult),
                waits=[("dve", dxs), ("pe", P.cm.get(("xdr", hp), 0))], inc="dve")
            sbk = 7
            pe_s = P.emit("tensor", lambda tt, c=c, hp=hp: tt.matmul(
                P.bank(sbk).ap()[:, 0:64], btm[0:64, c * 128:(c + 1) * 128], xds[hp][:, :],
                start=True, stop=True),
                waits=[("dve", dxd), ("dve", P.cm.get(("sb",), 0))], inc="pe")
            P.cm[("xdr", hp)] = pe_s
            dsu = P.emit("vector", lambda v, h=h: v.scalar_tensor_tensor(
                S[:, h * 64:(h + 1) * 64], S[:, h * 64:(h + 1) * 64], lbc[:, h:h + 1],
                P.bank(sbk).ap()[:, 0:64], OP.mult, OP.add),
                waits=[("pe", pe_s), ("dve", dl)], inc="dve")
            P.cm[("sb",)] = dsu
        if main:
            dt1 = P.emit("vector", lambda v, xp=xp: v.tensor_mul(
                tmp[0:64, :], xt[xp][0:64, :], dbc[0:64, :]), inc="dve")
            dt2_ = P.emit("vector", lambda v, c=c: v.tensor_add(
                ysb[c % 2][0:64, :], ysb[c % 2][0:64, :], tmp[0:64, :]), inc="dve")
            st = P.dma("sync", t["y_s"].ap()[c * 64:(c + 1) * 64, :], ysb[c % 2][0:64, :],
                       f"D_y{c % 2}", waits=[("dve", dt2_)])
            P.cm[("ysb", c % 2)] = st
            P.cm[("xtv", xp, main)] = dt2_
        P.cm[("xt", xp, main)] = P.counts["pe"]
        P.cm[("xsr", main)] = P.counts["pe"]

    P.cm = {}
    for c in range(NCHUNK):
        chunk_head_loop(False, c)
    P.barrier()
    if upto == "m4":
        return
    e1 = P.dma("gpsimd", t["ccin"].ap(), S[:, :], "D_cc")
    cc = P.emit("gpsimd", lambda g: g.collective_compute(
        "AllGather", OP.bypass, replica_groups=[[0, 1], [2, 3], [4, 5], [6, 7]],
        ins=[t["ccin"].ap().opt()], outs=[t["ccout"].ap().opt()]),
        waits=[("D_cc", e1)], inc="cc", is_dma=True)
    e2 = P.dma("gpsimd", S[:, :], t["ccout"].ap()[0:128, :], "D_cc", waits=[("cc", cc)])
    P.emit("vector", lambda v: v.tensor_scalar(S[:, :], S[:, :], oddcol[:, 0:1], None, OP.mult),
           waits=[("D_cc", e2)], inc="dve")
    P.barrier()
    if upto == "m5":
        return
    import os
    nmain = int(os.environ.get("MAMBA_MAIN_CHUNKS", NCHUNK))
    for c in range(nmain):
        chunk_head_loop(True, c)
    P.barrier()
    if upto == "m6":
        return
    emit_mamba_out(P, t, xin, xout, tag)


def emit_mamba_out(P, t, xin, xout, tag):
    A = Carve(P.sbufs["arA"], ARENA_A)
    B_ = Carve(P.sbufs["arB"], ARENA_B)
    H = Carve(P.sbufs["arH"], ARENA_H)
    id128b = P.sbufs["c_id128b"]
    ygf = A.take(32 * TT)
    yt = [A.take(2 * NT), A.take(2 * NT)]
    zt = [B_.take(32 * 128), B_.take(32 * 128)]
    zs = B_.take(32 * 128)
    w_sl = [H.take(32 * 256), H.take(32 * 256)]
    xi = [H.take(2 * TT, F32), H.take(2 * TT, F32)]
    xo = [H.take(2 * TT, F32), H.take(2 * TT, F32)]
    woT = t["L0_attn_out_proj_wT"]
    cm = {}
    for tg in range(4):
        for tc in range(4):
            gtok = tg * 4 + tc
            ypar = tc % 2
            ysem = f"D_yt{ypar}"
            yl = P.dma("gpsimd", yt[ypar][0:128, :], t["y_s"].ap()[gtok * 128:(gtok + 1) * 128, :],
                       ysem, waits=[("pe", cm.get(("yt", ypar), 0))])
            zsem = f"D_zt{ypar}"
            zl = P.dma("sync", zt[ypar][:, :].rearrange("p (c w) -> p c w", c=32),
                       t["z_s"].ap()[:, gtok * 128:(gtok + 1) * 128].rearrange("(c p) w -> p c w", p=128),
                       zsem, waits=[("act", cm.get(("zt", ypar), 0))])
            az = P.emit("scalar", lambda s, ypar=ypar: s.activation(
                zs[:, :], zt[ypar][:, :], AF.Silu),
                waits=[(zsem, zl), ("dve", cm.get(("zs",), 0))], inc="act")
            cm[("zt", ypar)] = az
            for fck in range(32):
                pe1 = P.emit("tensor", lambda tt, ypar=ypar, fck=fck: tt.transpose(
                    P.bank_bf(fck % 2)[0:128, 0:128], yt[ypar][0:128, fck * 128:(fck + 1) * 128],
                    id128b[:, :]),
                    waits=([(ysem, yl), ("dve", cm.get(("tb", fck % 2), 0))] if True else ()),
                    inc="pe")
                d1 = P.emit("vector", lambda v, fck=fck, tc=tc, ypar=ypar: v.scalar_tensor_tensor(
                    ygf[:, fck * TT + tc * 128: fck * TT + (tc + 1) * 128],
                    P.bank_bf(fck % 2)[0:128, 0:128], 1.0, zs[:, fck * 128:(fck + 1) * 128],
                    OP.mult, OP.mult),
                    waits=[("pe", pe1), ("act", az)], inc="dve")
                cm[("tb", fck % 2)] = d1
            cm[("yt", ypar)] = P.counts["pe"]
            cm[("zs",)] = P.counts["dve"]
        dve_ygf = P.counts["dve"]
        for ob in range(8):
            wpar = ob % 2
            wsem = f"D_ow{wpar}"
            wl = P.dma("gpsimd", w_sl[wpar][:, :].rearrange("p (c w) -> p c w", c=32),
                       woT.ap()[:, ob * 256:(ob + 1) * 256].rearrange("(c p) w -> p c w", p=128),
                       wsem, waits=[("pe", cm.get(("ow", wpar), 0))])
            xsem = f"D_oxi{wpar}"
            xl = P.dma("gpsimd", xi[wpar][:, :].rearrange("p (c w) -> p c w", c=2),
                       xin.ap()[ob * 256:(ob + 1) * 256, HALO + tg * TT:HALO + (tg + 1) * TT]
                       .rearrange("(c p) w -> p c w", p=128),
                       xsem, waits=[("dve", cm.get(("oxi", wpar), 0))])
            wv = w_sl[wpar][:, :].rearrange("p (c w) -> p c w", c=32)
            yv = ygf[:, :].rearrange("p (c w) -> p c w", c=32)
            for oc in range(2):
                bk = 4 + (ob % 2) * 2 + oc
                pe_done = mm_group(P, P.bank(bk).ap()[:, :],
                                   [(wv[:, ch, oc * 128:(oc + 1) * 128], yv[:, ch, :]) for ch in range(32)],
                                   waits=[(wsem, wl), ("dve", dve_ygf), cm.get(("bnk", bk), ("dve", 0))])
                d2 = P.emit("vector", lambda v, oc=oc, wpar=wpar, bk=bk: v.tensor_add(
                    xo[wpar][:, oc * TT:(oc + 1) * TT], P.bank(bk).ap()[:, :],
                    xi[wpar][:, oc * TT:(oc + 1) * TT]),
                    waits=[("pe", pe_done), (xsem, xl),
                           (f"D_oxo{wpar}", cm.get(("oxo", wpar), 0))], inc="dve")
                cm[("bnk", bk)] = ("dve", d2)
            cm[("oxi", wpar)] = P.counts["dve"]
            st = P.dma("sync", xout.ap()[ob * 256:(ob + 1) * 256, HALO + tg * TT:HALO + (tg + 1) * TT]
                       .rearrange("(c p) w -> p c w", p=128),
                       xo[wpar][:, :].rearrange("p (c w) -> p c w", c=2),
                       f"D_oxo{wpar}", waits=[P.now("dve")])
            cm[("oxo", wpar)] = st
            cm[("ow", wpar)] = P.counts["pe"]
    P.barrier()


def declare_mamba_tensors(nc, kind_x="ExternalInput", kind_y="ExternalOutput"):
    t = {}
    for nm, shape, dt in CONST_SPECS:
        t[nm] = nc.dram_tensor(nm, shape, dt, kind="ExternalInput")
    t["L0_attn_in_proj_wT"] = nc.dram_tensor("L0_attn_in_proj_wT", [DM, 2 * D_INNER], F32, kind="ExternalInput")
    t["L0_attn_x_proj_wT"] = nc.dram_tensor("L0_attn_x_proj_wT", [D_INNER, 4416], F32, kind="ExternalInput")
    t["L0_attn_dt_proj_wT"] = nc.dram_tensor("L0_attn_dt_proj_wT", [64, 64], F32, kind="ExternalInput")
    t["L0_attn_out_proj_wT"] = nc.dram_tensor("L0_attn_out_proj_wT", [D_INNER, DM], F32, kind="ExternalInput")
    t["L0_attn_conv_wT"] = nc.dram_tensor("L0_attn_conv_wT", [4, D_INNER], F32, kind="ExternalInput")
    t["L0_attn_conv_b"] = nc.dram_tensor("L0_attn_conv_b", [D_INNER], F32, kind="ExternalInput")
    t["L0_attn_dt_proj_b"] = nc.dram_tensor("L0_attn_dt_proj_b", [64], F32, kind="ExternalInput")
    t["L0_attn_A_log"] = nc.dram_tensor("L0_attn_A_log", [64], F32, kind="ExternalInput")
    t["L0_attn_D"] = nc.dram_tensor("L0_attn_D", [64], F32, kind="ExternalInput")
    t["xin"] = nc.dram_tensor("xin", [DM, NTH], F32, kind=kind_x)
    t["xout"] = nc.dram_tensor("xout", [DM, NTH], F32, kind=kind_y)
    t["hln2"] = nc.dram_tensor("hln2", [DM, NTH], BF16)
    t["z_s"] = nc.dram_tensor("z_s", [D_INNER, NT], BF16)
    t["u_s"] = nc.dram_tensor("u_s", [D_INNER, NT], BF16)
    t["xtm_s"] = nc.dram_tensor("xtm_s", [NT, D_INNER], BF16)
    t["y_s"] = nc.dram_tensor("y_s", [NT, D_INNER], BF16)
    t["ccin"] = nc.dram_tensor("ccin", [128, 4096], F32)
    t["ccout"] = nc.dram_tensor("ccout", [256, 4096], F32)
    return t


def emit_mamba_cols(P, t):
    for k in range(4):
        build_colset(P, f"cw{k}", t["L0_attn_conv_wT"].ap()[k, :], D_INNER)
    build_colset(P, "cb", t["L0_attn_conv_b"], D_INNER)


def build_mamba_program(upto="m7", dbg=()):
    from contextlib import ExitStack
    nc = bass.Bass("TRN2", target_bir_lowering=False)
    ctx = ExitStack()
    t = declare_mamba_tensors(nc)
    P = Prog(nc, ctx)
    with ctx:
        alloc_arenas(P)
        emit_consts(P, t)
        P.barrier()
        emit_mamba_cols(P, t)
        P.barrier()
        emit_mamba(P, t, t["xin"], t["xout"], "m", upto=upto)
        for nm in dbg:
            shp = {"hln2": [DM, NTH], "z_s": [D_INNER, NT], "u_s": [D_INNER, NT],
                   "xtm_s": [NT, D_INNER], "y_s": [NT, D_INNER],
                   "ccout": [256, 4096]}[nm]
            dtp = F32 if nm == "ccout" else BF16
            o = nc.dram_tensor("dump_" + nm, shp, dtp, kind="ExternalOutput")
            st = P.dma("gpsimd", o.ap(), t[nm].ap(), "D_dump")
            P.emit("gpsimd", lambda g: None, waits=[("D_dump", st)], inc="pe")
        P.finish()
    return nc


def host_prep_mamba(inputs):
    d = {}
    d["L0_attn_in_proj_wT"] = np.ascontiguousarray(inputs["L0_attn_in_proj_w"].T)
    d["L0_attn_x_proj_wT"] = np.ascontiguousarray(inputs["L0_attn_x_proj_w"].T)
    d["L0_attn_dt_proj_wT"] = np.ascontiguousarray(inputs["L0_attn_dt_proj_w"].T)
    d["L0_attn_out_proj_wT"] = np.ascontiguousarray(inputs["L0_attn_out_proj_w"].T)
    d["L0_attn_conv_wT"] = np.ascontiguousarray(inputs["L0_attn_conv_w"].T)
    for nm in ["L0_attn_conv_b", "L0_attn_dt_proj_b", "L0_attn_A_log", "L0_attn_D"]:
        d[nm] = inputs[nm]
    return d


def mamba_in_map(prep, prepm, xshard, core):
    m = {nm: np.asarray(prep[nm]) for nm, _, _ in CONST_SPECS if nm != "oddcol"}
    m["oddcol"] = prep["oddcol"][core]
    m.update(prepm)
    m["xin"] = xshard
    return m


def emit_attn1(P, t, xin, xout, tag="a"):
    A = Carve(P.sbufs["arA"], ARENA_A)
    B_ = Carve(P.sbufs["arB"], ARENA_B)
    H = Carve(P.sbufs["arH"], ARENA_H)
    hln = H.take(DC * NT)
    kv = H.take(4 * NT)
    bq = P.sbufs["col_l1bq"]
    bkv = P.sbufs["col_l1bkv"]

    def hw(c, t0, W):
        return hln[:, c * NT + (t0 - HALO): c * NT + (t0 - HALO) + W]

    emit_ln(P, xin, hw, tiles=TILES_MAIN, tag=f"{tag}0", ar1=A, ar2=B_)
    P.barrier()

    A = Carve(P.sbufs["arA"], ARENA_A)
    B_ = Carve(P.sbufs["arB"], ARENA_B)
    w_sl = A.take(DC * TT)
    qg = A.take(4 * NT)
    kdup = B_.take(4 * NT)
    prod = B_.take(NT)
    srow = B_.take(TT, F32)
    zrow = B_.take(TT, F32)
    erow = B_.take(TT, F32)
    rrow = B_.take(TT, F32)
    arow = B_.take(TT, F32)
    tv = B_.take(TT, F32)
    xi = [B_.take(TT, F32), B_.take(TT, F32)]
    xo = [B_.take(TT, F32), B_.take(TT, F32)]
    onescol_b = P.sbufs["c_onescol_b"]
    onesrow_f = P.sbufs["c_onesrow_f"]
    hv = hln.rearrange("p (c w) -> p c w", c=DC)
    kvv = kv.rearrange("p (c w) -> p c w", c=4)
    qv = qg.rearrange("p (c w) -> p c w", c=4)
    Wq, Wkv = t["L1_attn_Wq"], t["L1_attn_Wkv"]
    cm = {}

    def proj_block(wt, c0, dst_view, bias, kind):
        dsem = f"D_w"
        ld = P.dma("gpsimd", w_sl[:, :].rearrange("p (c w) -> p c w", c=DC),
                   wt.ap()[:, c0:c0 + TT].rearrange("(c p) w -> p c w", p=128),
                   dsem, waits=[("pe", P.counts["pe"])])
        wv = w_sl[:, :].rearrange("p (c w) -> p c w", c=DC)
        for ti in range(4):
            t0 = ti * TT
            for oc in range(4):
                bk = (ti * 4 + oc) % 4
                pe_done = mm_group(P, P.bank(bk).ap()[:, :],
                                   [(wv[:, c, oc * 128:(oc + 1) * 128],
                                     hv[:, c, t0:t0 + TT]) for c in range(DC)],
                                   waits=[(dsem, ld), cm.get(("bk", bk), ("dve", 0))])
                d = P.emit("vector", lambda v, bk=bk, oc=oc, t0=t0: v.tensor_scalar(
                    dst_view[:, oc, t0:t0 + TT], P.bank(bk).ap()[:, :],
                    bias[:, (c0 // 128 + oc) if kind == "q" else oc:
                         (c0 // 128 + oc) if kind == "q" else oc][:, 0:0] if False else
                    bias[:, (c0 // 128 + oc) if kind == "q" else oc:
                         ((c0 // 128 + oc) if kind == "q" else oc) + 1],
                    None, OP.add),
                    waits=[("pe", pe_done)], inc="dve")
                cm[("bk", bk)] = ("dve", d)

    proj_block(Wkv, 0, kvv, bkv, "kv")
    kddone = P.dma("gpsimd", kdup.rearrange("p (c w) -> p c w", c=4).__getitem__((slice(64, 128), slice(None), slice(None))),
                   kv.rearrange("p (c w) -> p c w", c=4)[0:64, :, :],
                   f"D_kd", waits=[P.now("dve")])
    kdv = kdup.rearrange("p (c w) -> p c w", c=4)
    for g in range(4):
        proj_block(Wq, g * TT, qv, bq, "q")
        for h in range(8):
            f0 = g * 512 + h * 64
            cq = (h * 64) // 128
            pq = (h * 64) % 128
            ksrc = (kvv[0:64, g, :] if pq == 0 else kdv[64:128, g, :])
            ps = slice(pq, pq + 64)
            dp = P.emit("vector", lambda v, cq=cq, ps=ps, ksrc=ksrc: v.tensor_mul(
                prod[ps, :], qv[ps.start:ps.stop, cq, :], ksrc),
                waits=[("pe", cm.get(("pr",), 0)), (f"D_kd", kddone)], inc="dve")
            for ti in range(4):
                t0 = ti * TT
                bk = 4 + ti % 2
                pe1 = P.emit("tensor", lambda tt, t0=t0, bk=bk, ps=ps: tt.matmul(
                    P.bank(bk).ap()[0:1, :], onescol_b[ps, 0:1], prod[ps, t0:t0 + TT],
                    start=True, stop=True),
                    waits=[("dve", dp), ("dve", cm.get(("sb", bk), 0))], inc="pe")
                d1 = P.emit("vector", lambda v, bk=bk: v.tensor_copy(srow[0:1, :], P.bank(bk).ap()[0:1, :]),
                            waits=[("pe", pe1)], inc="dve")
                cm[("sb", bk)] = d1
                P.emit("vector", lambda v: v.tensor_sub(zrow[0:1, :], srow[0:1, :], srow[0:1, :]),
                       waits=[P.now("act")], inc="dve")
                ae = P.emit("scalar", lambda s: s.activation(erow[0:1, :], zrow[0:1, :], AF.Exp),
                            waits=[P.now("dve")], inc="act")
                dr = P.emit("vector", lambda v: v.reciprocal(rrow[0:1, :], erow[0:1, :]),
                            waits=[("act", ae)], inc="dve")
                da = P.emit("vector", lambda v: v.tensor_mul(arow[0:1, :], erow[0:1, :], rrow[0:1, :]),
                            inc="dve")
                bk2 = 6 + ti % 2
                pe2 = P.emit("tensor", lambda tt, bk2=bk2: tt.matmul(
                    P.bank(bk2).ap()[:, :], onesrow_f[0:1, :], arow[0:1, :],
                    start=True, stop=True),
                    waits=[("dve", da), ("dve", cm.get(("ab", bk2), 0))], inc="pe")
                xp = ti % 2
                xsem = f"D_xi{xp}"
                xl = P.dma("gpsimd", xi[xp][64:128, :],
                           xin.ap()[f0:f0 + 64, HALO + t0:HALO + t0 + TT],
                           xsem, waits=[("dve", cm.get(("xi", xp), 0))])
                d2 = P.emit("vector", lambda v, g=g, bk2=bk2, t0=t0: v.tensor_mul(
                    tv[64:128, :], kvv[64:128, g, t0:t0 + TT], P.bank(bk2).ap()[64:128, :]),
                    waits=[("pe", pe2)], inc="dve")
                cm[("ab", bk2)] = d2
                d3 = P.emit("vector", lambda v, xp=xp: v.tensor_add(
                    xo[xp][64:128, :], tv[64:128, :], xi[xp][64:128, :]),
                    waits=[(xsem, xl), (f"D_xo{xp}", cm.get(("xo", xp), 0))], inc="dve")
                cm[("xi", xp)] = d3
                st = P.dma("sync", xout.ap()[f0:f0 + 64, HALO + t0:HALO + t0 + TT],
                           xo[xp][64:128, :], f"D_xo{xp}", waits=[("dve", d3)])
                cm[("xo", xp)] = st
            cm[("pr",)] = P.counts["pe"]
    P.barrier()


def emit_ffn1(P, t, xin, xout, tag, fold_wd=False):
    A = Carve(P.sbufs["arA"], ARENA_A)
    B_ = Carve(P.sbufs["arB"], ARENA_B)
    H = Carve(P.sbufs["arH"], ARENA_H)
    had = P.sbufs["c_had128"]
    idb, nidb = P.sbufs["c_id128b"], P.sbufs["c_negid128b"]
    su = P.sbufs["col_l1su"]
    wsg = P.sbufs["col_l1wsg"]
    combo = P.sbufs["col_l1combo"]
    wsd = P.sbufs["col_l1wsd"]

    if fold_wd:
        fbA = A.take(64 * 256)
        fbB = B_.take(64 * 256)
        fv = [fbA.rearrange("p (c w) -> p c w", c=64), fbB.rearrange("p (c w) -> p c w", c=64)]
        wdT = t["L1_ffn_W_downT"]
        cm = {}
        for ot in range(8):
            ld = P.dma("gpsimd", fv[0],
                       wdT.ap()[:, ot * 256:(ot + 1) * 256].rearrange("(c p) w -> p c w", p=128),
                       f"D_fw", waits=[("pe", cm.get(("lda",), 0)), P.now("dve")])
            cur = 0
            first_w = [(f"D_fw", ld), P.now("dve")]
            for c in range(64):
                bk = c % 4
                pe1 = P.emit("tensor", lambda tt, src_ap=fv[cur][:, c, :], bk=bk: tt.matmul(
                    P.bank(bk).ap()[:, 0:256], had[:, :], src_ap,
                    start=True, stop=True),
                    waits=(first_w if c == 0 else [P.now("dve")] if c < 4 else
                           [("dve", cm.get(("fb", bk), 0))]), inc="pe")
                d1 = P.emit("vector", lambda v, dst_ap=fv[1 - cur][:, c, :], bk=bk: v.tensor_copy(
                    dst_ap, P.bank(bk).ap()[:, 0:256]),
                    waits=[("pe", pe1)], inc="dve")
                cm[("fb", bk)] = d1
            cur = 1 - cur
            for sbit in range(6):
                bit = 1 << sbit
                for c in range(64):
                    bk = c % 4
                    pa = c & ~bit
                    pb = c | bit
                    sgn = nidb if (c & bit) else idb
                    pe1 = mm_group(P, P.bank(bk).ap()[:, 0:256],
                                   [(idb[:, :], fv[cur][:, pa, :]),
                                    (sgn[:, :], fv[cur][:, pb, :])],
                                   waits=([P.now("dve")] if c < 4 else [("dve", cm.get(("fb", bk), 0))]))
                    d1 = P.emit("vector", lambda v, dst_ap=fv[1 - cur][:, c, :], bk=bk: v.tensor_copy(
                        dst_ap, P.bank(bk).ap()[:, 0:256]),
                        waits=[("pe", pe1)], inc="dve")
                    cm[("fb", bk)] = d1
                cur = 1 - cur
            st = P.dma("sync", t["wdp_s"].ap()[:, ot * 256:(ot + 1) * 256].rearrange("(c p) w -> p c w", p=128),
                       fv[cur], f"D_fo", waits=[P.now("dve")])
            cm[("lda",)] = P.counts["pe"]
        P.barrier()
        A = Carve(P.sbufs["arA"], ARENA_A)
        B_ = Carve(P.sbufs["arB"], ARENA_B)

    hln = H.take(DC * NT)

    def hw(c, t0, W):
        return hln[:, c * NT + (t0 - HALO): c * NT + (t0 - HALO) + W]

    emit_ln(P, xin, hw, su_col=su, tiles=TILES_MAIN, tag=f"{tag}0", ar1=A, ar2=B_)
    P.barrier()
    A = Carve(P.sbufs["arA"], ARENA_A)
    B_ = Carve(P.sbufs["arB"], ARENA_B)
    wbA = A.take(DC * TT)
    wbB = B_.take(DC * TT)
    wv2 = [wbA.rearrange("p (c w) -> p c w", c=DC), wbB.rearrange("p (c w) -> p c w", c=DC)]
    hv = hln.rearrange("p (c w) -> p c w", c=DC)
    cm = {}
    for ti in range(4):
        t0 = ti * TT
        cur = 0
        for c in range(DC):
            bk = c % 4
            pe1 = P.emit("tensor", lambda tt, c=c, t0=t0, bk=bk: tt.matmul(
                P.bank(bk).ap()[:, :], had[:, :], hv[:, c, t0:t0 + TT],
                start=True, stop=True),
                waits=([P.now("dve")] if c < 4 else [("dve", cm.get(("fb", bk), 0))]), inc="pe")
            d1 = P.emit("vector", lambda v, dst_ap=wv2[0][:, c, :], bk=bk: v.tensor_copy(
                dst_ap, P.bank(bk).ap()[:, :]),
                waits=[("pe", pe1)], inc="dve")
            cm[("fb", bk)] = d1
        for sbit in range(4):
            bit = 1 << sbit
            last = sbit == 3
            for c in range(DC):
                bk = c % 4
                pa = c & ~bit
                pb = c | bit
                sgn = nidb if (c & bit) else idb
                pe1 = mm_group(P, P.bank(bk).ap()[:, :],
                               [(idb[:, :], wv2[cur][:, pa, :]),
                                (sgn[:, :], wv2[cur][:, pb, :])],
                               waits=[("dve", cm.get(("fb", bk), 0))])
                dst = (hv[:, c, t0:t0 + TT] if last else wv2[1 - cur][:, c, :])
                d1 = P.emit("vector", lambda v, dst=dst, bk=bk: v.tensor_copy(
                    dst, P.bank(bk).ap()[:, :]),
                    waits=[("pe", pe1)], inc="dve")
                cm[("fb", bk)] = d1
            cur = 1 - cur
    P.barrier()

    A = Carve(P.sbufs["arA"], ARENA_A)
    B_ = Carve(P.sbufs["arB"], ARENA_B)
    wg = [A.take(DC * TT), B_.take(DC * TT)]
    wu = [A.take(DC * TT), B_.take(DC * TT)]
    sg_sb = [A.take(TT), A.take(TT)]
    h_sb = [A.take(TT) for _ in range(4)]
    hmid = t["hmid"]
    wgT, wuT = t["L1_ffn_W_gateT"], t["L1_ffn_W_upT"]
    bank_free = {}
    slot_free = {}
    hs_free = {}
    sg_free = {}
    for ob in range(16):
        wpar = ob % 2
        dsem = f"D_w{wpar}"
        for (wt, dst) in ((wgT, wg[wpar]), (wuT, wu[wpar])):
            ld = P.dma("gpsimd", dst[:, :].rearrange("p (c w) -> p c w", c=DC),
                       wt.ap()[:, ob * TT:(ob + 1) * TT].rearrange("(c p) w -> p c w", p=128),
                       dsem, waits=[("pe", slot_free.get(wpar, 0))])
        for ti in range(4):
            t0 = ti * TT
            for oc in range(4):
                bsel = (ti * 4 + oc) % 2
                bg, bu = P.bank(bsel).ap(), P.bank(2 + bsel).ap()
                og = ob * 4 + oc
                gw = wg[wpar][:, :].rearrange("p (c w) -> p c w", c=DC)
                uw = wu[wpar][:, :].rearrange("p (c w) -> p c w", c=DC)
                mm_group(P, bg[:, :],
                         [(gw[:, c, oc * 128:(oc + 1) * 128], hv[:, c, t0:t0 + TT]) for c in range(DC)],
                         waits=[(dsem, ld), bank_free.get(bsel, ("pe", 0))])
                pe_done = mm_group(P, bu[:, :],
                                   [(uw[:, c, oc * 128:(oc + 1) * 128], hv[:, c, t0:t0 + TT]) for c in range(DC)],
                                   waits=[bank_free.get(2 + bsel, ("pe", 0))])
                sgp = (ti * 4 + oc) % 2
                a = P.emit("scalar", lambda s, bg=bg, og=og, sgp=sgp: s.activation(
                    sg_sb[sgp][:, :], bg[:, :], AF.Silu, scale=wsg[:, og:og + 1]),
                    waits=[("pe", pe_done), ("dve", sg_free.get(sgp, 0))], inc="act")
                bank_free[bsel] = ("act", a)
                hp = (ti * 4 + oc) % 4
                d = P.emit("vector", lambda v, bu=bu, og=og, sgp=sgp, hp=hp: v.scalar_tensor_tensor(
                    h_sb[hp][:, :], sg_sb[sgp][:, :], combo[:, og:og + 1], bu[:, :],
                    OP.mult, OP.mult),
                    waits=[("act", a), (f"D_h{hp}", hs_free.get(hp, 0))], inc="dve")
                bank_free[2 + bsel] = ("dve", d)
                sg_free[sgp] = d
                st = P.dma("sync", hmid.ap()[og * 128:(og + 1) * 128, t0:t0 + TT],
                           h_sb[hp][:, :], f"D_h{hp}", waits=[("dve", d)])
                hs_free[hp] = st
        slot_free[wpar] = P.counts["pe"]
    P.barrier()

    A = Carve(P.sbufs["arA"], ARENA_A)
    B_ = Carve(P.sbufs["arB"], ARENA_B)
    H2 = Carve(P.sbufs["arH"], ARENA_H)
    OBW = 256
    wd = [A.take(64 * OBW), B_.take(64 * OBW)]
    hq = [H2.take(DC * TT), H2.take(DC * TT)]
    xi = [H2.take(2 * TT, F32), H2.take(2 * TT, F32)]
    xo = [H2.take(2 * TT, F32), H2.take(2 * TT, F32)]
    wdp = t["wdp_s"]
    bank_free = {}
    slot_free = {}
    xo_free = {}
    xi_free = {}
    hq_free = {}
    for ob in range(8):
        wpar = ob % 2
        dsem = f"D_wd{wpar}"
        ld = P.dma("gpsimd", wd[wpar][:, :].rearrange("p (c w) -> p c w", c=64),
                   wdp.ap()[:, ob * OBW:(ob + 1) * OBW].rearrange("(c p) w -> p c w", p=128),
                   dsem, waits=[("pe", slot_free.get(wpar, 0))])
        for ti in range(4):
            t0 = HALO + ti * TT
            tb = (ti % 2) * 2
            hpar = ti % 2
            xsem = f"D_xi{hpar}"
            xl = P.dma("gpsimd", xi[hpar][:, :].rearrange("p (c w) -> p c w", c=2),
                       xin.ap()[ob * OBW:(ob + 1) * OBW, t0:t0 + TT].rearrange("(c p) w -> p c w", p=128),
                       xsem, waits=[("dve", xi_free.get(hpar, 0))])
            hsem = f"D_hq{hpar}"
            wdv = wd[wpar][:, :].rearrange("p (c w) -> p c w", c=64)
            for kq in range(4):
                hl = P.dma("gpsimd", hq[hpar][:, :].rearrange("p (c w) -> p c w", c=DC),
                           hmid.ap()[kq * 2048:(kq + 1) * 2048, t0 - HALO:t0 - HALO + TT]
                           .rearrange("(c p) w -> p c w", p=128),
                           hsem, waits=[("pe", hq_free.get(hpar, 0) if kq == 0 else P.counts["pe"])])
                hqv = hq[hpar][:, :].rearrange("p (c w) -> p c w", c=DC)
                for oc in range(2):
                    wts = [(hsem, hl)]
                    if kq == 0:
                        wts += [bank_free.get(tb + oc, ("dve", 0))]
                    for c in range(DC):
                        gk = kq * DC + c
                        P.emit("tensor", lambda t_, gk=gk, oc=oc, c=c, wdv=wdv, hqv=hqv, tb=tb:
                               t_.matmul(P.bank(tb + oc).ap()[:, :],
                                         wdv[:, gk, oc * 128:(oc + 1) * 128], hqv[:, c, :],
                                         start=(gk == 0), stop=(gk == 63)),
                               waits=(wts if c == 0 else ()),
                               inc=("pe" if c == DC - 1 else None))
            hq_free[hpar] = P.counts["pe"]
            pe_done = P.counts["pe"]
            for oc in range(2):
                og = ob * 2 + oc
                d = P.emit("vector", lambda v, oc=oc, og=og, hpar=hpar, tb=tb: v.scalar_tensor_tensor(
                    xo[hpar][:, oc * TT:(oc + 1) * TT], P.bank(tb + oc).ap()[:, :], wsd[:, og:og + 1],
                    xi[hpar][:, oc * TT:(oc + 1) * TT], OP.mult, OP.add),
                    waits=[("pe", pe_done), (xsem, xl), (f"D_xo{hpar}", xo_free.get(hpar, 0))],
                    inc="dve")
                bank_free[tb + oc] = ("dve", d)
            xi_free[hpar] = P.counts["dve"]
            st = P.dma("sync", xout.ap()[ob * OBW:(ob + 1) * OBW, t0:t0 + TT].rearrange("(c p) w -> p c w", p=128),
                       xo[hpar][:, :].rearrange("p (c w) -> p c w", c=2),
                       f"D_xo{hpar}", waits=[P.now("dve")]) 
            xo_free[hpar] = st
        slot_free[wpar] = P.counts["pe"]
    P.barrier()


def declare_ffn1_tensors(nc, kind_x="ExternalInput", kind_y="ExternalOutput"):
    t = {}
    for nm, shape, dt in CONST_SPECS:
        t[nm] = nc.dram_tensor(nm, shape, dt, kind="ExternalInput")
    t["L1_ffn_W_upT"] = nc.dram_tensor("L1_ffn_W_upT", [DM, INTER], F32, kind="ExternalInput")
    t["L1_ffn_W_gateT"] = nc.dram_tensor("L1_ffn_W_gateT", [DM, INTER], F32, kind="ExternalInput")
    t["L1_ffn_W_downT"] = nc.dram_tensor("L1_ffn_W_downT", [INTER, DM], F32, kind="ExternalInput")
    t["L1_ffn_SU_ug"] = nc.dram_tensor("L1_ffn_SU_ug", [DM], F32, kind="ExternalInput")
    t["L1_ffn_SU_dp"] = nc.dram_tensor("L1_ffn_SU_dp", [INTER], F32, kind="ExternalInput")
    t["L1_ffn_Wscale_ug"] = nc.dram_tensor("L1_ffn_Wscale_ug", [2 * INTER], F32, kind="ExternalInput")
    t["L1_ffn_Wscale_dp"] = nc.dram_tensor("L1_ffn_Wscale_dp", [DM], F32, kind="ExternalInput")
    t["xin"] = nc.dram_tensor("xin", [DM, NTH], F32, kind=kind_x)
    t["xout"] = nc.dram_tensor("xout", [DM, NTH], F32, kind=kind_y)
    t["hmid"] = nc.dram_tensor("hmid", [INTER, NT], BF16)
    t["wdp_s"] = nc.dram_tensor("wdp_s", [INTER, DM], BF16)
    return t


def emit_ffn1_cols(P, t):
    import math
    s2 = 1.0 / math.sqrt(2048.0)
    s8 = 1.0 / math.sqrt(8192.0)
    build_colset(P, "l1su", t["L1_ffn_SU_ug"], DM)
    build_colset(P, "l1wsg", t["L1_ffn_Wscale_ug"].ap()[INTER:2 * INTER], INTER, post_scale=s2)
    wsu = build_colset(P, "l1wsu", t["L1_ffn_Wscale_ug"].ap()[0:INTER], INTER, post_scale=s2)
    sdp = build_colset(P, "l1sdp", t["L1_ffn_SU_dp"], INTER)
    build_colset(P, "l1wsd", t["L1_ffn_Wscale_dp"], DM, post_scale=0.5 * s8)
    combo = P.sbuf("col_l1combo", [128, 64], F32)
    P.emit("vector", lambda v: v.tensor_mul(combo[:, :], wsu[:, :], sdp[:, :]),
           waits=[P.now("dve"), P.now("pe")], inc="dve")


def declare_attn1_tensors(nc, kind_x="ExternalInput", kind_y="ExternalOutput"):
    t = {}
    for nm, shape, dt in CONST_SPECS:
        t[nm] = nc.dram_tensor(nm, shape, dt, kind="ExternalInput")
    t["L1_attn_Wq"] = nc.dram_tensor("L1_attn_Wq", [DM, DM], F32, kind="ExternalInput")
    t["L1_attn_bq"] = nc.dram_tensor("L1_attn_bq", [DM], F32, kind="ExternalInput")
    t["L1_attn_Wkv"] = nc.dram_tensor("L1_attn_Wkv", [DM, 512], F32, kind="ExternalInput")
    t["L1_attn_bkv"] = nc.dram_tensor("L1_attn_bkv", [512], F32, kind="ExternalInput")
    t["xin"] = nc.dram_tensor("xin", [DM, NTH], F32, kind=kind_x)
    t["xout"] = nc.dram_tensor("xout", [DM, NTH], F32, kind=kind_y)
    return t


def emit_attn1_cols(P, t):
    build_colset(P, "l1bq", t["L1_attn_bq"], DM)
    build_colset(P, "l1bkv", t["L1_attn_bkv"], 512)


def build_ffn1_program(fold=True):
    from contextlib import ExitStack
    nc = bass.Bass("TRN2", target_bir_lowering=False)
    ctx = ExitStack()
    t = declare_ffn1_tensors(nc)
    P = Prog(nc, ctx)
    with ctx:
        alloc_arenas(P)
        emit_consts(P, t)
        P.barrier()
        emit_ffn1_cols(P, t)
        P.barrier()
        emit_ffn1(P, t, t["xin"], t["xout"], "f1", fold_wd=fold)
        P.finish()
    return nc


def build_attn1_program():
    from contextlib import ExitStack
    nc = bass.Bass("TRN2", target_bir_lowering=False)
    ctx = ExitStack()
    t = declare_attn1_tensors(nc)
    P = Prog(nc, ctx)
    with ctx:
        alloc_arenas(P)
        emit_consts(P, t)
        P.barrier()
        emit_attn1_cols(P, t)
        P.barrier()
        emit_attn1(P, t, t["xin"], t["xout"], "a")
        P.finish()
    return nc


def host_prep_ffn1(inputs):
    d = {}
    d["L1_ffn_W_upT"] = np.ascontiguousarray(inputs["L1_ffn_W_up"].T)
    d["L1_ffn_W_gateT"] = np.ascontiguousarray(inputs["L1_ffn_W_gate"].T)
    d["L1_ffn_W_downT"] = np.ascontiguousarray(inputs["L1_ffn_W_down"].T)
    for nm in ["L1_ffn_SU_ug", "L1_ffn_SU_dp", "L1_ffn_Wscale_ug", "L1_ffn_Wscale_dp"]:
        d[nm] = inputs[nm]
    return d


def host_prep_attn1(inputs):
    return {nm: inputs[nm] for nm in ["L1_attn_Wq", "L1_attn_bq", "L1_attn_Wkv", "L1_attn_bkv"]}


def generic_in_map(prep, extra, xshard, core):
    m = {nm: np.asarray(prep[nm]) for nm, _, _ in CONST_SPECS if nm != "oddcol"}
    m["oddcol"] = prep["oddcol"][core]
    m.update(extra)
    m["xin"] = xshard
    return m


def declare_all_tensors(nc):
    t = {}
    for nm, shape, dt in CONST_SPECS:
        t[nm] = nc.dram_tensor(nm, shape, dt, kind="ExternalInput")
    t["L0_ffn_gate_wT"] = nc.dram_tensor("L0_ffn_gate_wT", [DM, INTER], I32, kind="ExternalInput")
    t["L0_ffn_up_wT"] = nc.dram_tensor("L0_ffn_up_wT", [DM, INTER], I32, kind="ExternalInput")
    t["L0_ffn_down_wT"] = nc.dram_tensor("L0_ffn_down_wT", [INTER, DM], I32, kind="ExternalInput")
    t["L0_ffn_gate_scale"] = nc.dram_tensor("L0_ffn_gate_scale", [INTER], F32, kind="ExternalInput")
    t["L0_ffn_up_scale"] = nc.dram_tensor("L0_ffn_up_scale", [INTER], F32, kind="ExternalInput")
    t["L0_ffn_down_scale"] = nc.dram_tensor("L0_ffn_down_scale", [DM], F32, kind="ExternalInput")
    t["L0_attn_in_proj_wT"] = nc.dram_tensor("L0_attn_in_proj_wT", [DM, 2 * D_INNER], F32, kind="ExternalInput")
    t["L0_attn_x_proj_wT"] = nc.dram_tensor("L0_attn_x_proj_wT", [D_INNER, 4416], F32, kind="ExternalInput")
    t["L0_attn_dt_proj_wT"] = nc.dram_tensor("L0_attn_dt_proj_wT", [64, 64], F32, kind="ExternalInput")
    t["L0_attn_out_proj_wT"] = nc.dram_tensor("L0_attn_out_proj_wT", [D_INNER, DM], F32, kind="ExternalInput")
    t["L0_attn_conv_wT"] = nc.dram_tensor("L0_attn_conv_wT", [4, D_INNER], F32, kind="ExternalInput")
    t["L0_attn_conv_b"] = nc.dram_tensor("L0_attn_conv_b", [D_INNER], F32, kind="ExternalInput")
    t["L0_attn_dt_proj_b"] = nc.dram_tensor("L0_attn_dt_proj_b", [64], F32, kind="ExternalInput")
    t["L0_attn_A_log"] = nc.dram_tensor("L0_attn_A_log", [64], F32, kind="ExternalInput")
    t["L0_attn_D"] = nc.dram_tensor("L0_attn_D", [64], F32, kind="ExternalInput")
    t["L1_attn_Wq"] = nc.dram_tensor("L1_attn_Wq", [DM, DM], F32, kind="ExternalInput")
    t["L1_attn_bq"] = nc.dram_tensor("L1_attn_bq", [DM], F32, kind="ExternalInput")
    t["L1_attn_Wkv"] = nc.dram_tensor("L1_attn_Wkv", [DM, 512], F32, kind="ExternalInput")
    t["L1_attn_bkv"] = nc.dram_tensor("L1_attn_bkv", [512], F32, kind="ExternalInput")
    t["L1_ffn_W_upT"] = nc.dram_tensor("L1_ffn_W_upT", [DM, INTER], F32, kind="ExternalInput")
    t["L1_ffn_W_gateT"] = nc.dram_tensor("L1_ffn_W_gateT", [DM, INTER], F32, kind="ExternalInput")
    t["L1_ffn_W_downT"] = nc.dram_tensor("L1_ffn_W_downT", [INTER, DM], F32, kind="ExternalInput")
    t["L1_ffn_SU_ug"] = nc.dram_tensor("L1_ffn_SU_ug", [DM], F32, kind="ExternalInput")
    t["L1_ffn_SU_dp"] = nc.dram_tensor("L1_ffn_SU_dp", [INTER], F32, kind="ExternalInput")
    t["L1_ffn_Wscale_ug"] = nc.dram_tensor("L1_ffn_Wscale_ug", [2 * INTER], F32, kind="ExternalInput")
    t["L1_ffn_Wscale_dp"] = nc.dram_tensor("L1_ffn_Wscale_dp", [DM], F32, kind="ExternalInput")
    t["xin"] = nc.dram_tensor("xin", [DM, NTH], F32, kind="ExternalInput")
    t["xout"] = nc.dram_tensor("xout", [DM, NTH], F32, kind="ExternalOutput")
    t["xA"] = nc.dram_tensor("xA", [DM, NTH], F32)
    t["xB"] = nc.dram_tensor("xB", [DM, NTH], F32)
    t["hmid"] = nc.dram_tensor("hmid", [INTER, NTH], BF16)
    t["wdp_s"] = nc.dram_tensor("wdp_s", [INTER, DM], BF16)
    t["hln2"] = nc.dram_tensor("hln2", [DM, NTH], BF16)
    t["z_s"] = nc.dram_tensor("z_s", [D_INNER, NT], BF16)
    t["u_s"] = nc.dram_tensor("u_s", [D_INNER, NT], BF16)
    t["xtm_s"] = nc.dram_tensor("xtm_s", [NT, D_INNER], BF16)
    t["y_s"] = nc.dram_tensor("y_s", [NT, D_INNER], BF16)
    t["ccin"] = nc.dram_tensor("ccin", [128, 4096], F32)
    t["ccout"] = nc.dram_tensor("ccout", [256, 4096], F32)
    return t


def build_fused_program():
    from contextlib import ExitStack
    nc = bass.Bass("TRN2", target_bir_lowering=False)
    ctx = ExitStack()
    t = declare_all_tensors(nc)
    P = Prog(nc, ctx)
    with ctx:
        alloc_arenas(P)
        emit_consts(P, t)
        P.barrier()
        build_colset(P, "l0g", t["L0_ffn_gate_scale"], INTER)
        build_colset(P, "l0u", t["L0_ffn_up_scale"], INTER)
        build_colset(P, "l0d", t["L0_ffn_down_scale"], DM, post_scale=0.5)
        emit_mamba_cols(P, t)
        emit_ffn1_cols(P, t)
        emit_attn1_cols(P, t)
        P.barrier()
        emit_ffn0(P, t, t["xin"], t["xA"], "s1")
        emit_mamba(P, t, t["xA"], t["xB"], "s2")
        emit_ffn0(P, t, t["xB"], t["xA"], "s3")
        emit_ffn1(P, t, t["xA"], t["xB"], "s4", fold_wd=True)
        emit_attn1(P, t, t["xB"], t["xA"], "s5")
        emit_ffn1(P, t, t["xA"], t["xout"], "s6", fold_wd=False)
        P.barrier()
        P.finish()
    return nc


def fused_in_map(inputs, prep, prepm, prepf1, prepa1, xshard, core):
    m = {nm: np.asarray(prep[nm]) for nm, _, _ in CONST_SPECS if nm != "oddcol"}
    m["oddcol"] = prep["oddcol"][core]
    for nm in ["L0_ffn_gate_wT", "L0_ffn_up_wT", "L0_ffn_down_wT",
               "L0_ffn_gate_scale", "L0_ffn_up_scale", "L0_ffn_down_scale"]:
        m[nm] = prep[nm]
    m.update(prepm)
    m.update(prepf1)
    m.update(prepa1)
    m["xin"] = xshard
    return m


def kernel_fused(**inputs):
    x = inputs["L0_attn_x"]
    prep = host_prep_common(inputs)
    prepm = host_prep_mamba(inputs)
    prepf1 = host_prep_ffn1(inputs)
    prepa1 = host_prep_attn1(inputs)
    nc = build_fused_program()
    shards = shard_x(x)
    maps = [fused_in_map(inputs, prep, prepm, prepf1, prepa1, shards[k], k)
            for k in range(NCORES)]
    res = run_bass_kernel_spmd(nc, maps, core_ids=list(range(NCORES)))
    return _gather(res)


def kernel(**inputs):
    return kernel_fused(**inputs)


# version 03
# speedup vs baseline: 1.4274x; 1.4274x over previous
import numpy as np
import ml_dtypes

import concourse.bass as bass
import concourse.mybir as mybir
from concourse.bass_utils import run_bass_kernel_spmd

F32 = mybir.dt.float32
BF16 = mybir.dt.bfloat16
I32 = mybir.dt.int32
AF = mybir.ActivationFunctionType
OP = mybir.AluOpType

NCORES = 8
DM = 2048
NT = 2048
HALO = 4
NTH = NT + HALO
DC = DM // 128
TT = 512
TILES_ALL = [(0, TT), (TT, TT), (2 * TT, TT), (3 * TT, TT), (4 * TT, HALO)]
TILES_MAIN = [(HALO + i * TT, TT) for i in range(4)]

D_INNER = 4096
NHEADS = 64
HEADDIM = 64
D_STATE = 128
CHUNK = 64
NCHUNK = NT // CHUNK
INTER = 8192

ENGINES = ("tensor", "scalar", "vector", "gpsimd", "sync")

BF = ml_dtypes.bfloat16


class Prog:
    def __init__(self, nc, ctx):
        self.nc = nc
        self._ctx = ctx
        self.ops = {e: [] for e in ENGINES}
        self.sems = {}
        self.counts = {}
        self.sbufs = {}
        self.psums = {}
        self.bar_tgt = 0

    def sem(self, name):
        if name not in self.sems:
            self.sems[name] = self._ctx.enter_context(self.nc.semaphore(name))
            self.counts[name] = 0
        return self.sems[name]

    def sbuf(self, name, shape, dtype):
        if name not in self.sbufs:
            self.sbufs[name] = self._ctx.enter_context(self.nc.sbuf_tensor(name, shape, dtype))
        return self.sbufs[name]

    def bank(self, i):
        nm = f"pb{i}"
        if nm not in self.psums:
            self.psums[nm] = self._ctx.enter_context(self.nc.psum_tensor(nm, [128, 512], F32))
        return self.psums[nm]

    def bank_bf(self, i):
        return self.bank(i).ap().bitcast(BF16)

    def emit(self, eng, fn, waits=(), inc=None, amt=1, is_dma=False):
        if inc:
            self.sem(inc)
        for s, _ in waits:
            self.sem(s)
        new = None
        if inc is not None:
            self.counts[inc] += amt
            new = self.counts[inc]
        hard = (not is_dma) and eng in ("scalar", "vector", "gpsimd")

        def closure(e, fn=fn, waits=tuple(waits), inc=inc, amt=amt, P=self, hard=hard):
            for s, v in waits:
                if v > 0:
                    e.wait_ge(P.sems[s], v)
            inst = fn(e)
            if inc is not None:
                if inst is None:
                    e.sem_inc(P.sems[inc], amt)
                elif hard:
                    e.drain()
                    e.sem_inc(P.sems[inc], amt)
                else:
                    inst.then_inc(P.sems[inc], amt)

        self.ops[eng].append(closure)
        return new

    def dma(self, q, out, in_, sem, waits=(), **kw):
        return self.emit(q, lambda e: e.dma_start(out=out, in_=in_, **kw), waits, sem, 16,
                         is_dma=True)

    def now(self, sem):
        return (sem, self.counts.get(sem, 0))

    def barrier(self):
        self.sem("bar")
        dma_waits = tuple((s, self.counts[s]) for s in self.counts if s.startswith("D_"))
        self.counts["bar"] += len(ENGINES)
        tgt = self.counts["bar"]
        for e in ENGINES:
            def closure(eng, dw=dma_waits, tgt=tgt, P=self, e=e):
                for s, v in dw:
                    if v > 0:
                        eng.wait_ge(P.sems[s], v)
                if e != "sync":
                    eng.drain()
                eng.sem_inc(P.sems["bar"], 1)
                eng.wait_ge(P.sems["bar"], tgt)
            self.ops[e].append(closure)

    def finish(self):
        nc = self.nc
        with nc.Block() as block:
            @block.tensor
            def _(t):
                for f in self.ops["tensor"]:
                    f(t)

            @block.scalar
            def _(s):
                for f in self.ops["scalar"]:
                    f(s)

            @block.vector
            def _(v):
                for f in self.ops["vector"]:
                    f(v)

            @block.gpsimd
            def _(g):
                for f in self.ops["gpsimd"]:
                    f(g)

            @block.sync
            def _(s):
                for f in self.ops["sync"]:
                    f(s)


class Carve:

    def __init__(self, handle, elems):
        self.h = handle
        self.off = 0
        self.n = elems

    def take(self, elems, dtype=BF16):
        e2 = elems if dtype == BF16 else 2 * elems
        off = (self.off + 1) & ~1
        self.off = off + e2
        assert self.off <= self.n, f"arena overflow {self.off} > {self.n}"
        v = self.h[:, off:off + e2]
        if dtype != BF16:
            v = v.bitcast(dtype)
        return v


def mm_group(P, out_ap, pairs, waits=(), inc="pe"):
    n = len(pairs)
    last = None
    for j, (l, r) in enumerate(pairs):
        last = P.emit(
            "tensor",
            lambda t, l=l, r=r, s=(j == 0), st=(j == n - 1), o=out_ap: t.matmul(o, l, r, start=s, stop=st),
            waits=(waits if j == 0 else ()),
            inc=(inc if j == n - 1 else None),
        )
    return last


CONST_SPECS = [
    ("had128", [128, 128], BF16), ("id128b", [128, 128], BF16),
    ("negid128b", [128, 128], BF16), ("rowsel", [64, 64 * 64], BF16),
    ("onescol_b", [128, 1], BF16),
    ("onesrow_f", [1, 128], F32), ("id128f", [128, 128], F32),
    ("triu64", [64, 64], F32), ("oddcol", [128, 1], F32),
    ("fconst", [128, 4], F32),
]


def emit_consts(P, t):
    for nm, shape, dt in CONST_SPECS:
        dst = P.sbuf("c_" + nm, shape, dt)
        P.dma("gpsimd", dst[:, :], t[nm].ap(), "D_init")


def build_colset(P, name, vec_dram, n, post_scale=None):
    k = n // 128
    assert k <= 128
    stage = P.sbuf("cs_stage", [128, 128], F32)
    out = P.sbuf(f"col_{name}", [128, k], F32)
    ptr = P.bank(7)
    id128f = P.sbufs["c_id128f"]
    vap = vec_dram if isinstance(vec_dram, bass.AP) else vec_dram.ap()
    ld = P.dma("gpsimd", stage[0:k, :], vap.rearrange("(k p) -> k p", p=128),
               "D_cs", waits=[P.now("pe")])
    P.emit("tensor", lambda t: t.transpose(ptr[0:128, 0:k], stage[0:k, :], id128f[0:k, 0:k]),
           waits=[("D_cs", ld), P.now("dve")], inc="pe")
    sc = 1.0 if post_scale is None else float(post_scale)
    P.emit("vector", lambda v: v.tensor_scalar(out[:, :], ptr.ap()[0:128, 0:k], sc, None, OP.mult),
           waits=[P.now("pe")], inc="dve")
    return out


def emit_ln(P, xin, hln_write, su_col=None, tiles=TILES_ALL, tag="ln", ar1=None, ar2=None,
            tile_done=None, extra_waits=None):
    xb = [ar1.take(DC * TT), ar2.take(DC * TT)]
    sq = [ar1.take(DC * TT), ar2.take(DC * TT)]
    rows = ar2.take(4 * TT, F32)[0:1, :]
    sub_s = ar2.take(TT, F32)
    onescol_b = P.sbufs["c_onescol_b"]
    onesrow_f = P.sbufs["c_onesrow_f"]
    sx = [P.bank(0).ap()[0:1, :], P.bank(1).ap()[0:1, :]]
    sxx = [P.bank(2).ap()[0:1, :], P.bank(3).ap()[0:1, :]]
    mubc = [P.bank(4).ap(), P.bank(5).ap()]
    rbc = [P.bank(6).ap(), P.bank(7).ap()]

    for ti, (t0, W) in enumerate(tiles):
        par = ti % 2
        dsem = f"D_x{par}"
        ld = P.dma("gpsimd", xb[par][:, 0:DC * W].rearrange("p (c w) -> p c w", c=DC),
                   xin.ap()[:, t0:t0 + W].rearrange("(c p) w -> p c w", p=128),
                   dsem, waits=[P.now("dve")])
        sq_done = P.emit("scalar", lambda s, par=par, W=W: s.activation(
            sq[par][:, 0:DC * W], xb[par][:, 0:DC * W], AF.Square),
            waits=[(dsem, ld)], inc="act")
        pe_w = [(dsem, ld), ("act", sq_done), P.now("dve")]
        for c in range(DC):
            P.emit("tensor", lambda t, par=par, c=c, W=W: t.matmul(
                sx[par][0:1, 0:W], onescol_b[:, 0:1], xb[par][:, c * W:c * W + W],
                start=(c == 0), stop=(c == DC - 1)),
                waits=(pe_w if c == 0 else ()), inc=("pe" if c == DC - 1 else None))
        for c in range(DC):
            P.emit("tensor", lambda t, par=par, c=c, W=W: t.matmul(
                sxx[par][0:1, 0:W], onescol_b[:, 0:1], sq[par][:, c * W:c * W + W],
                start=(c == 0), stop=(c == DC - 1)),
                inc=("pe" if c == DC - 1 else None))
        pe_stats = P.counts["pe"]
        P.emit("vector", lambda v, par=par, W=W: v.tensor_scalar(
            rows[0:1, 0:W], sx[par][0:1, 0:W], 1.0 / DM, None, OP.mult),
            waits=[("pe", pe_stats)], inc="dve")
        P.emit("vector", lambda v, par=par, W=W: v.tensor_scalar(
            rows[0:1, TT:TT + W], sxx[par][0:1, 0:W], 1.0 / DM, None, OP.mult), inc="dve")
        P.emit("vector", lambda v, W=W: v.tensor_mul(
            rows[0:1, 2 * TT:2 * TT + W], rows[0:1, 0:W], rows[0:1, 0:W]), inc="dve")
        dve_fin = P.emit("vector", lambda v, W=W: v.tensor_sub(
            rows[0:1, 2 * TT:2 * TT + W], rows[0:1, TT:TT + W], rows[0:1, 2 * TT:2 * TT + W]),
            inc="dve")
        fc = P.sbufs["c_fconst"]
        P.emit("scalar", lambda s, W=W, fc=fc: s.activation(
            rows[0:1, TT:TT + W], rows[0:1, 2 * TT:2 * TT + W], AF.Ln, bias=fc[0:1, 1:2]),
            waits=[("dve", dve_fin)], inc="act")
        act_r = P.emit("scalar", lambda s, W=W: s.activation(
            rows[0:1, 2 * TT:2 * TT + W], rows[0:1, TT:TT + W], AF.Exp, scale=-0.5), inc="act")
        P.emit("tensor", lambda t, par=par, W=W: t.matmul(
            mubc[par][:, 0:W], onesrow_f[0:1, :], rows[0:1, 0:W], start=True, stop=True),
            waits=[("act", act_r), P.now("dve")], inc="pe")
        pe_bc = P.emit("tensor", lambda t, par=par, W=W: t.matmul(
            rbc[par][:, 0:W], onesrow_f[0:1, :], rows[0:1, 2 * TT:2 * TT + W],
            start=True, stop=True), inc="pe")
        ew = list(extra_waits(ti)) if extra_waits else []
        for c in range(DC):
            P.emit("vector", lambda v, par=par, c=c, W=W: v.tensor_sub(
                sub_s[:, 0:W], xb[par][:, c * W:c * W + W], mubc[par][:, 0:W]),
                waits=(([("pe", pe_bc)] + ew) if c == 0 else ()), inc="dve")
            dst = hln_write(c, t0, W)
            if su_col is None:
                P.emit("vector", lambda v, dst=dst, par=par, W=W: v.tensor_mul(
                    dst, sub_s[:, 0:W], rbc[par][:, 0:W]), inc="dve")
            else:
                P.emit("vector", lambda v, dst=dst, par=par, c=c, W=W: v.scalar_tensor_tensor(
                    dst, sub_s[:, 0:W], su_col[:, c:c + 1], rbc[par][:, 0:W],
                    OP.mult, OP.mult), inc="dve")
        if tile_done is not None:
            tile_done(ti, t0, W)


def emit_ffn0(P, t, xin, xout, tag, tiles=TILES_ALL):
    A = Carve(P.sbufs["arA"], ARENA_A)
    B_ = Carve(P.sbufs["arB"], ARENA_B)
    H = Carve(P.sbufs["arH"], ARENA_H)
    gsc, usc, dsc = P.sbufs["col_l0g"], P.sbufs["col_l0u"], P.sbufs["col_l0d"]

    hln = Carve(P.sbufs["arH"], ARENA_H).take(DC * NTH)

    def hw(c, t0, W):
        return hln[:, c * NTH + t0: c * NTH + t0 + W]

    emit_ln(P, xin, hw, tiles=tiles, tag=f"{tag}a", ar1=A, ar2=B_)
    P.barrier()

    A = Carve(P.sbufs["arA"], ARENA_A)
    B_ = Carve(P.sbufs["arB"], ARENA_B)
    wg = [A.take(DC * TT), B_.take(DC * TT)]
    wu = [A.take(DC * TT), B_.take(DC * TT)]
    sg_sb = [A.take(TT), A.take(TT)]
    h_sb = [A.take(TT) for _ in range(4)]
    hmid = t["hmid"]
    wgT, wuT = t["L0_ffn_gate_wT"], t["L0_ffn_up_wT"]

    bank_free = {}
    slot_free = {}
    hs_free = {}
    sg_free = {}
    for ob in range(16):
        wpar = ob % 2
        dsem = f"D_w{wpar}"
        for (wt, dst) in ((wgT, wg[wpar]), (wuT, wu[wpar])):
            ld = P.dma("gpsimd", dst[:, :].rearrange("p (c w) -> p c w", c=DC),
                       wt.ap()[:, ob * TT:(ob + 1) * TT].rearrange("(c p) w -> p c w", p=128),
                       dsem, waits=[("pe", slot_free.get(wpar, 0))])
        sh = None
        fc = P.sbufs["c_fconst"]
        for dst in (wg[wpar], wu[wpar]):
            sh = P.emit("scalar", lambda sc, d=dst, fc=fc: sc.activation(
                d[:, :], d[:, :], AF.Identity, bias=fc[:, 0:1]),
                waits=[(dsem, ld)], inc="act")
        for ti, (t0, W) in enumerate(tiles):
            for oc in range(4):
                bsel = (ti * 4 + oc) % 2
                bg, bu = P.bank(bsel).ap(), P.bank(2 + bsel).ap()
                og = ob * 4 + oc
                gw = wg[wpar][:, :].rearrange("p (c w) -> p c w", c=DC)
                uw = wu[wpar][:, :].rearrange("p (c w) -> p c w", c=DC)
                mm_group(P, bg[:, 0:W],
                         [(gw[:, c, oc * 128:(oc + 1) * 128], hw(c, t0, W)) for c in range(DC)],
                         waits=[("act", sh), bank_free.get(bsel, ("pe", 0))])
                pe_done = mm_group(P, bu[:, 0:W],
                                   [(uw[:, c, oc * 128:(oc + 1) * 128], hw(c, t0, W)) for c in range(DC)],
                                   waits=[bank_free.get(2 + bsel, ("pe", 0))])
                sgp = (ti * 4 + oc) % 2
                a = P.emit("scalar", lambda s, bg=bg, og=og, W=W, sgp=sgp: s.activation(
                    sg_sb[sgp][:, 0:W], bg[:, 0:W], AF.Silu, scale=gsc[:, og:og + 1]),
                    waits=[("pe", pe_done), ("dve", sg_free.get(sgp, 0))], inc="act")
                bank_free[bsel] = ("act", a)
                hp = (ti * 4 + oc) % 4
                d = P.emit("vector", lambda v, bu=bu, og=og, W=W, sgp=sgp, hp=hp: v.scalar_tensor_tensor(
                    h_sb[hp][:, 0:W], sg_sb[sgp][:, 0:W], usc[:, og:og + 1], bu[:, 0:W],
                    OP.mult, OP.mult),
                    waits=[("act", a), (f"D_h{hp}", hs_free.get(hp, 0))], inc="dve")
                bank_free[2 + bsel] = ("dve", d)
                sg_free[sgp] = d
                st = P.dma("sync", hmid.ap()[og * 128:(og + 1) * 128, t0:t0 + W],
                           h_sb[hp][:, 0:W], f"D_h{hp}", waits=[("dve", d)])
                hs_free[hp] = st
        slot_free[wpar] = P.counts["pe"]
    P.barrier()

    A = Carve(P.sbufs["arA"], ARENA_A)
    B_ = Carve(P.sbufs["arB"], ARENA_B)
    H = Carve(P.sbufs["arH"], ARENA_H)
    OBW = 256
    wd = [A.take(64 * OBW), B_.take(64 * OBW)]
    hq = [H.take(DC * TT), H.take(DC * TT)]
    xi = [H.take(2 * TT, F32), H.take(2 * TT, F32)]
    xo = [H.take(2 * TT, F32), H.take(2 * TT, F32)]
    wdT = t["L0_ffn_down_wT"]

    bank_free = {}
    slot_free = {}
    xo_free = {}
    xi_free = {}
    hq_free = {}
    for ob in range(8):
        wpar = ob % 2
        dsem = f"D_wd{wpar}"
        ld = P.dma("gpsimd", wd[wpar][:, :].rearrange("p (c w) -> p c w", c=64),
                   wdT.ap()[:, ob * OBW:(ob + 1) * OBW].rearrange("(c p) w -> p c w", p=128),
                   dsem, waits=[("pe", slot_free.get(wpar, 0))])
        fc = P.sbufs["c_fconst"]
        sh = P.emit("scalar", lambda sc, d=wd[wpar], fc=fc: sc.activation(
            d[:, :], d[:, :], AF.Identity, bias=fc[:, 0:1]),
            waits=[(dsem, ld)], inc="act")
        for ti, (t0, W) in enumerate(tiles):
            tb = (ti % 2) * 2
            hpar = ti % 2
            xsem = f"D_xi{hpar}"
            xl = P.dma("gpsimd", xi[hpar][:, 0:2 * W].rearrange("p (c w) -> p c w", c=2),
                       xin.ap()[ob * OBW:(ob + 1) * OBW, t0:t0 + W].rearrange("(c p) w -> p c w", p=128),
                       xsem, waits=[("dve", xi_free.get(hpar, 0))])
            wdv = wd[wpar][:, :].rearrange("p (c w) -> p c w", c=64)
            for kq in range(4):
                hb = kq % 2
                hsem = f"D_hq{hb}"
                hl = P.dma("gpsimd", hq[hb][:, 0:DC * W].rearrange("p (c w) -> p c w", c=DC),
                           hmid.ap()[kq * 2048:(kq + 1) * 2048, t0:t0 + W].rearrange("(c p) w -> p c w", p=128),
                           hsem, waits=[("pe", hq_free.get(hb, 0))])
                hqv = hq[hb][:, 0:DC * W].rearrange("p (c w) -> p c w", c=DC)
                for oc in range(2):
                    wts = [(hsem, hl)]
                    if kq == 0:
                        wts += [("act", sh), bank_free.get(tb + oc, ("pe", 0))]
                    for c in range(DC):
                        gk = kq * DC + c
                        P.emit("tensor", lambda t_, gk=gk, oc=oc, c=c, W=W, wdv=wdv, hqv=hqv, tb=tb:
                               t_.matmul(P.bank(tb + oc).ap()[:, 0:W],
                                         wdv[:, gk, oc * 128:(oc + 1) * 128], hqv[:, c, :],
                                         start=(gk == 0), stop=(gk == 63)),
                               waits=(wts if c == 0 else ()),
                               inc=("pe" if c == DC - 1 else None))
                hq_free[hb] = P.counts["pe"]
            pe_done = P.counts["pe"]
            for oc in range(2):
                og = ob * 2 + oc
                d = P.emit("vector", lambda v, oc=oc, og=og, W=W, hpar=hpar, tb=tb: v.scalar_tensor_tensor(
                    xo[hpar][:, oc * W:oc * W + W], P.bank(tb + oc).ap()[:, 0:W], dsc[:, og:og + 1],
                    xi[hpar][:, oc * W:oc * W + W], OP.mult, OP.add),
                    waits=[("pe", pe_done), (xsem, xl), (f"D_xo{hpar}", xo_free.get(hpar, 0))],
                    inc="dve")
                bank_free[tb + oc] = ("dve", d)
            xi_free[hpar] = P.counts["dve"]
            st = P.dma("sync", xout.ap()[ob * OBW:(ob + 1) * OBW, t0:t0 + W].rearrange("(c p) w -> p c w", p=128),
                       xo[hpar][:, 0:2 * W].rearrange("p (c w) -> p c w", c=2),
                       f"D_xo{hpar}", waits=[P.now("dve")])
            xo_free[hpar] = st
        slot_free[wpar] = P.counts["pe"]
    P.barrier()


ARENA_A = 24576
ARENA_B = 24576
ARENA_H = 45184


def alloc_arenas(P):
    P.sbuf("arA", [128, ARENA_A], BF16)
    P.sbuf("arB", [128, ARENA_B], BF16)
    P.sbuf("arH", [128, ARENA_H], BF16)


def _hadamard(n):
    H = np.array([[1.0]], dtype=np.float64)
    while H.shape[0] < n:
        H = np.block([[H, H], [H, -H]])
    return H


def host_constants():
    c = {}
    c["had128"] = _hadamard(128).astype(BF)
    c["id128b"] = np.eye(128, dtype=np.float32).astype(BF)
    c["negid128b"] = (-np.eye(128, dtype=np.float32)).astype(BF)
    rs = np.zeros((64, 64 * 64), dtype=np.float32)
    for h in range(64):
        rs[h, h * 64:(h + 1) * 64] = 1.0
    c["rowsel"] = rs.astype(BF)
    c["onescol_b"] = np.ones((128, 1), dtype=np.float32).astype(BF)
    c["onesrow_f"] = np.ones((1, 128), dtype=np.float32)
    c["id128f"] = np.eye(128, dtype=np.float32)
    c["triu64"] = np.triu(np.ones((64, 64), dtype=np.float32))
    fc = np.zeros((128, 4), np.float32)
    fc[:, 0] = -128.0
    fc[:, 1] = 1e-5
    fc[:, 2] = 1.0
    fc[:, 3] = 0.5
    c["fconst"] = fc
    return c


def shard_x(x):
    shards = []
    for k in range(NCORES):
        b, hf = k // 2, k % 2
        main = x[b, hf * NT:(hf + 1) * NT]
        if hf == 0:
            halo = np.zeros((HALO, DM), x.dtype)
        else:
            halo = x[b, NT - HALO:NT]
        xt = np.concatenate([halo, main], axis=0)
        shards.append(np.ascontiguousarray(xt.T))
    return shards


def host_prep_common(inputs):
    d = {}
    d.update(host_constants())
    d["oddcol"] = [np.full((128, 1), float(k % 2), np.float32) for k in range(NCORES)]
    d["L0_ffn_gate_wT"] = np.ascontiguousarray(inputs["L0_ffn_gate_w"].T)
    d["L0_ffn_up_wT"] = np.ascontiguousarray(inputs["L0_ffn_up_w"].T)
    d["L0_ffn_down_wT"] = np.ascontiguousarray(inputs["L0_ffn_down_w"].T)
    d["L0_ffn_gate_scale"] = inputs["L0_ffn_gate_scale"]
    d["L0_ffn_up_scale"] = inputs["L0_ffn_up_scale"]
    d["L0_ffn_down_scale"] = inputs["L0_ffn_down_scale"]
    return d


def declare_ffn0_tensors(nc, kind_x="ExternalInput", kind_y="ExternalOutput"):
    t = {}
    for nm, shape, dt in CONST_SPECS:
        t[nm] = nc.dram_tensor(nm, shape, dt, kind="ExternalInput")
    t["L0_ffn_gate_wT"] = nc.dram_tensor("L0_ffn_gate_wT", [DM, INTER], I32, kind="ExternalInput")
    t["L0_ffn_up_wT"] = nc.dram_tensor("L0_ffn_up_wT", [DM, INTER], I32, kind="ExternalInput")
    t["L0_ffn_down_wT"] = nc.dram_tensor("L0_ffn_down_wT", [INTER, DM], I32, kind="ExternalInput")
    t["L0_ffn_gate_scale"] = nc.dram_tensor("L0_ffn_gate_scale", [INTER], F32, kind="ExternalInput")
    t["L0_ffn_up_scale"] = nc.dram_tensor("L0_ffn_up_scale", [INTER], F32, kind="ExternalInput")
    t["L0_ffn_down_scale"] = nc.dram_tensor("L0_ffn_down_scale", [DM], F32, kind="ExternalInput")
    t["xin"] = nc.dram_tensor("xin", [DM, NTH], F32, kind=kind_x)
    t["xout"] = nc.dram_tensor("xout", [DM, NTH], F32, kind=kind_y)
    t["hmid"] = nc.dram_tensor("hmid", [INTER, NTH], F32 if False else BF16)
    return t


def build_ffn0_program():
    from contextlib import ExitStack
    nc = bass.Bass("TRN2", target_bir_lowering=False)
    ctx = ExitStack()
    t = declare_ffn0_tensors(nc)
    P = Prog(nc, ctx)
    with ctx:
        alloc_arenas(P)
        emit_consts(P, t)
        P.barrier()
        build_colset(P, "l0g", t["L0_ffn_gate_scale"], INTER)
        build_colset(P, "l0u", t["L0_ffn_up_scale"], INTER)
        build_colset(P, "l0d", t["L0_ffn_down_scale"], DM, post_scale=0.5)
        P.barrier()
        emit_ffn0(P, t, t["xin"], t["xout"], "f0")
        P.finish()
    return nc


def ffn0_in_map(prep, xshard, core):
    m = {nm: np.asarray(prep[nm]) for nm, _, _ in CONST_SPECS if nm != "oddcol"}
    m["oddcol"] = prep["oddcol"][core]
    for nm in ["L0_ffn_gate_wT", "L0_ffn_up_wT", "L0_ffn_down_wT",
               "L0_ffn_gate_scale", "L0_ffn_up_scale", "L0_ffn_down_scale"]:
        m[nm] = prep[nm]
    m["xin"] = xshard
    return m


def _gather(res):
    out = np.empty((4, 4096, 2048), np.float32)
    for k in range(NCORES):
        b, hf = k // 2, k % 2
        out[b, hf * NT:(hf + 1) * NT] = res.results[k]["xout"][:, HALO:].T
    return out


def kernel_unfused(**inputs):
    x = inputs["L0_attn_x"]
    prep = host_prep_common(inputs)
    prepm = host_prep_mamba(inputs)
    prepf1 = host_prep_ffn1(inputs)
    prepa1 = host_prep_attn1(inputs)

    nc_ffn0 = build_ffn0_program()
    nc_mamba = build_mamba_program()
    nc_ffn1 = build_ffn1_program(fold=True)
    nc_attn1 = build_attn1_program()

    shards = shard_x(x)
    res = run_bass_kernel_spmd(nc_ffn0, [ffn0_in_map(prep, shards[k], k) for k in range(NCORES)],
                               core_ids=list(range(NCORES)))
    x = _gather(res)
    shards = shard_x(x)
    res = run_bass_kernel_spmd(nc_mamba, [mamba_in_map(prep, prepm, shards[k], k) for k in range(NCORES)],
                               core_ids=list(range(NCORES)))
    x = _gather(res)
    shards = shard_x(x)
    res = run_bass_kernel_spmd(nc_ffn0, [ffn0_in_map(prep, shards[k], k) for k in range(NCORES)],
                               core_ids=list(range(NCORES)))
    x = _gather(res)
    shards = shard_x(x)
    res = run_bass_kernel_spmd(nc_ffn1, [generic_in_map(prep, prepf1, shards[k], k) for k in range(NCORES)],
                               core_ids=list(range(NCORES)))
    x = _gather(res)
    shards = shard_x(x)
    res = run_bass_kernel_spmd(nc_attn1, [generic_in_map(prep, prepa1, shards[k], k) for k in range(NCORES)],
                               core_ids=list(range(NCORES)))
    x = _gather(res)
    shards = shard_x(x)
    res = run_bass_kernel_spmd(nc_ffn1, [generic_in_map(prep, prepf1, shards[k], k) for k in range(NCORES)],
                               core_ids=list(range(NCORES)))
    return _gather(res)


def emit_mamba(P, t, xin, xout, tag="m", upto="m7"):
    A = Carve(P.sbufs["arA"], ARENA_A)
    B_ = Carve(P.sbufs["arB"], ARENA_B)
    H = Carve(P.sbufs["arH"], ARENA_H)
    hst = [H.take(DC * TT), H.take(DC * TT)]
    hst_free = {}
    hln2 = t["hln2"]

    def hw0(c, t0, W):
        par = (t0 // TT) % 2
        return hst[par][:, c * W: c * W + W]

    def flush0(ti, t0, W):
        par = ti % 2
        st = P.dma("sync", hln2.ap()[:, t0:t0 + W].rearrange("(c p) w -> p c w", p=128),
                   hst[par][:, 0:DC * W].rearrange("p (c w) -> p c w", c=DC),
                   f"D_hst{par}", waits=[P.now("dve")])
        hst_free[par] = st

    emit_ln(P, xin, hw0, tiles=TILES_ALL, tag=f"{tag}0", ar1=A, ar2=B_, tile_done=flush0,
            extra_waits=lambda ti: [(f"D_hst{ti % 2}", hst_free.get(ti % 2, 0))])
    P.barrier()
    if upto == "m0":
        return

    A = Carve(P.sbufs["arA"], ARENA_A)
    B_ = Carve(P.sbufs["arB"], ARENA_B)
    H = Carve(P.sbufs["arH"], ARENA_H)
    w_sl = [A.take(DC * TT), A.take(DC * TT)]
    hti = [B_.take(DC * TT), B_.take(DC * TT)]
    rb = [H.take(4 * NTH), H.take(4 * NTH)]
    vt = H.take(NT, F32)
    ut = [H.take(NT), H.take(NT)]
    zst = [B_.take(TT), B_.take(TT)]
    wT = t["L0_attn_in_proj_wT"]
    cw = [P.sbufs[f"col_cw{k}"] for k in range(4)]
    cb = P.sbufs["col_cb"]
    hln_free = {}
    wslot_free = {}
    rb_free = {}
    zst_free = {}
    ut_free = {}
    bank_free = {}
    for ob in range(16):
        wpar = ob % 2
        dsem = f"D_w{wpar}"
        ld = P.dma("gpsimd", w_sl[wpar][:, :].rearrange("p (c w) -> p c w", c=DC),
                   wT.ap()[:, ob * TT:(ob + 1) * TT].rearrange("(c p) w -> p c w", p=128),
                   dsem, waits=[("pe", wslot_free.get(wpar, 0))])
        is_x = ob < 8
        tiles = TILES_ALL if is_x else TILES_MAIN
        for ti, (t0, W) in enumerate(tiles):
            hpar = ti % 2
            hsem = f"D_h{hpar}"
            hl = P.dma("sync", hti[hpar][:, 0:DC * W].rearrange("p (c w) -> p c w", c=DC),
                       hln2.ap()[:, t0:t0 + W].rearrange("(c p) w -> p c w", p=128),
                       hsem, waits=[("pe", hln_free.get(hpar, 0))])
            wv = w_sl[wpar][:, :].rearrange("p (c w) -> p c w", c=DC)
            hv = hti[hpar][:, 0:DC * W].rearrange("p (c w) -> p c w", c=DC)
            for oc in range(4):
                bk = (ti * 4 + oc) % 2 if is_x else 2 + (ti * 4 + oc) % 2
                wts = [(dsem, ld), (hsem, hl), bank_free.get(bk, ("dve", 0))]
                pe_done = mm_group(P, P.bank(bk).ap()[:, 0:W],
                                   [(wv[:, c, oc * 128:(oc + 1) * 128], hv[:, c, :]) for c in range(DC)],
                                   waits=wts)
                if is_x:
                    d = P.emit("vector", lambda v, bk=bk, oc=oc, t0=t0, W=W, wpar=wpar: v.tensor_copy(
                        rb[wpar][:, oc * NTH + t0: oc * NTH + t0 + W], P.bank(bk).ap()[:, 0:W]),
                        waits=[("pe", pe_done)], inc="dve")
                else:
                    zp = (ti * 4 + oc) % 2
                    d = P.emit("vector", lambda v, bk=bk, zp=zp, W=W: v.tensor_copy(
                        zst[zp][:, 0:W], P.bank(bk).ap()[:, 0:W]),
                        waits=[("pe", pe_done), (f"D_z{zp}", zst_free.get(zp, 0))], inc="dve")
                    gz = (ob - 8) * 4 + oc
                    st = P.dma("sync", t["z_s"].ap()[gz * 128:(gz + 1) * 128, t0 - HALO:t0 - HALO + W],
                               zst[zp][:, 0:W], f"D_z{zp}", waits=[("dve", d)])
                    zst_free[zp] = st
                bank_free[bk] = ("dve", d)
            hln_free[hpar] = P.counts["pe"]
        if is_x:
            dve_rb = P.counts["dve"]
            for oc in range(4):
                gc = ob * 4 + oc
                up = gc % 2
                r = rb[wpar]
                P.emit("vector", lambda v, r=r, oc=oc, gc=gc: v.tensor_scalar(
                    vt[:, :], r[:, oc * NTH + 1: oc * NTH + 1 + NT], cw[0][:, gc:gc + 1], None, OP.mult),
                    waits=([("dve", dve_rb)] if oc == 0 else []) + [P.now("act")], inc="dve")
                for k in (1, 2):
                    P.emit("vector", lambda v, r=r, oc=oc, gc=gc, k=k: v.scalar_tensor_tensor(
                        vt[:, :], r[:, oc * NTH + 1 + k: oc * NTH + 1 + k + NT], cw[k][:, gc:gc + 1],
                        vt[:, :], OP.mult, OP.add), inc="dve")
                d3 = P.emit("vector", lambda v, r=r, oc=oc, gc=gc: v.scalar_tensor_tensor(
                    vt[:, :], r[:, oc * NTH + 4: oc * NTH + 4 + NT], cw[3][:, gc:gc + 1],
                    vt[:, :], OP.mult, OP.add), inc="dve")
                a = P.emit("scalar", lambda s, gc=gc, up=up: s.activation(
                    ut[up][:, :], vt[:, :], AF.Silu, bias=cb[:, gc:gc + 1]),
                    waits=[("dve", d3), (f"D_u{up}", ut_free.get(up, 0))], inc="act")
                st = P.dma("sync", t["u_s"].ap()[gc * 128:(gc + 1) * 128, :],
                           ut[up][:, :], f"D_u{up}", waits=[("act", a)])
                ut_free[up] = st
        wslot_free[wpar] = P.counts["pe"]
    P.barrier()
    if upto == "m1":
        return

    A = Carve(P.sbufs["arA"], ARENA_A)
    B_ = Carve(P.sbufs["arB"], ARENA_B)
    H = Carve(P.sbufs["arH"], ARENA_H)
    xpT = t["L0_attn_x_proj_wT"]
    wsl = A.take(32 * TT)
    w_sl = [wsl, wsl]
    uc = [B_.take(32 * 128), B_.take(32 * 128)]
    xst = [B_.take(TT), B_.take(TT)]
    wslot_free = {}
    uc_free = {}
    xst_free = {}
    bank_free = {}
    for on in range(8):
        wpar = 0
        dsem = f"D_xw{wpar}"
        ld = P.dma("gpsimd", w_sl[wpar][:, :].rearrange("p (c w) -> p c w", c=32),
                   xpT.ap()[:, on * TT:(on + 1) * TT].rearrange("(c p) w -> p c w", p=128),
                   dsem, waits=[("pe", P.counts["pe"])])
        for tg in range(16):
            up = tg % 2
            usem = f"D_uc{up}"
            ul = P.dma("gpsimd", uc[up][:, :].rearrange("p (c w) -> p c w", c=32),
                       t["u_s"].ap()[:, tg * 128:(tg + 1) * 128].rearrange("(c p) w -> p c w", p=128),
                       usem, waits=[("pe", uc_free.get(up, 0))])
            bk = tg % 2
            wv = w_sl[wpar][:, :].rearrange("p (c w) -> p c w", c=32)
            uv = uc[up][:, :].rearrange("p (c w) -> p c w", c=32)
            pe_done = mm_group(P, P.bank(bk).ap()[:, :],
                               [(uv[:, c, :], wv[:, c, :]) for c in range(32)],
                               waits=[(dsem, ld), (usem, ul), bank_free.get(bk, ("dve", 0))])
            uc_free[up] = pe_done
            sp = tg % 2
            d = P.emit("vector", lambda v, bk=bk, sp=sp: v.tensor_copy(
                xst[sp][:, :], P.bank(bk).ap()[:, :]),
                waits=[("pe", pe_done), (f"D_xs{sp}", xst_free.get(sp, 0))], inc="dve")
            bank_free[bk] = ("dve", d)
            st = P.dma("sync", t["xtm_s"].ap()[tg * 128:(tg + 1) * 128, on * TT:(on + 1) * TT],
                       xst[sp][:, :], f"D_xs{sp}", waits=[("dve", d)])
            xst_free[sp] = st
        wslot_free[wpar] = P.counts["pe"]
    P.barrier()
    Hp = Carve(P.sbufs["arH"], ARENA_H)
    sl = {
        "bfm": Hp.take(NT),
        "cfm": Hp.take(NT),
        "dt2r": Hp.take(NT, F32),
        "acs": Hp.take(NT, F32),
        "hi": Hp.take(NT),
        "lo": Hp.take(NT),
        "tm_acs": Hp.take(NT, F32),
        "tm_dt": Hp.take(NT, F32),
        "tm_decay": Hp.take(NT, F32),
        "btm": Hp.take(2 * NT),
        "S": Hp.take(2 * NT, F32),
        "dbc": Hp.take(2 * NT),
    }
    bfm, cfm = sl["bfm"], sl["cfm"]
    B2 = Carve(P.sbufs["arB"], ARENA_B)
    A2 = Carve(P.sbufs["arA"], ARENA_A)
    w2 = B2.take(32 * 320)
    ut2 = A2.take(32 * TT)
    dsem2 = f"D_xw2"
    ld2 = P.dma("gpsimd", w2[:, :].rearrange("p (c w) -> p c w", c=32),
                xpT.ap()[:, D_INNER:D_INNER + 320].rearrange("(c p) w -> p c w", p=128),
                dsem2)
    ut2_free = {}
    for ti, (t0, W) in enumerate(TILES_MAIN):
        usem = f"D_u2"
        ul = P.dma("gpsimd", ut2[:, 0:32 * W].rearrange("p (c w) -> p c w", c=32),
                   t["u_s"].ap()[:, t0 - HALO:t0 - HALO + W].rearrange("(c p) w -> p c w", p=128),
                   usem, waits=[("pe", ut2_free.get(0, 0))])
        wv = w2[:, :].rearrange("p (c w) -> p c w", c=32)
        uv = ut2[:, 0:32 * W].rearrange("p (c w) -> p c w", c=32)
        m0 = t0 - HALO
        for (bk, o0, o1, dstf) in ((4, 0, 64, None), (5, 64, 192, bfm), (6, 192, 320, cfm)):
            pe_done = mm_group(P, P.bank(bk).ap()[0:(o1 - o0), 0:W],
                               [(wv[:, c, o0:o1], uv[:, c, :]) for c in range(32)],
                               waits=[(dsem2, ld2), (usem, ul), bank_free.get(bk, ("dve", 0))])
            if dstf is None:
                d = P.emit("vector", lambda v, m0=m0, W=W: v.tensor_copy(
                    sl["dt2r"][0:64, m0:m0 + W], P.bank(4).ap()[0:64, 0:W]),
                    waits=[("pe", pe_done)], inc="dve")
            else:
                d = P.emit("vector", lambda v, dstf=dstf, bk=bk, m0=m0, W=W: v.tensor_copy(
                    dstf[:, m0:m0 + W], P.bank(bk).ap()[0:128, 0:W]),
                    waits=[("pe", pe_done)], inc="dve")
            bank_free[bk] = ("dve", d)
        ut2_free[0] = P.counts["pe"]
    P.barrier()
    if upto == "m2":
        return
    emit_mamba_ssd(P, t, xin, xout, tag, sl, upto)


def emit_mamba_ssd(P, t, xin, xout, tag, sl, upto="m7"):
    nc = P.nc
    fc = P.sbufs["c_fconst"]
    id128f = P.sbufs["c_id128f"]
    id128b = P.sbufs["c_id128b"]
    onesrow_f = P.sbufs["c_onesrow_f"]
    rowsel = P.sbufs["c_rowsel"]
    triu = P.sbufs["c_triu64"]
    oddcol = P.sbufs["c_oddcol"]
    B3 = Carve(P.sbufs["arB"], ARENA_B)
    sp_v = B3.take(TT, F32)[0:64, :]
    sp_b = B3.take(TT, F32)[0:64, :]
    tmpb = B3.take(2 * NT)
    rbc4 = B3.take(4096, F32)
    dt_bf = rbc4.bitcast(BF16)[0:64, 0:NT]
    lrow = B3.take(NCHUNK * 64, F32)[0:1, :]
    zt64 = P.sbuf("m_zt64", [64, 64], F32)
    spd = P.sbuf("m_spd", [64, 64], F32)
    spe = P.sbuf("m_spe", [64, 64], F32)
    lbc = P.sbuf("m_lbc", [128, 64], F32)
    dtb_col = P.sbuf("m_dtb", [64, 1], F32)
    negA = P.sbuf("m_negA", [64, 1], F32)
    dtpT = P.sbuf("m_dtpT", [64, 64], BF16)
    drow = P.sbuf("m_drow", [1, 64], F32)
    gtm = P.sbuf("m_gtm", [64, 64], BF16)
    e1s = P.sbuf("m_e1", [64, 64], F32)
    e2s = P.sbuf("m_e2", [64, 64], F32)
    lrow_s = P.sbuf("m_lrs", [1, 64], F32)

    acs, dt2r = sl["acs"], sl["dt2r"]
    hi, lo = sl["hi"], sl["lo"]
    tm_acs, tm_dt = sl["tm_acs"], sl["tm_dt"]
    tm_decay = sl["tm_decay"]
    tm_eacs = P.sbuf("m_tme", [64, 64], F32)
    btm, S, dbc = sl["btm"], sl["S"], sl["dbc"]
    bfm, cfm = sl["bfm"], sl["cfm"]

    l1 = P.dma("gpsimd", dtpT[:, :], t["L0_attn_dt_proj_wT"].ap(), "D_m3")
    l2 = P.dma("gpsimd", dtb_col[:, :], t["L0_attn_dt_proj_b"].ap().rearrange("(a b) -> a b", b=1), "D_m3")
    l3 = P.dma("gpsimd", negA[:, :], t["L0_attn_A_log"].ap().rearrange("(a b) -> a b", b=1), "D_m3")
    l4 = P.dma("gpsimd", drow[:, :], t["L0_attn_D"].ap().rearrange("(a b) -> b a", b=1), "D_m3")
    a1 = P.emit("scalar", lambda s: s.activation(negA[:, :], negA[:, :], AF.Exp),
                waits=[("D_m3", l4)], inc="act")
    n1 = P.emit("vector", lambda v: v.tensor_scalar(negA[:, :], negA[:, :], -1.0, None, OP.mult),
                waits=[("act", a1)], inc="dve")
    P.emit("vector", lambda v: v.memset(zt64[:, :], 0.0), inc="dve")
    pe1 = P.emit("tensor", lambda tt: tt.matmul(
        P.bank(7).ap()[:, 0:64], onesrow_f[0:1, :], drow[0:1, :],
        start=True, stop=True), waits=[("D_m3", l4)], inc="pe")
    dsm = P.sbuf("m_dsm", [128, 64], F32)
    dd1 = P.emit("vector", lambda v: v.tensor_copy(dsm[:, :], P.bank(7).ap()[:, 0:64]),
                 waits=[("pe", pe1)], inc="dve")
    P.emit("vector", lambda v: v.tensor_copy(
        dbc[:, :].rearrange("p (h q) -> p h q", h=64),
        dsm[:, :].unsqueeze(2).broadcast_to([128, 64, 64])), inc="dve")
    c1 = P.emit("vector", lambda v: v.tensor_copy(dt_bf[:, :], dt2r[0:64, 0:NT]), inc="dve")
    for ti in range(4):
        t0 = ti * TT
        pe_done = P.emit("tensor", lambda tt, t0=t0: tt.matmul(
            P.bank(0).ap()[0:64, :], dtpT[:, :], dt_bf[:, t0:t0 + TT], start=True, stop=True),
            waits=[("D_m3", l2), ("dve", c1), P.now("dve")], inc="pe")
        v1 = P.emit("vector", lambda v, t0=t0: v.tensor_scalar(
            sp_v[:, :], P.bank(0).ap()[0:64, :], dtb_col[:, 0:1], None, OP.add),
            waits=[("pe", pe_done), P.now("act")], inc="dve")
        a2 = P.emit("scalar", lambda s: s.activation(sp_b[:, :], sp_v[:, :], AF.Abs),
                    waits=[("dve", v1)], inc="act")
        a3 = P.emit("scalar", lambda s: s.activation(sp_b[:, :], sp_b[:, :], AF.Exp, scale=-1.0),
                    inc="act")
        a4 = P.emit("scalar", lambda s: s.activation(sp_b[:, :], sp_b[:, :], AF.Ln, bias=fc[0:64, 2:3]),
                    inc="act")
        v2 = P.emit("vector", lambda v: v.tensor_scalar(sp_v[:, :], sp_v[:, :], 0.0, None, OP.max),
                    waits=[("act", a4)], inc="dve")
        v3 = P.emit("vector", lambda v, t0=t0: v.tensor_add(dt2r[0:64, t0:t0 + TT], sp_v[:, :], sp_b[:, :]),
                    inc="dve")
        v4 = P.emit("vector", lambda v, t0=t0: v.tensor_scalar(
            sp_v[:, :], dt2r[0:64, t0:t0 + TT], negA[:, 0:1], None, OP.mult),
            waits=[("dve", n1)], inc="dve")
        for cc in range(TT // CHUNK):
            P.emit("vector", lambda v, t0=t0, cc=cc: v.tensor_tensor_scan(
                acs[0:64, t0 + cc * CHUNK: t0 + (cc + 1) * CHUNK],
                sp_v[:, cc * CHUNK:(cc + 1) * CHUNK], zt64[:, :], 0.0, OP.add, OP.add),
                inc="dve")
    for c in range(NCHUNK):
        cs = slice(c * CHUNK, (c + 1) * CHUNK)
        d0 = P.emit("vector", lambda v, cs=cs, c=c: v.tensor_scalar(
            spd[:, :], acs[0:64, cs], acs[0:64, c * CHUNK + 63:c * CHUNK + 64], None, OP.subtract),
            waits=[P.now("pe")], inc="dve")
        a5 = P.emit("scalar", lambda s: s.activation(spd[:, :], spd[:, :], AF.Exp, scale=-1.0),
                    waits=[("dve", d0)], inc="act")
        a6 = P.emit("scalar", lambda s, cs=cs: s.activation(spe[:, :], acs[0:64, cs], AF.Exp),
                    waits=[P.now("pe")], inc="act")
        pw = [("act", a6), ("dve", P.counts["dve"])]
        tps = [(acs[0:64, cs], tm_acs, 2), (dt2r[0:64, cs], tm_dt, 3),
               (spd[:, :], tm_decay, 4)]
        for (src_ap, dst, bk) in tps:
            pe2 = P.emit("tensor", lambda tt, src_ap=src_ap, bk=bk: tt.transpose(
                P.bank(bk).ap()[0:64, 0:64], src_ap, id128f[0:64, 0:64]),
                waits=(pw if bk == 2 else ()), inc="pe")
            P.emit("vector", lambda v, dst=dst, c=c, bk=bk: v.tensor_copy(
                dst[0:64, c * 64:(c + 1) * 64], P.bank(bk).ap()[0:64, 0:64]),
                waits=[("pe", pe2)], inc="dve")
        pe3 = P.emit("tensor", lambda tt: tt.transpose(
            P.bank(6).ap()[0:1, 0:64], spe[:, 63:64], id128f[0:64, 0:64]),
            inc="pe")
        P.emit("vector", lambda v, c=c: v.tensor_copy(
            lrow[0:1, c * 64:(c + 1) * 64], P.bank(6).ap()[0:1, 0:64]),
            waits=[("pe", pe3)], inc="dve")
        pe4 = P.emit("tensor", lambda tt, cs=cs: tt.transpose(
            P.bank_bf(7)[0:64, 0:128], bfm[:, cs], id128b[:, :]),
            waits=[P.now("dve")], inc="pe")
        P.emit("vector", lambda v, c=c: v.tensor_copy(
            btm[0:64, c * 128:(c + 1) * 128], P.bank_bf(7)[0:64, 0:128]),
            waits=[("pe", pe4)], inc="dve")
    P.barrier()
    if upto == "m3":
        return

    A = Carve(P.sbufs["arA"], ARENA_A)
    xt = A.take(2 * NT)
    xs = A.take(2 * NT)
    ysb = [A.take(2 * NT), A.take(2 * NT)]
    sbf_all = A.take(2 * NT)
    rbc = rbc4
    em_all = B3.take(4096)
    tmp512 = B3.take(512)
    xda = tmpb
    rowd = t["rowd"]

    zS = P.emit("vector", lambda v: v.memset(S[:, :], 0.0), inc="dve")
    P.cm = {}

    def chunk_loop2(main, c):
        cs = slice(c * CHUNK, (c + 1) * CHUNK)
        xsem = f"D_xt{int(main)}"
        xl = P.dma("gpsimd", xt[0:64, :], t["xtm_s"].ap()[c * 64:(c + 1) * 64, :],
                   xsem, waits=[("dve", P.cm.get(("xtc", main), 0)),
                                ("pe", P.cm.get(("xtp", main), 0))])
        pe_l = P.emit("tensor", lambda tt, c=c: tt.matmul(
            P.bank(1).ap()[:, 0:64], onesrow_f[0:1, :], lrow[0:1, c * 64:(c + 1) * 64],
            start=True, stop=True), waits=[P.now("dve")], inc="pe")
        dl = P.emit("vector", lambda v: v.tensor_copy(lbc[:, :], P.bank(1).ap()[:, 0:64]),
                    waits=[("pe", pe_l)], inc="dve")
        dxs = P.emit("vector", lambda v, c=c: v.tensor_mul(
            xs[0:64, :].rearrange("p (h q) -> p h q", h=64),
            xt[0:64, :].rearrange("p (h q) -> p h q", h=64),
            tm_dt[0:64, c * 64:(c + 1) * 64].unsqueeze(2).broadcast_to([64, 64, 64])),
            waits=[(xsem, xl), ("pe", P.cm.get(("xsp", main), 0))], inc="dve")
        dxd = P.emit("vector", lambda v, c=c: v.tensor_mul(
            xda[0:64, :].rearrange("p (h q) -> p h q", h=64),
            xs[0:64, :].rearrange("p (h q) -> p h q", h=64),
            tm_decay[0:64, c * 64:(c + 1) * 64].unsqueeze(2).broadcast_to([64, 64, 64])),
            waits=[("pe", P.cm.get(("xdp",), 0))], inc="dve")
        if main:
            pe_g = P.emit("tensor", lambda tt, cs=cs: tt.matmul(
                P.bank(0).ap()[0:64, 0:64], bfm[:, cs], cfm[:, cs], start=True, stop=True),
                waits=[("dve", P.cm.get(("gtb",), 0))], inc="pe")
            dg = P.emit("vector", lambda v: v.tensor_mul(gtm[:, :], P.bank(0).ap()[0:64, 0:64],
                                                         triu[:, :]),
                        waits=[("pe", pe_g)], inc="dve")
            P.cm[("gtb",)] = dg
            ate = P.emit("scalar", lambda s, c=c: s.activation(
                tm_eacs[:, :], tm_acs[0:64, c * 64:(c + 1) * 64], AF.Exp),
                waits=[P.now("dve")], inc="act")
            rd = rowd.ap()[c % 2, :]
            g1 = P.dma("gpsimd", rd, acs[0:64, cs], "D_rb",
                       waits=[("act", P.cm.get(("rbc",), 0))])
            g2 = P.dma("gpsimd", rbc[0:64, :], bass.AP(t["rowd"], (c % 2) * 4096, [[0, 64], [1, 4096]]),
                       "D_rb", waits=[("D_rb", g1)])
            e1a = P.emit("vector", lambda v, c=c: v.tensor_sub(
                rbc[0:64, :].rearrange("p (h q) -> p h q", h=64),
                rbc[0:64, :].rearrange("p (h q) -> p h q", h=64),
                tm_acs[0:64, c * 64:(c + 1) * 64].unsqueeze(2).broadcast_to([64, 64, 64])),
                waits=[("D_rb", g2)], inc="dve")
            e1b = P.emit("vector", lambda v: v.tensor_scalar(
                rbc[0:64, :], rbc[0:64, :], 0.0, None, OP.min), inc="dve")
            eex = P.emit("scalar", lambda s: s.activation(em_all[0:64, :], rbc[0:64, :], AF.Exp),
                         waits=[("dve", e1b)], inc="act")
            P.cm[("rbc",)] = eex
            emg = P.emit("vector", lambda v: v.tensor_mul(
                em_all[0:64, :].rearrange("p (h q) -> p h q", h=64),
                em_all[0:64, :].rearrange("p (h q) -> p h q", h=64),
                gtm[:, :].unsqueeze(1).broadcast_to([64, 64, 64])),
                waits=[("act", eex), ("dve", dg)], inc="dve")
            dsb = P.emit("vector", lambda v: v.tensor_copy(sbf_all[:, :], S[:, :]),
                         waits=[("pe", P.cm.get(("sbp",), 0))], inc="dve")
            P.cm[("sbp",)] = 0
        for h8 in range(8):
            sb_ = 6 + h8 % 2
            if main:
                yb = 2 + h8 % 2
                ob = 4 + h8 % 2
                pe_o = P.emit("tensor", lambda tt, cs=cs, h8=h8, ob=ob: tt.matmul(
                    P.bank(ob).ap()[0:64, :], cfm[:, cs], sbf_all[:, h8 * 512:(h8 + 1) * 512],
                    start=True, stop=True),
                    waits=[("dve", dsb), ("dve", P.cm.get(("obk", ob), 0))], inc="pe")
                for hh in range(8):
                    h = h8 * 8 + hh
                    w = [("dve", emg)]
                    if hh == 0:
                        w.append(("dve", P.cm.get(("ybk", yb), 0)))
                    P.emit("tensor", lambda tt, h=h, yb=yb, hh=hh: tt.matmul(
                        P.bank(yb).ap()[0:64, hh * 64:(hh + 1) * 64],
                        em_all[0:64, h * 64:(h + 1) * 64], xs[0:64, h * 64:(h + 1) * 64],
                        start=True, stop=True),
                        waits=w, inc=("pe" if hh == 7 else None))
                pyd = P.counts["pe"]
            pe_s = P.emit("tensor", lambda tt, c=c, h8=h8, sb_=sb_: tt.matmul(
                P.bank(sb_).ap()[:, :], btm[0:64, c * 128:(c + 1) * 128],
                xda[0:64, h8 * 512:(h8 + 1) * 512],
                start=True, stop=True),
                waits=[("dve", dxd), ("dve", P.cm.get(("stb", sb_), 0))], inc="pe")
            P.emit("vector", lambda v, h8=h8: v.tensor_mul(
                S[:, h8 * 512:(h8 + 1) * 512].rearrange("p (h q) -> p h q", h=8),
                S[:, h8 * 512:(h8 + 1) * 512].rearrange("p (h q) -> p h q", h=8),
                lbc[:, h8 * 8:(h8 + 1) * 8].unsqueeze(2).broadcast_to([128, 8, 64])),
                waits=[("pe", pe_s), ("dve", dl)], inc="dve")
            d2 = P.emit("vector", lambda v, h8=h8, sb_=sb_: v.tensor_add(
                S[:, h8 * 512:(h8 + 1) * 512], S[:, h8 * 512:(h8 + 1) * 512],
                P.bank(sb_).ap()[:, :]), inc="dve")
            P.cm[("stb", sb_)] = d2
            if main:
                w = [("pe", pyd)]
                if h8 < 2:
                    w.append((f"D_y{c % 2}", P.cm.get(("ysb", c % 2), 0)))
                dv1 = P.emit("vector", lambda v, h8=h8, ob=ob, c=c: v.tensor_mul(
                    tmp512[0:64, :].rearrange("p (h q) -> p h q", h=8),
                    P.bank(ob).ap()[0:64, :].rearrange("p (h q) -> p h q", h=8),
                    tm_eacs[:, h8 * 8:(h8 + 1) * 8].unsqueeze(2).broadcast_to([64, 8, 64])),
                    waits=w, inc="dve")
                dv2 = P.emit("vector", lambda v, h8=h8, yb=yb, c=c: v.tensor_add(
                    ysb[c % 2][0:64, h8 * 512:(h8 + 1) * 512], tmp512[0:64, :],
                    P.bank(yb).ap()[0:64, :]), inc="dve")
                P.cm[("ybk", yb)] = dv2
                P.cm[("obk", ob)] = dv2
        P.cm[("sbp",)] = P.counts["pe"]
        P.cm[("xdp",)] = P.counts["pe"]
        if not main:
            P.cm[("xsp", main)] = P.counts["pe"]
        else:
            P.cm[("xsp", main)] = P.counts["pe"]
        if main:
            dt1 = P.emit("vector", lambda v: v.tensor_mul(
                xda[0:64, :], xt[0:64, :], dbc[0:64, :]),
                waits=[("pe", P.cm.get(("xdp",), 0))], inc="dve")
            dt2_ = P.emit("vector", lambda v, c=c: v.tensor_add(
                ysb[c % 2][0:64, :], ysb[c % 2][0:64, :], xda[0:64, :]), inc="dve")
            st = P.dma("sync", t["y_s"].ap()[c * 64:(c + 1) * 64, :], ysb[c % 2][0:64, :],
                       f"D_y{c % 2}", waits=[("dve", dt2_)])
            P.cm[("ysb", c % 2)] = st
            P.cm[("xtc", main)] = dt2_
        else:
            P.cm[("xtc", main)] = dxd
        P.cm[("xtp", main)] = P.counts["pe"]

    for c in range(NCHUNK):
        chunk_loop2(False, c)
    P.barrier()
    if upto == "m4":
        return
    e1 = P.dma("gpsimd", t["ccin"].ap(), S[:, :], "D_cc")
    cc = P.emit("gpsimd", lambda g: g.collective_compute(
        "AllGather", OP.bypass, replica_groups=[[0, 1], [2, 3], [4, 5], [6, 7]],
        ins=[t["ccin"].ap().opt()], outs=[t["ccout"].ap().opt()]),
        waits=[("D_cc", e1)], inc="cc", is_dma=True)
    e2 = P.dma("gpsimd", S[:, :], t["ccout"].ap()[0:128, :], "D_cc", waits=[("cc", cc)])
    P.emit("vector", lambda v: v.tensor_scalar(S[:, :], S[:, :], oddcol[:, 0:1], None, OP.mult),
           waits=[("D_cc", e2)], inc="dve")
    P.barrier()
    if upto == "m5":
        return
    import os
    nmain = int(os.environ.get("MAMBA_MAIN_CHUNKS", NCHUNK))
    for c in range(nmain):
        chunk_loop2(True, c)
    P.barrier()
    if upto == "m6":
        return
    emit_mamba_out(P, t, xin, xout, tag)


def emit_mamba_out(P, t, xin, xout, tag):
    A = Carve(P.sbufs["arA"], ARENA_A)
    B_ = Carve(P.sbufs["arB"], ARENA_B)
    H = Carve(P.sbufs["arH"], ARENA_H)
    id128b = P.sbufs["c_id128b"]
    ygf = A.take(32 * TT)
    yt = [A.take(2 * NT), A.take(2 * NT)]
    zt = [B_.take(32 * 128), B_.take(32 * 128)]
    zs = B_.take(32 * 128)
    w_sl = [H.take(32 * 256), H.take(32 * 256)]
    xi = [H.take(2 * TT, F32), H.take(2 * TT, F32)]
    xo = [H.take(2 * TT, F32), H.take(2 * TT, F32)]
    woT = t["L0_attn_out_proj_wT"]
    cm = {}
    for tg in range(4):
        for tc in range(4):
            gtok = tg * 4 + tc
            ypar = tc % 2
            ysem = f"D_yt{ypar}"
            yl = P.dma("gpsimd", yt[ypar][0:128, :], t["y_s"].ap()[gtok * 128:(gtok + 1) * 128, :],
                       ysem, waits=[("pe", cm.get(("yt", ypar), 0))])
            zsem = f"D_zt{ypar}"
            zl = P.dma("sync", zt[ypar][:, :].rearrange("p (c w) -> p c w", c=32),
                       t["z_s"].ap()[:, gtok * 128:(gtok + 1) * 128].rearrange("(c p) w -> p c w", p=128),
                       zsem, waits=[("act", cm.get(("zt", ypar), 0))])
            az = P.emit("scalar", lambda s, ypar=ypar: s.activation(
                zs[:, :], zt[ypar][:, :], AF.Silu),
                waits=[(zsem, zl), ("dve", cm.get(("zs",), 0))], inc="act")
            cm[("zt", ypar)] = az
            for fck in range(32):
                pe1 = P.emit("tensor", lambda tt, ypar=ypar, fck=fck: tt.transpose(
                    P.bank_bf(fck % 2)[0:128, 0:128], yt[ypar][0:128, fck * 128:(fck + 1) * 128],
                    id128b[:, :]),
                    waits=([(ysem, yl), ("dve", cm.get(("tb", fck % 2), 0))] if True else ()),
                    inc="pe")
                d1 = P.emit("vector", lambda v, fck=fck, tc=tc, ypar=ypar: v.scalar_tensor_tensor(
                    ygf[:, fck * TT + tc * 128: fck * TT + (tc + 1) * 128],
                    P.bank_bf(fck % 2)[0:128, 0:128], 1.0, zs[:, fck * 128:(fck + 1) * 128],
                    OP.mult, OP.mult),
                    waits=[("pe", pe1), ("act", az)], inc="dve")
                cm[("tb", fck % 2)] = d1
            cm[("yt", ypar)] = P.counts["pe"]
            cm[("zs",)] = P.counts["dve"]
        dve_ygf = P.counts["dve"]
        for ob in range(8):
            wpar = ob % 2
            wsem = f"D_ow{wpar}"
            wl = P.dma("gpsimd", w_sl[wpar][:, :].rearrange("p (c w) -> p c w", c=32),
                       woT.ap()[:, ob * 256:(ob + 1) * 256].rearrange("(c p) w -> p c w", p=128),
                       wsem, waits=[("pe", cm.get(("ow", wpar), 0))])
            xsem = f"D_oxi{wpar}"
            xl = P.dma("gpsimd", xi[wpar][:, :].rearrange("p (c w) -> p c w", c=2),
                       xin.ap()[ob * 256:(ob + 1) * 256, HALO + tg * TT:HALO + (tg + 1) * TT]
                       .rearrange("(c p) w -> p c w", p=128),
                       xsem, waits=[("dve", cm.get(("oxi", wpar), 0))])
            wv = w_sl[wpar][:, :].rearrange("p (c w) -> p c w", c=32)
            yv = ygf[:, :].rearrange("p (c w) -> p c w", c=32)
            for oc in range(2):
                bk = 4 + (ob % 2) * 2 + oc
                pe_done = mm_group(P, P.bank(bk).ap()[:, :],
                                   [(wv[:, ch, oc * 128:(oc + 1) * 128], yv[:, ch, :]) for ch in range(32)],
                                   waits=[(wsem, wl), ("dve", dve_ygf), cm.get(("bnk", bk), ("dve", 0))])
                d2 = P.emit("vector", lambda v, oc=oc, wpar=wpar, bk=bk: v.tensor_add(
                    xo[wpar][:, oc * TT:(oc + 1) * TT], P.bank(bk).ap()[:, :],
                    xi[wpar][:, oc * TT:(oc + 1) * TT]),
                    waits=[("pe", pe_done), (xsem, xl),
                           (f"D_oxo{wpar}", cm.get(("oxo", wpar), 0))], inc="dve")
                cm[("bnk", bk)] = ("dve", d2)
            cm[("oxi", wpar)] = P.counts["dve"]
            st = P.dma("sync", xout.ap()[ob * 256:(ob + 1) * 256, HALO + tg * TT:HALO + (tg + 1) * TT]
                       .rearrange("(c p) w -> p c w", p=128),
                       xo[wpar][:, :].rearrange("p (c w) -> p c w", c=2),
                       f"D_oxo{wpar}", waits=[P.now("dve")])
            cm[("oxo", wpar)] = st
            cm[("ow", wpar)] = P.counts["pe"]
    P.barrier()


def declare_mamba_tensors(nc, kind_x="ExternalInput", kind_y="ExternalOutput"):
    t = {}
    for nm, shape, dt in CONST_SPECS:
        t[nm] = nc.dram_tensor(nm, shape, dt, kind="ExternalInput")
    t["L0_attn_in_proj_wT"] = nc.dram_tensor("L0_attn_in_proj_wT", [DM, 2 * D_INNER], F32, kind="ExternalInput")
    t["L0_attn_x_proj_wT"] = nc.dram_tensor("L0_attn_x_proj_wT", [D_INNER, 4416], F32, kind="ExternalInput")
    t["L0_attn_dt_proj_wT"] = nc.dram_tensor("L0_attn_dt_proj_wT", [64, 64], F32, kind="ExternalInput")
    t["L0_attn_out_proj_wT"] = nc.dram_tensor("L0_attn_out_proj_wT", [D_INNER, DM], F32, kind="ExternalInput")
    t["L0_attn_conv_wT"] = nc.dram_tensor("L0_attn_conv_wT", [4, D_INNER], F32, kind="ExternalInput")
    t["L0_attn_conv_b"] = nc.dram_tensor("L0_attn_conv_b", [D_INNER], F32, kind="ExternalInput")
    t["L0_attn_dt_proj_b"] = nc.dram_tensor("L0_attn_dt_proj_b", [64], F32, kind="ExternalInput")
    t["L0_attn_A_log"] = nc.dram_tensor("L0_attn_A_log", [64], F32, kind="ExternalInput")
    t["L0_attn_D"] = nc.dram_tensor("L0_attn_D", [64], F32, kind="ExternalInput")
    t["xin"] = nc.dram_tensor("xin", [DM, NTH], F32, kind=kind_x)
    t["xout"] = nc.dram_tensor("xout", [DM, NTH], F32, kind=kind_y)
    t["hln2"] = nc.dram_tensor("hln2", [DM, NTH], BF16)
    t["z_s"] = nc.dram_tensor("z_s", [D_INNER, NT], BF16)
    t["u_s"] = nc.dram_tensor("u_s", [D_INNER, NT], BF16)
    t["xtm_s"] = nc.dram_tensor("xtm_s", [NT, D_INNER], BF16)
    t["y_s"] = nc.dram_tensor("y_s", [NT, D_INNER], BF16)
    t["ccin"] = nc.dram_tensor("ccin", [128, 4096], F32)
    t["ccout"] = nc.dram_tensor("ccout", [256, 4096], F32)
    t["rowd"] = nc.dram_tensor("rowd", [2, 4096], F32)
    return t


def emit_mamba_cols(P, t):
    for k in range(4):
        build_colset(P, f"cw{k}", t["L0_attn_conv_wT"].ap()[k, :], D_INNER)
    build_colset(P, "cb", t["L0_attn_conv_b"], D_INNER)


def build_mamba_program(upto="m7", dbg=()):
    from contextlib import ExitStack
    nc = bass.Bass("TRN2", target_bir_lowering=False)
    ctx = ExitStack()
    t = declare_mamba_tensors(nc)
    P = Prog(nc, ctx)
    with ctx:
        alloc_arenas(P)
        emit_consts(P, t)
        P.barrier()
        emit_mamba_cols(P, t)
        P.barrier()
        emit_mamba(P, t, t["xin"], t["xout"], "m", upto=upto)
        for nm in dbg:
            shp = {"hln2": [DM, NTH], "z_s": [D_INNER, NT], "u_s": [D_INNER, NT],
                   "xtm_s": [NT, D_INNER], "y_s": [NT, D_INNER],
                   "ccout": [256, 4096]}[nm]
            dtp = F32 if nm == "ccout" else BF16
            o = nc.dram_tensor("dump_" + nm, shp, dtp, kind="ExternalOutput")
            st = P.dma("gpsimd", o.ap(), t[nm].ap(), "D_dump")
            P.emit("gpsimd", lambda g: None, waits=[("D_dump", st)], inc="pe")
        P.finish()
    return nc


def host_prep_mamba(inputs):
    d = {}
    d["L0_attn_in_proj_wT"] = np.ascontiguousarray(inputs["L0_attn_in_proj_w"].T)
    d["L0_attn_x_proj_wT"] = np.ascontiguousarray(inputs["L0_attn_x_proj_w"].T)
    d["L0_attn_dt_proj_wT"] = np.ascontiguousarray(inputs["L0_attn_dt_proj_w"].T)
    d["L0_attn_out_proj_wT"] = np.ascontiguousarray(inputs["L0_attn_out_proj_w"].T)
    d["L0_attn_conv_wT"] = np.ascontiguousarray(inputs["L0_attn_conv_w"].T)
    for nm in ["L0_attn_conv_b", "L0_attn_dt_proj_b", "L0_attn_A_log", "L0_attn_D"]:
        d[nm] = inputs[nm]
    return d


def mamba_in_map(prep, prepm, xshard, core):
    m = {nm: np.asarray(prep[nm]) for nm, _, _ in CONST_SPECS if nm != "oddcol"}
    m["oddcol"] = prep["oddcol"][core]
    m.update(prepm)
    m["xin"] = xshard
    return m


def emit_attn1(P, t, xin, xout, tag="a"):
    A = Carve(P.sbufs["arA"], ARENA_A)
    B_ = Carve(P.sbufs["arB"], ARENA_B)
    H = Carve(P.sbufs["arH"], ARENA_H)
    hln = H.take(DC * NT)
    kv = H.take(4 * NT)
    bq = P.sbufs["col_l1bq"]
    bkv = P.sbufs["col_l1bkv"]

    def hw(c, t0, W):
        return hln[:, c * NT + (t0 - HALO): c * NT + (t0 - HALO) + W]

    emit_ln(P, xin, hw, tiles=TILES_MAIN, tag=f"{tag}0", ar1=A, ar2=B_)
    P.barrier()

    A = Carve(P.sbufs["arA"], ARENA_A)
    B_ = Carve(P.sbufs["arB"], ARENA_B)
    w_sl = A.take(DC * TT)
    qg = A.take(4 * NT)
    kdup = B_.take(4 * NT)
    prod = B_.take(NT)
    srow = B_.take(TT, F32)
    zrow = B_.take(TT, F32)
    erow = B_.take(TT, F32)
    rrow = B_.take(TT, F32)
    arow = B_.take(TT, F32)
    tv = B_.take(TT, F32)
    xi = [B_.take(TT, F32), B_.take(TT, F32)]
    xo = [B_.take(TT, F32), B_.take(TT, F32)]
    onescol_b = P.sbufs["c_onescol_b"]
    onesrow_f = P.sbufs["c_onesrow_f"]
    hv = hln.rearrange("p (c w) -> p c w", c=DC)
    kvv = kv.rearrange("p (c w) -> p c w", c=4)
    qv = qg.rearrange("p (c w) -> p c w", c=4)
    Wq, Wkv = t["L1_attn_Wq"], t["L1_attn_Wkv"]
    cm = {}

    def proj_block(wt, c0, dst_view, bias, kind):
        dsem = f"D_w"
        ld = P.dma("gpsimd", w_sl[:, :].rearrange("p (c w) -> p c w", c=DC),
                   wt.ap()[:, c0:c0 + TT].rearrange("(c p) w -> p c w", p=128),
                   dsem, waits=[("pe", P.counts["pe"])])
        wv = w_sl[:, :].rearrange("p (c w) -> p c w", c=DC)
        for ti in range(4):
            t0 = ti * TT
            for oc in range(4):
                bk = (ti * 4 + oc) % 4
                pe_done = mm_group(P, P.bank(bk).ap()[:, :],
                                   [(wv[:, c, oc * 128:(oc + 1) * 128],
                                     hv[:, c, t0:t0 + TT]) for c in range(DC)],
                                   waits=[(dsem, ld), cm.get(("bk", bk), ("dve", 0))])
                d = P.emit("vector", lambda v, bk=bk, oc=oc, t0=t0: v.tensor_scalar(
                    dst_view[:, oc, t0:t0 + TT], P.bank(bk).ap()[:, :],
                    bias[:, (c0 // 128 + oc) if kind == "q" else oc:
                         (c0 // 128 + oc) if kind == "q" else oc][:, 0:0] if False else
                    bias[:, (c0 // 128 + oc) if kind == "q" else oc:
                         ((c0 // 128 + oc) if kind == "q" else oc) + 1],
                    None, OP.add),
                    waits=[("pe", pe_done)], inc="dve")
                cm[("bk", bk)] = ("dve", d)

    proj_block(Wkv, 0, kvv, bkv, "kv")
    kddone = P.dma("gpsimd", kdup.rearrange("p (c w) -> p c w", c=4).__getitem__((slice(64, 128), slice(None), slice(None))),
                   kv.rearrange("p (c w) -> p c w", c=4)[0:64, :, :],
                   f"D_kd", waits=[P.now("dve")])
    kdv = kdup.rearrange("p (c w) -> p c w", c=4)
    for g in range(4):
        proj_block(Wq, g * TT, qv, bq, "q")
        for h in range(8):
            f0 = g * 512 + h * 64
            cq = (h * 64) // 128
            pq = (h * 64) % 128
            ksrc = (kvv[0:64, g, :] if pq == 0 else kdv[64:128, g, :])
            ps = slice(pq, pq + 64)
            dp = P.emit("vector", lambda v, cq=cq, ps=ps, ksrc=ksrc: v.tensor_mul(
                prod[ps, :], qv[ps.start:ps.stop, cq, :], ksrc),
                waits=[("pe", cm.get(("pr",), 0)), (f"D_kd", kddone)], inc="dve")
            for ti in range(4):
                t0 = ti * TT
                bk = 4 + ti % 2
                pe1 = P.emit("tensor", lambda tt, t0=t0, bk=bk, ps=ps: tt.matmul(
                    P.bank(bk).ap()[0:1, :], onescol_b[ps, 0:1], prod[ps, t0:t0 + TT],
                    start=True, stop=True),
                    waits=[("dve", dp), ("dve", cm.get(("sb", bk), 0))], inc="pe")
                d1 = P.emit("vector", lambda v, bk=bk: v.tensor_copy(srow[0:1, :], P.bank(bk).ap()[0:1, :]),
                            waits=[("pe", pe1)], inc="dve")
                cm[("sb", bk)] = d1
                P.emit("vector", lambda v: v.tensor_sub(zrow[0:1, :], srow[0:1, :], srow[0:1, :]),
                       waits=[P.now("act")], inc="dve")
                ae = P.emit("scalar", lambda s: s.activation(erow[0:1, :], zrow[0:1, :], AF.Exp),
                            waits=[P.now("dve")], inc="act")
                dr = P.emit("vector", lambda v: v.reciprocal(rrow[0:1, :], erow[0:1, :]),
                            waits=[("act", ae)], inc="dve")
                da = P.emit("vector", lambda v: v.tensor_mul(arow[0:1, :], erow[0:1, :], rrow[0:1, :]),
                            inc="dve")
                bk2 = 6 + ti % 2
                pe2 = P.emit("tensor", lambda tt, bk2=bk2: tt.matmul(
                    P.bank(bk2).ap()[:, :], onesrow_f[0:1, :], arow[0:1, :],
                    start=True, stop=True),
                    waits=[("dve", da), ("dve", cm.get(("ab", bk2), 0))], inc="pe")
                xp = ti % 2
                xsem = f"D_xi{xp}"
                xl = P.dma("gpsimd", xi[xp][64:128, :],
                           xin.ap()[f0:f0 + 64, HALO + t0:HALO + t0 + TT],
                           xsem, waits=[("dve", cm.get(("xi", xp), 0))])
                d2 = P.emit("vector", lambda v, g=g, bk2=bk2, t0=t0: v.tensor_mul(
                    tv[64:128, :], kvv[64:128, g, t0:t0 + TT], P.bank(bk2).ap()[64:128, :]),
                    waits=[("pe", pe2)], inc="dve")
                cm[("ab", bk2)] = d2
                d3 = P.emit("vector", lambda v, xp=xp: v.tensor_add(
                    xo[xp][64:128, :], tv[64:128, :], xi[xp][64:128, :]),
                    waits=[(xsem, xl), (f"D_xo{xp}", cm.get(("xo", xp), 0))], inc="dve")
                cm[("xi", xp)] = d3
                st = P.dma("sync", xout.ap()[f0:f0 + 64, HALO + t0:HALO + t0 + TT],
                           xo[xp][64:128, :], f"D_xo{xp}", waits=[("dve", d3)])
                cm[("xo", xp)] = st
            cm[("pr",)] = P.counts["pe"]
    P.barrier()


def emit_ffn1(P, t, xin, xout, tag, fold_wd=False):
    A = Carve(P.sbufs["arA"], ARENA_A)
    B_ = Carve(P.sbufs["arB"], ARENA_B)
    H = Carve(P.sbufs["arH"], ARENA_H)
    had = P.sbufs["c_had128"]
    idb, nidb = P.sbufs["c_id128b"], P.sbufs["c_negid128b"]
    su = P.sbufs["col_l1su"]
    wsg = P.sbufs["col_l1wsg"]
    combo = P.sbufs["col_l1combo"]
    wsd = P.sbufs["col_l1wsd"]

    if fold_wd:
        fbA = A.take(64 * 256)
        fbB = B_.take(64 * 256)
        fv = [fbA.rearrange("p (c w) -> p c w", c=64), fbB.rearrange("p (c w) -> p c w", c=64)]
        wdT = t["L1_ffn_W_downT"]
        cm = {}
        for ot in range(8):
            ld = P.dma("gpsimd", fv[0],
                       wdT.ap()[:, ot * 256:(ot + 1) * 256].rearrange("(c p) w -> p c w", p=128),
                       f"D_fw", waits=[("pe", cm.get(("lda",), 0)), P.now("dve")])
            cur = 0
            first_w = [(f"D_fw", ld), P.now("dve")]
            for c in range(64):
                bk = c % 4
                pe1 = P.emit("tensor", lambda tt, src_ap=fv[cur][:, c, :], bk=bk: tt.matmul(
                    P.bank(bk).ap()[:, 0:256], had[:, :], src_ap,
                    start=True, stop=True),
                    waits=(first_w if c == 0 else [P.now("dve")] if c < 4 else
                           [("dve", cm.get(("fb", bk), 0))]), inc="pe")
                d1 = P.emit("vector", lambda v, dst_ap=fv[1 - cur][:, c, :], bk=bk: v.tensor_copy(
                    dst_ap, P.bank(bk).ap()[:, 0:256]),
                    waits=[("pe", pe1)], inc="dve")
                cm[("fb", bk)] = d1
            cur = 1 - cur
            for sbit in range(6):
                bit = 1 << sbit
                for c in range(64):
                    bk = c % 4
                    pa = c & ~bit
                    pb = c | bit
                    sgn = nidb if (c & bit) else idb
                    pe1 = mm_group(P, P.bank(bk).ap()[:, 0:256],
                                   [(idb[:, :], fv[cur][:, pa, :]),
                                    (sgn[:, :], fv[cur][:, pb, :])],
                                   waits=([P.now("dve")] if c < 4 else [("dve", cm.get(("fb", bk), 0))]))
                    d1 = P.emit("vector", lambda v, dst_ap=fv[1 - cur][:, c, :], bk=bk: v.tensor_copy(
                        dst_ap, P.bank(bk).ap()[:, 0:256]),
                        waits=[("pe", pe1)], inc="dve")
                    cm[("fb", bk)] = d1
                cur = 1 - cur
            st = P.dma("sync", t["wdp_s"].ap()[:, ot * 256:(ot + 1) * 256].rearrange("(c p) w -> p c w", p=128),
                       fv[cur], f"D_fo", waits=[P.now("dve")])
            cm[("lda",)] = P.counts["pe"]
        P.barrier()
        A = Carve(P.sbufs["arA"], ARENA_A)
        B_ = Carve(P.sbufs["arB"], ARENA_B)

    hln = H.take(DC * NT)

    def hw(c, t0, W):
        return hln[:, c * NT + (t0 - HALO): c * NT + (t0 - HALO) + W]

    emit_ln(P, xin, hw, su_col=su, tiles=TILES_MAIN, tag=f"{tag}0", ar1=A, ar2=B_)
    P.barrier()
    A = Carve(P.sbufs["arA"], ARENA_A)
    B_ = Carve(P.sbufs["arB"], ARENA_B)
    wbA = A.take(DC * TT)
    wbB = B_.take(DC * TT)
    wv2 = [wbA.rearrange("p (c w) -> p c w", c=DC), wbB.rearrange("p (c w) -> p c w", c=DC)]
    hv = hln.rearrange("p (c w) -> p c w", c=DC)
    cm = {}
    for ti in range(4):
        t0 = ti * TT
        cur = 0
        for c in range(DC):
            bk = c % 4
            pe1 = P.emit("tensor", lambda tt, c=c, t0=t0, bk=bk: tt.matmul(
                P.bank(bk).ap()[:, :], had[:, :], hv[:, c, t0:t0 + TT],
                start=True, stop=True),
                waits=([P.now("dve")] if c < 4 else [("dve", cm.get(("fb", bk), 0))]), inc="pe")
            d1 = P.emit("vector", lambda v, dst_ap=wv2[0][:, c, :], bk=bk: v.tensor_copy(
                dst_ap, P.bank(bk).ap()[:, :]),
                waits=[("pe", pe1)], inc="dve")
            cm[("fb", bk)] = d1
        for sbit in range(4):
            bit = 1 << sbit
            last = sbit == 3
            for c in range(DC):
                bk = c % 4
                pa = c & ~bit
                pb = c | bit
                sgn = nidb if (c & bit) else idb
                pe1 = mm_group(P, P.bank(bk).ap()[:, :],
                               [(idb[:, :], wv2[cur][:, pa, :]),
                                (sgn[:, :], wv2[cur][:, pb, :])],
                               waits=[("dve", cm.get(("fb", bk), 0))])
                dst = (hv[:, c, t0:t0 + TT] if last else wv2[1 - cur][:, c, :])
                d1 = P.emit("vector", lambda v, dst=dst, bk=bk: v.tensor_copy(
                    dst, P.bank(bk).ap()[:, :]),
                    waits=[("pe", pe1)], inc="dve")
                cm[("fb", bk)] = d1
            cur = 1 - cur
    P.barrier()

    A = Carve(P.sbufs["arA"], ARENA_A)
    B_ = Carve(P.sbufs["arB"], ARENA_B)
    wg = [A.take(DC * TT), B_.take(DC * TT)]
    wu = [A.take(DC * TT), B_.take(DC * TT)]
    sg_sb = [A.take(TT), A.take(TT)]
    h_sb = [A.take(TT) for _ in range(4)]
    hmid = t["hmid"]
    wgT, wuT = t["L1_ffn_W_gateT"], t["L1_ffn_W_upT"]
    bank_free = {}
    slot_free = {}
    hs_free = {}
    sg_free = {}
    for ob in range(16):
        wpar = ob % 2
        dsem = f"D_w{wpar}"
        for (wt, dst) in ((wgT, wg[wpar]), (wuT, wu[wpar])):
            ld = P.dma("gpsimd", dst[:, :].rearrange("p (c w) -> p c w", c=DC),
                       wt.ap()[:, ob * TT:(ob + 1) * TT].rearrange("(c p) w -> p c w", p=128),
                       dsem, waits=[("pe", slot_free.get(wpar, 0))])
        for ti in range(4):
            t0 = ti * TT
            for oc in range(4):
                bsel = (ti * 4 + oc) % 2
                bg, bu = P.bank(bsel).ap(), P.bank(2 + bsel).ap()
                og = ob * 4 + oc
                gw = wg[wpar][:, :].rearrange("p (c w) -> p c w", c=DC)
                uw = wu[wpar][:, :].rearrange("p (c w) -> p c w", c=DC)
                mm_group(P, bg[:, :],
                         [(gw[:, c, oc * 128:(oc + 1) * 128], hv[:, c, t0:t0 + TT]) for c in range(DC)],
                         waits=[(dsem, ld), bank_free.get(bsel, ("pe", 0))])
                pe_done = mm_group(P, bu[:, :],
                                   [(uw[:, c, oc * 128:(oc + 1) * 128], hv[:, c, t0:t0 + TT]) for c in range(DC)],
                                   waits=[bank_free.get(2 + bsel, ("pe", 0))])
                sgp = (ti * 4 + oc) % 2
                a = P.emit("scalar", lambda s, bg=bg, og=og, sgp=sgp: s.activation(
                    sg_sb[sgp][:, :], bg[:, :], AF.Silu, scale=wsg[:, og:og + 1]),
                    waits=[("pe", pe_done), ("dve", sg_free.get(sgp, 0))], inc="act")
                bank_free[bsel] = ("act", a)
                hp = (ti * 4 + oc) % 4
                d = P.emit("vector", lambda v, bu=bu, og=og, sgp=sgp, hp=hp: v.scalar_tensor_tensor(
                    h_sb[hp][:, :], sg_sb[sgp][:, :], combo[:, og:og + 1], bu[:, :],
                    OP.mult, OP.mult),
                    waits=[("act", a), (f"D_h{hp}", hs_free.get(hp, 0))], inc="dve")
                bank_free[2 + bsel] = ("dve", d)
                sg_free[sgp] = d
                st = P.dma("sync", hmid.ap()[og * 128:(og + 1) * 128, t0:t0 + TT],
                           h_sb[hp][:, :], f"D_h{hp}", waits=[("dve", d)])
                hs_free[hp] = st
        slot_free[wpar] = P.counts["pe"]
    P.barrier()

    A = Carve(P.sbufs["arA"], ARENA_A)
    B_ = Carve(P.sbufs["arB"], ARENA_B)
    H2 = Carve(P.sbufs["arH"], ARENA_H)
    OBW = 256
    wd = [A.take(64 * OBW), B_.take(64 * OBW)]
    hq = [H2.take(DC * TT), H2.take(DC * TT)]
    xi = [H2.take(2 * TT, F32), H2.take(2 * TT, F32)]
    xo = [H2.take(2 * TT, F32), H2.take(2 * TT, F32)]
    wdp = t["wdp_s"]
    bank_free = {}
    slot_free = {}
    xo_free = {}
    xi_free = {}
    hq_free = {}
    for ob in range(8):
        wpar = ob % 2
        dsem = f"D_wd{wpar}"
        ld = P.dma("gpsimd", wd[wpar][:, :].rearrange("p (c w) -> p c w", c=64),
                   wdp.ap()[:, ob * OBW:(ob + 1) * OBW].rearrange("(c p) w -> p c w", p=128),
                   dsem, waits=[("pe", slot_free.get(wpar, 0))])
        for ti in range(4):
            t0 = HALO + ti * TT
            tb = (ti % 2) * 2
            hpar = ti % 2
            xsem = f"D_xi{hpar}"
            xl = P.dma("gpsimd", xi[hpar][:, :].rearrange("p (c w) -> p c w", c=2),
                       xin.ap()[ob * OBW:(ob + 1) * OBW, t0:t0 + TT].rearrange("(c p) w -> p c w", p=128),
                       xsem, waits=[("dve", xi_free.get(hpar, 0))])
            wdv = wd[wpar][:, :].rearrange("p (c w) -> p c w", c=64)
            for kq in range(4):
                hb = kq % 2
                hsem = f"D_hq{hb}"
                hl = P.dma("gpsimd", hq[hb][:, :].rearrange("p (c w) -> p c w", c=DC),
                           hmid.ap()[kq * 2048:(kq + 1) * 2048, t0 - HALO:t0 - HALO + TT]
                           .rearrange("(c p) w -> p c w", p=128),
                           hsem, waits=[("pe", hq_free.get(hb, 0))])
                hqv = hq[hb][:, :].rearrange("p (c w) -> p c w", c=DC)
                for oc in range(2):
                    wts = [(hsem, hl)]
                    if kq == 0:
                        wts += [bank_free.get(tb + oc, ("dve", 0))]
                    for c in range(DC):
                        gk = kq * DC + c
                        P.emit("tensor", lambda t_, gk=gk, oc=oc, c=c, wdv=wdv, hqv=hqv, tb=tb:
                               t_.matmul(P.bank(tb + oc).ap()[:, :],
                                         wdv[:, gk, oc * 128:(oc + 1) * 128], hqv[:, c, :],
                                         start=(gk == 0), stop=(gk == 63)),
                               waits=(wts if c == 0 else ()),
                               inc=("pe" if c == DC - 1 else None))
                hq_free[hb] = P.counts["pe"]
            pe_done = P.counts["pe"]
            for oc in range(2):
                og = ob * 2 + oc
                d = P.emit("vector", lambda v, oc=oc, og=og, hpar=hpar, tb=tb: v.scalar_tensor_tensor(
                    xo[hpar][:, oc * TT:(oc + 1) * TT], P.bank(tb + oc).ap()[:, :], wsd[:, og:og + 1],
                    xi[hpar][:, oc * TT:(oc + 1) * TT], OP.mult, OP.add),
                    waits=[("pe", pe_done), (xsem, xl), (f"D_xo{hpar}", xo_free.get(hpar, 0))],
                    inc="dve")
                bank_free[tb + oc] = ("dve", d)
            xi_free[hpar] = P.counts["dve"]
            st = P.dma("sync", xout.ap()[ob * OBW:(ob + 1) * OBW, t0:t0 + TT].rearrange("(c p) w -> p c w", p=128),
                       xo[hpar][:, :].rearrange("p (c w) -> p c w", c=2),
                       f"D_xo{hpar}", waits=[P.now("dve")]) 
            xo_free[hpar] = st
        slot_free[wpar] = P.counts["pe"]
    P.barrier()


def declare_ffn1_tensors(nc, kind_x="ExternalInput", kind_y="ExternalOutput"):
    t = {}
    for nm, shape, dt in CONST_SPECS:
        t[nm] = nc.dram_tensor(nm, shape, dt, kind="ExternalInput")
    t["L1_ffn_W_upT"] = nc.dram_tensor("L1_ffn_W_upT", [DM, INTER], F32, kind="ExternalInput")
    t["L1_ffn_W_gateT"] = nc.dram_tensor("L1_ffn_W_gateT", [DM, INTER], F32, kind="ExternalInput")
    t["L1_ffn_W_downT"] = nc.dram_tensor("L1_ffn_W_downT", [INTER, DM], F32, kind="ExternalInput")
    t["L1_ffn_SU_ug"] = nc.dram_tensor("L1_ffn_SU_ug", [DM], F32, kind="ExternalInput")
    t["L1_ffn_SU_dp"] = nc.dram_tensor("L1_ffn_SU_dp", [INTER], F32, kind="ExternalInput")
    t["L1_ffn_Wscale_ug"] = nc.dram_tensor("L1_ffn_Wscale_ug", [2 * INTER], F32, kind="ExternalInput")
    t["L1_ffn_Wscale_dp"] = nc.dram_tensor("L1_ffn_Wscale_dp", [DM], F32, kind="ExternalInput")
    t["xin"] = nc.dram_tensor("xin", [DM, NTH], F32, kind=kind_x)
    t["xout"] = nc.dram_tensor("xout", [DM, NTH], F32, kind=kind_y)
    t["hmid"] = nc.dram_tensor("hmid", [INTER, NT], BF16)
    t["wdp_s"] = nc.dram_tensor("wdp_s", [INTER, DM], BF16)
    return t


def emit_ffn1_cols(P, t):
    import math
    s2 = 1.0 / math.sqrt(2048.0)
    s8 = 1.0 / math.sqrt(8192.0)
    build_colset(P, "l1su", t["L1_ffn_SU_ug"], DM)
    build_colset(P, "l1wsg", t["L1_ffn_Wscale_ug"].ap()[INTER:2 * INTER], INTER, post_scale=s2)
    wsu = build_colset(P, "l1wsu", t["L1_ffn_Wscale_ug"].ap()[0:INTER], INTER, post_scale=s2)
    sdp = build_colset(P, "l1sdp", t["L1_ffn_SU_dp"], INTER)
    build_colset(P, "l1wsd", t["L1_ffn_Wscale_dp"], DM, post_scale=0.5 * s8)
    combo = P.sbuf("col_l1combo", [128, 64], F32)
    P.emit("vector", lambda v: v.tensor_mul(combo[:, :], wsu[:, :], sdp[:, :]),
           waits=[P.now("dve"), P.now("pe")], inc="dve")


def declare_attn1_tensors(nc, kind_x="ExternalInput", kind_y="ExternalOutput"):
    t = {}
    for nm, shape, dt in CONST_SPECS:
        t[nm] = nc.dram_tensor(nm, shape, dt, kind="ExternalInput")
    t["L1_attn_Wq"] = nc.dram_tensor("L1_attn_Wq", [DM, DM], F32, kind="ExternalInput")
    t["L1_attn_bq"] = nc.dram_tensor("L1_attn_bq", [DM], F32, kind="ExternalInput")
    t["L1_attn_Wkv"] = nc.dram_tensor("L1_attn_Wkv", [DM, 512], F32, kind="ExternalInput")
    t["L1_attn_bkv"] = nc.dram_tensor("L1_attn_bkv", [512], F32, kind="ExternalInput")
    t["xin"] = nc.dram_tensor("xin", [DM, NTH], F32, kind=kind_x)
    t["xout"] = nc.dram_tensor("xout", [DM, NTH], F32, kind=kind_y)
    return t


def emit_attn1_cols(P, t):
    build_colset(P, "l1bq", t["L1_attn_bq"], DM)
    build_colset(P, "l1bkv", t["L1_attn_bkv"], 512)


def build_ffn1_program(fold=True):
    from contextlib import ExitStack
    nc = bass.Bass("TRN2", target_bir_lowering=False)
    ctx = ExitStack()
    t = declare_ffn1_tensors(nc)
    P = Prog(nc, ctx)
    with ctx:
        alloc_arenas(P)
        emit_consts(P, t)
        P.barrier()
        emit_ffn1_cols(P, t)
        P.barrier()
        emit_ffn1(P, t, t["xin"], t["xout"], "f1", fold_wd=fold)
        P.finish()
    return nc


def build_attn1_program():
    from contextlib import ExitStack
    nc = bass.Bass("TRN2", target_bir_lowering=False)
    ctx = ExitStack()
    t = declare_attn1_tensors(nc)
    P = Prog(nc, ctx)
    with ctx:
        alloc_arenas(P)
        emit_consts(P, t)
        P.barrier()
        emit_attn1_cols(P, t)
        P.barrier()
        emit_attn1(P, t, t["xin"], t["xout"], "a")
        P.finish()
    return nc


def host_prep_ffn1(inputs):
    d = {}
    d["L1_ffn_W_upT"] = np.ascontiguousarray(inputs["L1_ffn_W_up"].T)
    d["L1_ffn_W_gateT"] = np.ascontiguousarray(inputs["L1_ffn_W_gate"].T)
    d["L1_ffn_W_downT"] = np.ascontiguousarray(inputs["L1_ffn_W_down"].T)
    for nm in ["L1_ffn_SU_ug", "L1_ffn_SU_dp", "L1_ffn_Wscale_ug", "L1_ffn_Wscale_dp"]:
        d[nm] = inputs[nm]
    return d


def host_prep_attn1(inputs):
    return {nm: inputs[nm] for nm in ["L1_attn_Wq", "L1_attn_bq", "L1_attn_Wkv", "L1_attn_bkv"]}


def generic_in_map(prep, extra, xshard, core):
    m = {nm: np.asarray(prep[nm]) for nm, _, _ in CONST_SPECS if nm != "oddcol"}
    m["oddcol"] = prep["oddcol"][core]
    m.update(extra)
    m["xin"] = xshard
    return m


def declare_all_tensors(nc):
    t = {}
    for nm, shape, dt in CONST_SPECS:
        t[nm] = nc.dram_tensor(nm, shape, dt, kind="ExternalInput")
    t["L0_ffn_gate_wT"] = nc.dram_tensor("L0_ffn_gate_wT", [DM, INTER], I32, kind="ExternalInput")
    t["L0_ffn_up_wT"] = nc.dram_tensor("L0_ffn_up_wT", [DM, INTER], I32, kind="ExternalInput")
    t["L0_ffn_down_wT"] = nc.dram_tensor("L0_ffn_down_wT", [INTER, DM], I32, kind="ExternalInput")
    t["L0_ffn_gate_scale"] = nc.dram_tensor("L0_ffn_gate_scale", [INTER], F32, kind="ExternalInput")
    t["L0_ffn_up_scale"] = nc.dram_tensor("L0_ffn_up_scale", [INTER], F32, kind="ExternalInput")
    t["L0_ffn_down_scale"] = nc.dram_tensor("L0_ffn_down_scale", [DM], F32, kind="ExternalInput")
    t["L0_attn_in_proj_wT"] = nc.dram_tensor("L0_attn_in_proj_wT", [DM, 2 * D_INNER], F32, kind="ExternalInput")
    t["L0_attn_x_proj_wT"] = nc.dram_tensor("L0_attn_x_proj_wT", [D_INNER, 4416], F32, kind="ExternalInput")
    t["L0_attn_dt_proj_wT"] = nc.dram_tensor("L0_attn_dt_proj_wT", [64, 64], F32, kind="ExternalInput")
    t["L0_attn_out_proj_wT"] = nc.dram_tensor("L0_attn_out_proj_wT", [D_INNER, DM], F32, kind="ExternalInput")
    t["L0_attn_conv_wT"] = nc.dram_tensor("L0_attn_conv_wT", [4, D_INNER], F32, kind="ExternalInput")
    t["L0_attn_conv_b"] = nc.dram_tensor("L0_attn_conv_b", [D_INNER], F32, kind="ExternalInput")
    t["L0_attn_dt_proj_b"] = nc.dram_tensor("L0_attn_dt_proj_b", [64], F32, kind="ExternalInput")
    t["L0_attn_A_log"] = nc.dram_tensor("L0_attn_A_log", [64], F32, kind="ExternalInput")
    t["L0_attn_D"] = nc.dram_tensor("L0_attn_D", [64], F32, kind="ExternalInput")
    t["L1_attn_Wq"] = nc.dram_tensor("L1_attn_Wq", [DM, DM], F32, kind="ExternalInput")
    t["L1_attn_bq"] = nc.dram_tensor("L1_attn_bq", [DM], F32, kind="ExternalInput")
    t["L1_attn_Wkv"] = nc.dram_tensor("L1_attn_Wkv", [DM, 512], F32, kind="ExternalInput")
    t["L1_attn_bkv"] = nc.dram_tensor("L1_attn_bkv", [512], F32, kind="ExternalInput")
    t["L1_ffn_W_upT"] = nc.dram_tensor("L1_ffn_W_upT", [DM, INTER], F32, kind="ExternalInput")
    t["L1_ffn_W_gateT"] = nc.dram_tensor("L1_ffn_W_gateT", [DM, INTER], F32, kind="ExternalInput")
    t["L1_ffn_W_downT"] = nc.dram_tensor("L1_ffn_W_downT", [INTER, DM], F32, kind="ExternalInput")
    t["L1_ffn_SU_ug"] = nc.dram_tensor("L1_ffn_SU_ug", [DM], F32, kind="ExternalInput")
    t["L1_ffn_SU_dp"] = nc.dram_tensor("L1_ffn_SU_dp", [INTER], F32, kind="ExternalInput")
    t["L1_ffn_Wscale_ug"] = nc.dram_tensor("L1_ffn_Wscale_ug", [2 * INTER], F32, kind="ExternalInput")
    t["L1_ffn_Wscale_dp"] = nc.dram_tensor("L1_ffn_Wscale_dp", [DM], F32, kind="ExternalInput")
    t["xin"] = nc.dram_tensor("xin", [DM, NTH], F32, kind="ExternalInput")
    t["xout"] = nc.dram_tensor("xout", [DM, NTH], F32, kind="ExternalOutput")
    t["xA"] = nc.dram_tensor("xA", [DM, NTH], F32)
    t["xB"] = nc.dram_tensor("xB", [DM, NTH], F32)
    t["hmid"] = nc.dram_tensor("hmid", [INTER, NTH], BF16)
    t["wdp_s"] = nc.dram_tensor("wdp_s", [INTER, DM], BF16)
    t["hln2"] = nc.dram_tensor("hln2", [DM, NTH], BF16)
    t["z_s"] = nc.dram_tensor("z_s", [D_INNER, NT], BF16)
    t["u_s"] = nc.dram_tensor("u_s", [D_INNER, NT], BF16)
    t["xtm_s"] = nc.dram_tensor("xtm_s", [NT, D_INNER], BF16)
    t["y_s"] = nc.dram_tensor("y_s", [NT, D_INNER], BF16)
    t["ccin"] = nc.dram_tensor("ccin", [128, 4096], F32)
    t["ccout"] = nc.dram_tensor("ccout", [256, 4096], F32)
    t["rowd"] = nc.dram_tensor("rowd", [2, 4096], F32)
    return t


def build_fused_program():
    from contextlib import ExitStack
    nc = bass.Bass("TRN2", target_bir_lowering=False)
    ctx = ExitStack()
    t = declare_all_tensors(nc)
    P = Prog(nc, ctx)
    with ctx:
        alloc_arenas(P)
        emit_consts(P, t)
        P.barrier()
        build_colset(P, "l0g", t["L0_ffn_gate_scale"], INTER)
        build_colset(P, "l0u", t["L0_ffn_up_scale"], INTER)
        build_colset(P, "l0d", t["L0_ffn_down_scale"], DM, post_scale=0.5)
        emit_mamba_cols(P, t)
        emit_ffn1_cols(P, t)
        emit_attn1_cols(P, t)
        P.barrier()
        emit_ffn0(P, t, t["xin"], t["xA"], "s1")
        emit_mamba(P, t, t["xA"], t["xB"], "s2")
        emit_ffn0(P, t, t["xB"], t["xA"], "s3")
        emit_ffn1(P, t, t["xA"], t["xB"], "s4", fold_wd=True)
        emit_attn1(P, t, t["xB"], t["xA"], "s5")
        emit_ffn1(P, t, t["xA"], t["xout"], "s6", fold_wd=False)
        P.barrier()
        P.finish()
    return nc


def fused_in_map(inputs, prep, prepm, prepf1, prepa1, xshard, core):
    m = {nm: np.asarray(prep[nm]) for nm, _, _ in CONST_SPECS if nm != "oddcol"}
    m["oddcol"] = prep["oddcol"][core]
    for nm in ["L0_ffn_gate_wT", "L0_ffn_up_wT", "L0_ffn_down_wT",
               "L0_ffn_gate_scale", "L0_ffn_up_scale", "L0_ffn_down_scale"]:
        m[nm] = prep[nm]
    m.update(prepm)
    m.update(prepf1)
    m.update(prepa1)
    m["xin"] = xshard
    return m


def kernel_fused(**inputs):
    x = inputs["L0_attn_x"]
    prep = host_prep_common(inputs)
    prepm = host_prep_mamba(inputs)
    prepf1 = host_prep_ffn1(inputs)
    prepa1 = host_prep_attn1(inputs)
    nc = build_fused_program()
    shards = shard_x(x)
    maps = [fused_in_map(inputs, prep, prepm, prepf1, prepa1, shards[k], k)
            for k in range(NCORES)]
    res = run_bass_kernel_spmd(nc, maps, core_ids=list(range(NCORES)))
    return _gather(res)


def kernel(**inputs):
    return kernel_fused(**inputs)


# version 04 (final)
# speedup vs baseline: 1.5102x; 1.0581x over previous
import numpy as np
import ml_dtypes

import concourse.bass as bass
import concourse.mybir as mybir
from concourse.bass_utils import run_bass_kernel_spmd

F32 = mybir.dt.float32
BF16 = mybir.dt.bfloat16
I32 = mybir.dt.int32
AF = mybir.ActivationFunctionType
OP = mybir.AluOpType

NCORES = 8
DM = 2048
NT = 2048
HALO = 4
NTH = NT + HALO
DC = DM // 128
TT = 512
TILES_ALL = [(0, TT), (TT, TT), (2 * TT, TT), (3 * TT, TT), (4 * TT, HALO)]
TILES_MAIN = [(HALO + i * TT, TT) for i in range(4)]

D_INNER = 4096
NHEADS = 64
HEADDIM = 64
D_STATE = 128
CHUNK = 64
NCHUNK = NT // CHUNK
INTER = 8192

ENGINES = ("tensor", "scalar", "vector", "gpsimd", "sync")

BF = ml_dtypes.bfloat16


class Prog:
    def __init__(self, nc, ctx):
        self.nc = nc
        self._ctx = ctx
        self.ops = {e: [] for e in ENGINES}
        self.sems = {}
        self.counts = {}
        self.sbufs = {}
        self.psums = {}
        self.bar_tgt = 0

    def sem(self, name):
        if name not in self.sems:
            self.sems[name] = self._ctx.enter_context(self.nc.semaphore(name))
            self.counts[name] = 0
        return self.sems[name]

    def sbuf(self, name, shape, dtype):
        if name not in self.sbufs:
            self.sbufs[name] = self._ctx.enter_context(self.nc.sbuf_tensor(name, shape, dtype))
        return self.sbufs[name]

    def bank(self, i):
        nm = f"pb{i}"
        if nm not in self.psums:
            self.psums[nm] = self._ctx.enter_context(self.nc.psum_tensor(nm, [128, 512], F32))
        return self.psums[nm]

    def bank_bf(self, i):
        return self.bank(i).ap().bitcast(BF16)

    def emit(self, eng, fn, waits=(), inc=None, amt=1, is_dma=False):
        if inc:
            self.sem(inc)
        for s, _ in waits:
            self.sem(s)
        new = None
        if inc is not None:
            self.counts[inc] += amt
            new = self.counts[inc]
        hard = (not is_dma) and eng in ("scalar", "vector", "gpsimd")

        def closure(e, fn=fn, waits=tuple(waits), inc=inc, amt=amt, P=self, hard=hard):
            for s, v in waits:
                if v > 0:
                    e.wait_ge(P.sems[s], v)
            inst = fn(e)
            if inc is not None:
                if inst is None:
                    e.sem_inc(P.sems[inc], amt)
                elif hard:
                    e.drain()
                    e.sem_inc(P.sems[inc], amt)
                else:
                    inst.then_inc(P.sems[inc], amt)

        self.ops[eng].append(closure)
        return new

    def dma(self, q, out, in_, sem, waits=(), **kw):
        return self.emit(q, lambda e: e.dma_start(out=out, in_=in_, **kw), waits, sem, 16,
                         is_dma=True)

    def now(self, sem):
        return (sem, self.counts.get(sem, 0))

    def barrier(self):
        self.sem("bar")
        dma_waits = tuple((s, self.counts[s]) for s in self.counts if s.startswith("D_"))
        self.counts["bar"] += len(ENGINES)
        tgt = self.counts["bar"]
        for e in ENGINES:
            def closure(eng, dw=dma_waits, tgt=tgt, P=self, e=e):
                for s, v in dw:
                    if v > 0:
                        eng.wait_ge(P.sems[s], v)
                if e != "sync":
                    eng.drain()
                eng.sem_inc(P.sems["bar"], 1)
                eng.wait_ge(P.sems["bar"], tgt)
            self.ops[e].append(closure)

    def finish(self):
        nc = self.nc
        with nc.Block() as block:
            @block.tensor
            def _(t):
                for f in self.ops["tensor"]:
                    f(t)

            @block.scalar
            def _(s):
                for f in self.ops["scalar"]:
                    f(s)

            @block.vector
            def _(v):
                for f in self.ops["vector"]:
                    f(v)

            @block.gpsimd
            def _(g):
                for f in self.ops["gpsimd"]:
                    f(g)

            @block.sync
            def _(s):
                for f in self.ops["sync"]:
                    f(s)


class Carve:

    def __init__(self, handle, elems):
        self.h = handle
        self.off = 0
        self.n = elems

    def take(self, elems, dtype=BF16):
        e2 = elems if dtype == BF16 else 2 * elems
        off = (self.off + 1) & ~1
        self.off = off + e2
        assert self.off <= self.n, f"arena overflow {self.off} > {self.n}"
        v = self.h[:, off:off + e2]
        if dtype != BF16:
            v = v.bitcast(dtype)
        return v


def mm_group(P, out_ap, pairs, waits=(), inc="pe"):
    n = len(pairs)
    last = None
    for j, (l, r) in enumerate(pairs):
        last = P.emit(
            "tensor",
            lambda t, l=l, r=r, s=(j == 0), st=(j == n - 1), o=out_ap: t.matmul(o, l, r, start=s, stop=st),
            waits=(waits if j == 0 else ()),
            inc=(inc if j == n - 1 else None),
        )
    return last


CONST_SPECS = [
    ("had128", [128, 128], BF16), ("id128b", [128, 128], BF16),
    ("negid128b", [128, 128], BF16), ("rowsel", [64, 64 * 64], BF16),
    ("onescol_b", [128, 1], BF16),
    ("onesrow_f", [1, 128], F32), ("id128f", [128, 128], F32),
    ("triu64", [64, 64], F32), ("oddcol", [128, 1], F32),
    ("fconst", [128, 4], F32),
]


def emit_consts(P, t):
    for nm, shape, dt in CONST_SPECS:
        dst = P.sbuf("c_" + nm, shape, dt)
        P.dma("gpsimd", dst[:, :], t[nm].ap(), "D_init")


def build_colset(P, name, vec_dram, n, post_scale=None):
    k = n // 128
    assert k <= 128
    stage = P.sbuf("cs_stage", [128, 128], F32)
    out = P.sbuf(f"col_{name}", [128, k], F32)
    ptr = P.bank(7)
    id128f = P.sbufs["c_id128f"]
    vap = vec_dram if isinstance(vec_dram, bass.AP) else vec_dram.ap()
    ld = P.dma("gpsimd", stage[0:k, :], vap.rearrange("(k p) -> k p", p=128),
               "D_cs", waits=[P.now("pe")])
    P.emit("tensor", lambda t: t.transpose(ptr[0:128, 0:k], stage[0:k, :], id128f[0:k, 0:k]),
           waits=[("D_cs", ld), P.now("dve")], inc="pe")
    sc = 1.0 if post_scale is None else float(post_scale)
    P.emit("vector", lambda v: v.tensor_scalar(out[:, :], ptr.ap()[0:128, 0:k], sc, None, OP.mult),
           waits=[P.now("pe")], inc="dve")
    return out


def emit_ln(P, xin, hln_write, su_col=None, tiles=TILES_ALL, tag="ln", ar1=None, ar2=None,
            tile_done=None, extra_waits=None):
    xb = [ar1.take(DC * TT), ar2.take(DC * TT)]
    sq = [ar1.take(DC * TT), ar2.take(DC * TT)]
    rows = ar2.take(4 * TT, F32)[0:1, :]
    sub_s = ar2.take(TT, F32)
    onescol_b = P.sbufs["c_onescol_b"]
    onesrow_f = P.sbufs["c_onesrow_f"]
    sx = [P.bank(0).ap()[0:1, :], P.bank(1).ap()[0:1, :]]
    sxx = [P.bank(2).ap()[0:1, :], P.bank(3).ap()[0:1, :]]
    mubc = [P.bank(4).ap(), P.bank(5).ap()]
    rbc = [P.bank(6).ap(), P.bank(7).ap()]

    for ti, (t0, W) in enumerate(tiles):
        par = ti % 2
        dsem = f"D_x{par}"
        ld = P.dma("gpsimd", xb[par][:, 0:DC * W].rearrange("p (c w) -> p c w", c=DC),
                   xin.ap()[:, t0:t0 + W].rearrange("(c p) w -> p c w", p=128),
                   dsem, waits=[P.now("dve")])
        sq_done = P.emit("scalar", lambda s, par=par, W=W: s.activation(
            sq[par][:, 0:DC * W], xb[par][:, 0:DC * W], AF.Square),
            waits=[(dsem, ld)], inc="act")
        pe_w = [(dsem, ld), ("act", sq_done), P.now("dve")]
        for c in range(DC):
            P.emit("tensor", lambda t, par=par, c=c, W=W: t.matmul(
                sx[par][0:1, 0:W], onescol_b[:, 0:1], xb[par][:, c * W:c * W + W],
                start=(c == 0), stop=(c == DC - 1)),
                waits=(pe_w if c == 0 else ()), inc=("pe" if c == DC - 1 else None))
        for c in range(DC):
            P.emit("tensor", lambda t, par=par, c=c, W=W: t.matmul(
                sxx[par][0:1, 0:W], onescol_b[:, 0:1], sq[par][:, c * W:c * W + W],
                start=(c == 0), stop=(c == DC - 1)),
                inc=("pe" if c == DC - 1 else None))
        pe_stats = P.counts["pe"]
        P.emit("vector", lambda v, par=par, W=W: v.tensor_scalar(
            rows[0:1, 0:W], sx[par][0:1, 0:W], 1.0 / DM, None, OP.mult),
            waits=[("pe", pe_stats)], inc="dve")
        P.emit("vector", lambda v, par=par, W=W: v.tensor_scalar(
            rows[0:1, TT:TT + W], sxx[par][0:1, 0:W], 1.0 / DM, None, OP.mult), inc="dve")
        P.emit("vector", lambda v, W=W: v.tensor_mul(
            rows[0:1, 2 * TT:2 * TT + W], rows[0:1, 0:W], rows[0:1, 0:W]), inc="dve")
        dve_fin = P.emit("vector", lambda v, W=W: v.tensor_sub(
            rows[0:1, 2 * TT:2 * TT + W], rows[0:1, TT:TT + W], rows[0:1, 2 * TT:2 * TT + W]),
            inc="dve")
        fc = P.sbufs["c_fconst"]
        P.emit("scalar", lambda s, W=W, fc=fc: s.activation(
            rows[0:1, TT:TT + W], rows[0:1, 2 * TT:2 * TT + W], AF.Ln, bias=fc[0:1, 1:2]),
            waits=[("dve", dve_fin)], inc="act")
        act_r = P.emit("scalar", lambda s, W=W: s.activation(
            rows[0:1, 2 * TT:2 * TT + W], rows[0:1, TT:TT + W], AF.Exp, scale=-0.5), inc="act")
        P.emit("tensor", lambda t, par=par, W=W: t.matmul(
            mubc[par][:, 0:W], onesrow_f[0:1, :], rows[0:1, 0:W], start=True, stop=True),
            waits=[("act", act_r), P.now("dve")], inc="pe")
        pe_bc = P.emit("tensor", lambda t, par=par, W=W: t.matmul(
            rbc[par][:, 0:W], onesrow_f[0:1, :], rows[0:1, 2 * TT:2 * TT + W],
            start=True, stop=True), inc="pe")
        ew = list(extra_waits(ti)) if extra_waits else []
        for c in range(DC):
            P.emit("vector", lambda v, par=par, c=c, W=W: v.tensor_sub(
                sub_s[:, 0:W], xb[par][:, c * W:c * W + W], mubc[par][:, 0:W]),
                waits=(([("pe", pe_bc)] + ew) if c == 0 else ()), inc="dve")
            dst = hln_write(c, t0, W)
            if su_col is None:
                P.emit("vector", lambda v, dst=dst, par=par, W=W: v.tensor_mul(
                    dst, sub_s[:, 0:W], rbc[par][:, 0:W]), inc="dve")
            else:
                P.emit("vector", lambda v, dst=dst, par=par, c=c, W=W: v.scalar_tensor_tensor(
                    dst, sub_s[:, 0:W], su_col[:, c:c + 1], rbc[par][:, 0:W],
                    OP.mult, OP.mult), inc="dve")
        if tile_done is not None:
            tile_done(ti, t0, W)


def emit_ffn0(P, t, xin, xout, tag, tiles=TILES_ALL):
    A = Carve(P.sbufs["arA"], ARENA_A)
    B_ = Carve(P.sbufs["arB"], ARENA_B)
    H = Carve(P.sbufs["arH"], ARENA_H)
    gsc, usc, dsc = P.sbufs["col_l0g"], P.sbufs["col_l0u"], P.sbufs["col_l0d"]

    hln = Carve(P.sbufs["arH"], ARENA_H).take(DC * NTH)

    def hw(c, t0, W):
        return hln[:, c * NTH + t0: c * NTH + t0 + W]

    emit_ln(P, xin, hw, tiles=tiles, tag=f"{tag}a", ar1=A, ar2=B_)
    P.barrier()

    A = Carve(P.sbufs["arA"], ARENA_A)
    B_ = Carve(P.sbufs["arB"], ARENA_B)
    wg = [A.take(DC * TT), B_.take(DC * TT)]
    wu = [A.take(DC * TT), B_.take(DC * TT)]
    sg_sb = [A.take(TT), A.take(TT)]
    h_sb = [A.take(TT) for _ in range(4)]
    hmid = t["hmid"]
    wgT, wuT = t["L0_ffn_gate_wT"], t["L0_ffn_up_wT"]

    bank_free = {}
    slot_free = {}
    hs_free = {}
    sg_free = {}
    for ob in range(16):
        wpar = ob % 2
        dsem = f"D_w{wpar}"
        for (wt, dst) in ((wgT, wg[wpar]), (wuT, wu[wpar])):
            ld = P.dma("gpsimd", dst[:, :].rearrange("p (c w) -> p c w", c=DC),
                       wt.ap()[:, ob * TT:(ob + 1) * TT].rearrange("(c p) w -> p c w", p=128),
                       dsem, waits=[("pe", slot_free.get(wpar, 0))])
        sh = None
        fc = P.sbufs["c_fconst"]
        for dst in (wg[wpar], wu[wpar]):
            sh = P.emit("scalar", lambda sc, d=dst, fc=fc: sc.activation(
                d[:, :], d[:, :], AF.Identity, bias=fc[:, 0:1]),
                waits=[(dsem, ld)], inc="act")
        for ti, (t0, W) in enumerate(tiles):
            for oc in range(4):
                bsel = (ti * 4 + oc) % 2
                bg, bu = P.bank(bsel).ap(), P.bank(2 + bsel).ap()
                og = ob * 4 + oc
                gw = wg[wpar][:, :].rearrange("p (c w) -> p c w", c=DC)
                uw = wu[wpar][:, :].rearrange("p (c w) -> p c w", c=DC)
                mm_group(P, bg[:, 0:W],
                         [(gw[:, c, oc * 128:(oc + 1) * 128], hw(c, t0, W)) for c in range(DC)],
                         waits=[("act", sh), bank_free.get(bsel, ("pe", 0))])
                pe_done = mm_group(P, bu[:, 0:W],
                                   [(uw[:, c, oc * 128:(oc + 1) * 128], hw(c, t0, W)) for c in range(DC)],
                                   waits=[bank_free.get(2 + bsel, ("pe", 0))])
                sgp = (ti * 4 + oc) % 2
                a = P.emit("scalar", lambda s, bg=bg, og=og, W=W, sgp=sgp: s.activation(
                    sg_sb[sgp][:, 0:W], bg[:, 0:W], AF.Silu, scale=gsc[:, og:og + 1]),
                    waits=[("pe", pe_done), ("dve", sg_free.get(sgp, 0))], inc="act")
                bank_free[bsel] = ("act", a)
                hp = (ti * 4 + oc) % 4
                d = P.emit("vector", lambda v, bu=bu, og=og, W=W, sgp=sgp, hp=hp: v.scalar_tensor_tensor(
                    h_sb[hp][:, 0:W], sg_sb[sgp][:, 0:W], usc[:, og:og + 1], bu[:, 0:W],
                    OP.mult, OP.mult),
                    waits=[("act", a), (f"D_h{hp}", hs_free.get(hp, 0))], inc="dve")
                bank_free[2 + bsel] = ("dve", d)
                sg_free[sgp] = d
                st = P.dma("sync", hmid.ap()[og * 128:(og + 1) * 128, t0:t0 + W],
                           h_sb[hp][:, 0:W], f"D_h{hp}", waits=[("dve", d)])
                hs_free[hp] = st
        slot_free[wpar] = P.counts["pe"]
    P.barrier()

    A = Carve(P.sbufs["arA"], ARENA_A)
    B_ = Carve(P.sbufs["arB"], ARENA_B)
    H = Carve(P.sbufs["arH"], ARENA_H)
    OBW = 256
    wd = [A.take(64 * OBW), B_.take(64 * OBW)]
    hq = [H.take(DC * TT), H.take(DC * TT)]
    xi = [H.take(4 * TT, F32)]
    xo = [H.take(4 * TT, F32)]
    wdT = t["L0_ffn_down_wT"]
    fc = P.sbufs["c_fconst"]

    bank_free = {}
    slot_free = {}
    xo_free = {}
    xi_free = {}
    hq_free = {}
    for obp in range(4):
        sh = 0
        for half in range(2):
            ob = obp * 2 + half
            dsem = f"D_wd{half}"
            ld = P.dma("gpsimd", wd[half][:, :].rearrange("p (c w) -> p c w", c=64),
                       wdT.ap()[:, ob * OBW:(ob + 1) * OBW].rearrange("(c p) w -> p c w", p=128),
                       dsem, waits=[("pe", slot_free.get(half, 0))])
            sh = P.emit("scalar", lambda sc, d=wd[half], fc=fc: sc.activation(
                d[:, :], d[:, :], AF.Identity, bias=fc[:, 0:1]),
                waits=[(dsem, ld)], inc="act")
        for ti, (t0, W) in enumerate(tiles):
            tb = (ti % 2) * 4
            hpar = ti % 2
            xsem = f"D_xi0"
            xl = P.dma("gpsimd", xi[0][:, 0:4 * W].rearrange("p (c w) -> p c w", c=4),
                       xin.ap()[obp * 512:(obp + 1) * 512, t0:t0 + W].rearrange("(c p) w -> p c w", p=128),
                       xsem, waits=[("dve", xi_free.get(0, 0))])
            for kq in range(4):
                hb = kq % 2
                hsem = f"D_hq{hb}"
                hl = P.dma("gpsimd", hq[hb][:, 0:DC * W].rearrange("p (c w) -> p c w", c=DC),
                           hmid.ap()[kq * 2048:(kq + 1) * 2048, t0:t0 + W].rearrange("(c p) w -> p c w", p=128),
                           hsem, waits=[("pe", hq_free.get(hb, 0))])
                hqv = hq[hb][:, 0:DC * W].rearrange("p (c w) -> p c w", c=DC)
                for oc in range(4):
                    wdv = wd[oc // 2][:, :].rearrange("p (c w) -> p c w", c=64)
                    occ = oc % 2
                    wts = [(hsem, hl)]
                    if kq == 0:
                        wts += [("act", sh), bank_free.get(tb + oc, ("pe", 0))]
                    for c in range(DC):
                        gk = kq * DC + c
                        P.emit("tensor", lambda t_, gk=gk, occ=occ, c=c, W=W, wdv=wdv, hqv=hqv, tb=tb, oc=oc:
                               t_.matmul(P.bank(tb + oc).ap()[:, 0:W],
                                         wdv[:, gk, occ * 128:(occ + 1) * 128], hqv[:, c, :],
                                         start=(gk == 0), stop=(gk == 63)),
                               waits=(wts if c == 0 else ()),
                               inc=("pe" if c == DC - 1 else None))
                hq_free[hb] = P.counts["pe"]
            pe_done = P.counts["pe"]
            for oc in range(4):
                og = obp * 4 + oc
                d = P.emit("vector", lambda v, oc=oc, og=og, W=W, tb=tb: v.scalar_tensor_tensor(
                    xo[0][:, oc * W:oc * W + W], P.bank(tb + oc).ap()[:, 0:W], dsc[:, og:og + 1],
                    xi[0][:, oc * W:oc * W + W], OP.mult, OP.add),
                    waits=[("pe", pe_done), (xsem, xl), (f"D_xo0", xo_free.get(0, 0))],
                    inc="dve")
                bank_free[tb + oc] = ("dve", d)
            xi_free[0] = P.counts["dve"]
            st = P.dma("sync", xout.ap()[obp * 512:(obp + 1) * 512, t0:t0 + W].rearrange("(c p) w -> p c w", p=128),
                       xo[0][:, 0:4 * W].rearrange("p (c w) -> p c w", c=4),
                       f"D_xo0", waits=[P.now("dve")])
            xo_free[0] = st
        slot_free[0] = P.counts["pe"]
        slot_free[1] = P.counts["pe"]
    P.barrier()


ARENA_A = 24576
ARENA_B = 24576
ARENA_H = 45184


def alloc_arenas(P):
    P.sbuf("arA", [128, ARENA_A], BF16)
    P.sbuf("arB", [128, ARENA_B], BF16)
    P.sbuf("arH", [128, ARENA_H], BF16)


def _hadamard(n):
    H = np.array([[1.0]], dtype=np.float64)
    while H.shape[0] < n:
        H = np.block([[H, H], [H, -H]])
    return H


def host_constants():
    c = {}
    c["had128"] = _hadamard(128).astype(BF)
    c["id128b"] = np.eye(128, dtype=np.float32).astype(BF)
    c["negid128b"] = (-np.eye(128, dtype=np.float32)).astype(BF)
    rs = np.zeros((64, 64 * 64), dtype=np.float32)
    for h in range(64):
        rs[h, h * 64:(h + 1) * 64] = 1.0
    c["rowsel"] = rs.astype(BF)
    c["onescol_b"] = np.ones((128, 1), dtype=np.float32).astype(BF)
    c["onesrow_f"] = np.ones((1, 128), dtype=np.float32)
    c["id128f"] = np.eye(128, dtype=np.float32)
    c["triu64"] = np.triu(np.ones((64, 64), dtype=np.float32))
    fc = np.zeros((128, 4), np.float32)
    fc[:, 0] = -128.0
    fc[:, 1] = 1e-5
    fc[:, 2] = 1.0
    fc[:, 3] = 0.5
    c["fconst"] = fc
    return c


def shard_x(x):
    shards = []
    for k in range(NCORES):
        b, hf = k // 2, k % 2
        main = x[b, hf * NT:(hf + 1) * NT]
        if hf == 0:
            halo = np.zeros((HALO, DM), x.dtype)
        else:
            halo = x[b, NT - HALO:NT]
        xt = np.concatenate([halo, main], axis=0)
        shards.append(np.ascontiguousarray(xt.T))
    return shards


def host_prep_common(inputs):
    d = {}
    d.update(host_constants())
    d["oddcol"] = [np.full((128, 1), float(k % 2), np.float32) for k in range(NCORES)]
    d["L0_ffn_gate_wT"] = np.ascontiguousarray(inputs["L0_ffn_gate_w"].T)
    d["L0_ffn_up_wT"] = np.ascontiguousarray(inputs["L0_ffn_up_w"].T)
    d["L0_ffn_down_wT"] = np.ascontiguousarray(inputs["L0_ffn_down_w"].T)
    d["L0_ffn_gate_scale"] = inputs["L0_ffn_gate_scale"]
    d["L0_ffn_up_scale"] = inputs["L0_ffn_up_scale"]
    d["L0_ffn_down_scale"] = inputs["L0_ffn_down_scale"]
    return d


def declare_ffn0_tensors(nc, kind_x="ExternalInput", kind_y="ExternalOutput"):
    t = {}
    for nm, shape, dt in CONST_SPECS:
        t[nm] = nc.dram_tensor(nm, shape, dt, kind="ExternalInput")
    t["L0_ffn_gate_wT"] = nc.dram_tensor("L0_ffn_gate_wT", [DM, INTER], I32, kind="ExternalInput")
    t["L0_ffn_up_wT"] = nc.dram_tensor("L0_ffn_up_wT", [DM, INTER], I32, kind="ExternalInput")
    t["L0_ffn_down_wT"] = nc.dram_tensor("L0_ffn_down_wT", [INTER, DM], I32, kind="ExternalInput")
    t["L0_ffn_gate_scale"] = nc.dram_tensor("L0_ffn_gate_scale", [INTER], F32, kind="ExternalInput")
    t["L0_ffn_up_scale"] = nc.dram_tensor("L0_ffn_up_scale", [INTER], F32, kind="ExternalInput")
    t["L0_ffn_down_scale"] = nc.dram_tensor("L0_ffn_down_scale", [DM], F32, kind="ExternalInput")
    t["xin"] = nc.dram_tensor("xin", [DM, NTH], F32, kind=kind_x)
    t["xout"] = nc.dram_tensor("xout", [DM, NTH], F32, kind=kind_y)
    t["hmid"] = nc.dram_tensor("hmid", [INTER, NTH], F32 if False else BF16)
    return t


def build_ffn0_program():
    from contextlib import ExitStack
    nc = bass.Bass("TRN2", target_bir_lowering=False)
    ctx = ExitStack()
    t = declare_ffn0_tensors(nc)
    P = Prog(nc, ctx)
    with ctx:
        alloc_arenas(P)
        emit_consts(P, t)
        P.barrier()
        build_colset(P, "l0g", t["L0_ffn_gate_scale"], INTER)
        build_colset(P, "l0u", t["L0_ffn_up_scale"], INTER)
        build_colset(P, "l0d", t["L0_ffn_down_scale"], DM, post_scale=0.5)
        P.barrier()
        emit_ffn0(P, t, t["xin"], t["xout"], "f0")
        P.finish()
    return nc


def ffn0_in_map(prep, xshard, core):
    m = {nm: np.asarray(prep[nm]) for nm, _, _ in CONST_SPECS if nm != "oddcol"}
    m["oddcol"] = prep["oddcol"][core]
    for nm in ["L0_ffn_gate_wT", "L0_ffn_up_wT", "L0_ffn_down_wT",
               "L0_ffn_gate_scale", "L0_ffn_up_scale", "L0_ffn_down_scale"]:
        m[nm] = prep[nm]
    m["xin"] = xshard
    return m


def _gather(res):
    out = np.empty((4, 4096, 2048), np.float32)
    for k in range(NCORES):
        b, hf = k // 2, k % 2
        out[b, hf * NT:(hf + 1) * NT] = res.results[k]["xout"][:, HALO:].T
    return out


def kernel_unfused(**inputs):
    x = inputs["L0_attn_x"]
    prep = host_prep_common(inputs)
    prepm = host_prep_mamba(inputs)
    prepf1 = host_prep_ffn1(inputs)
    prepa1 = host_prep_attn1(inputs)

    nc_ffn0 = build_ffn0_program()
    nc_mamba = build_mamba_program()
    nc_ffn1 = build_ffn1_program(fold=True)
    nc_attn1 = build_attn1_program()

    shards = shard_x(x)
    res = run_bass_kernel_spmd(nc_ffn0, [ffn0_in_map(prep, shards[k], k) for k in range(NCORES)],
                               core_ids=list(range(NCORES)))
    x = _gather(res)
    shards = shard_x(x)
    res = run_bass_kernel_spmd(nc_mamba, [mamba_in_map(prep, prepm, shards[k], k) for k in range(NCORES)],
                               core_ids=list(range(NCORES)))
    x = _gather(res)
    shards = shard_x(x)
    res = run_bass_kernel_spmd(nc_ffn0, [ffn0_in_map(prep, shards[k], k) for k in range(NCORES)],
                               core_ids=list(range(NCORES)))
    x = _gather(res)
    shards = shard_x(x)
    res = run_bass_kernel_spmd(nc_ffn1, [generic_in_map(prep, prepf1, shards[k], k) for k in range(NCORES)],
                               core_ids=list(range(NCORES)))
    x = _gather(res)
    shards = shard_x(x)
    res = run_bass_kernel_spmd(nc_attn1, [generic_in_map(prep, prepa1, shards[k], k) for k in range(NCORES)],
                               core_ids=list(range(NCORES)))
    x = _gather(res)
    shards = shard_x(x)
    res = run_bass_kernel_spmd(nc_ffn1, [generic_in_map(prep, prepf1, shards[k], k) for k in range(NCORES)],
                               core_ids=list(range(NCORES)))
    return _gather(res)


def emit_mamba(P, t, xin, xout, tag="m", upto="m7"):
    A = Carve(P.sbufs["arA"], ARENA_A)
    B_ = Carve(P.sbufs["arB"], ARENA_B)
    H = Carve(P.sbufs["arH"], ARENA_H)
    hst = [H.take(DC * TT), H.take(DC * TT)]
    hst_free = {}
    hln2 = t["hln2"]

    def hw0(c, t0, W):
        par = (t0 // TT) % 2
        return hst[par][:, c * W: c * W + W]

    def flush0(ti, t0, W):
        par = ti % 2
        st = P.dma("sync", hln2.ap()[:, t0:t0 + W].rearrange("(c p) w -> p c w", p=128),
                   hst[par][:, 0:DC * W].rearrange("p (c w) -> p c w", c=DC),
                   f"D_hst{par}", waits=[P.now("dve")])
        hst_free[par] = st

    emit_ln(P, xin, hw0, tiles=TILES_ALL, tag=f"{tag}0", ar1=A, ar2=B_, tile_done=flush0,
            extra_waits=lambda ti: [(f"D_hst{ti % 2}", hst_free.get(ti % 2, 0))])
    P.barrier()
    if upto == "m0":
        return

    A = Carve(P.sbufs["arA"], ARENA_A)
    B_ = Carve(P.sbufs["arB"], ARENA_B)
    H = Carve(P.sbufs["arH"], ARENA_H)
    w_sl = [A.take(DC * TT), A.take(DC * TT)]
    hti = [B_.take(DC * TT), B_.take(DC * TT)]
    rb = [H.take(4 * NTH), H.take(4 * NTH)]
    vt = H.take(NT, F32)
    ut = [H.take(NT), H.take(NT)]
    zst = [B_.take(TT), B_.take(TT)]
    wT = t["L0_attn_in_proj_wT"]
    cw = [P.sbufs[f"col_cw{k}"] for k in range(4)]
    cb = P.sbufs["col_cb"]
    hln_free = {}
    wslot_free = {}
    rb_free = {}
    zst_free = {}
    ut_free = {}
    bank_free = {}
    for ob in range(16):
        wpar = ob % 2
        dsem = f"D_w{wpar}"
        ld = P.dma("gpsimd", w_sl[wpar][:, :].rearrange("p (c w) -> p c w", c=DC),
                   wT.ap()[:, ob * TT:(ob + 1) * TT].rearrange("(c p) w -> p c w", p=128),
                   dsem, waits=[("pe", wslot_free.get(wpar, 0))])
        is_x = ob < 8
        tiles = TILES_ALL if is_x else TILES_MAIN
        for ti, (t0, W) in enumerate(tiles):
            hpar = ti % 2
            hsem = f"D_h{hpar}"
            hl = P.dma("sync", hti[hpar][:, 0:DC * W].rearrange("p (c w) -> p c w", c=DC),
                       hln2.ap()[:, t0:t0 + W].rearrange("(c p) w -> p c w", p=128),
                       hsem, waits=[("pe", hln_free.get(hpar, 0))])
            wv = w_sl[wpar][:, :].rearrange("p (c w) -> p c w", c=DC)
            hv = hti[hpar][:, 0:DC * W].rearrange("p (c w) -> p c w", c=DC)
            for oc in range(4):
                bk = (ti * 4 + oc) % 2 if is_x else 2 + (ti * 4 + oc) % 2
                wts = [(dsem, ld), (hsem, hl), bank_free.get(bk, ("dve", 0))]
                pe_done = mm_group(P, P.bank(bk).ap()[:, 0:W],
                                   [(wv[:, c, oc * 128:(oc + 1) * 128], hv[:, c, :]) for c in range(DC)],
                                   waits=wts)
                if is_x:
                    d = P.emit("vector", lambda v, bk=bk, oc=oc, t0=t0, W=W, wpar=wpar: v.tensor_copy(
                        rb[wpar][:, oc * NTH + t0: oc * NTH + t0 + W], P.bank(bk).ap()[:, 0:W]),
                        waits=[("pe", pe_done)], inc="dve")
                else:
                    zp = (ti * 4 + oc) % 2
                    d = P.emit("vector", lambda v, bk=bk, zp=zp, W=W: v.tensor_copy(
                        zst[zp][:, 0:W], P.bank(bk).ap()[:, 0:W]),
                        waits=[("pe", pe_done), (f"D_z{zp}", zst_free.get(zp, 0))], inc="dve")
                    gz = (ob - 8) * 4 + oc
                    st = P.dma("sync", t["z_s"].ap()[gz * 128:(gz + 1) * 128, t0 - HALO:t0 - HALO + W],
                               zst[zp][:, 0:W], f"D_z{zp}", waits=[("dve", d)])
                    zst_free[zp] = st
                bank_free[bk] = ("dve", d)
            hln_free[hpar] = P.counts["pe"]
        if is_x:
            dve_rb = P.counts["dve"]
            for oc in range(4):
                gc = ob * 4 + oc
                up = gc % 2
                r = rb[wpar]
                P.emit("vector", lambda v, r=r, oc=oc, gc=gc: v.tensor_scalar(
                    vt[:, :], r[:, oc * NTH + 1: oc * NTH + 1 + NT], cw[0][:, gc:gc + 1], None, OP.mult),
                    waits=([("dve", dve_rb)] if oc == 0 else []) + [P.now("act")], inc="dve")
                for k in (1, 2):
                    P.emit("vector", lambda v, r=r, oc=oc, gc=gc, k=k: v.scalar_tensor_tensor(
                        vt[:, :], r[:, oc * NTH + 1 + k: oc * NTH + 1 + k + NT], cw[k][:, gc:gc + 1],
                        vt[:, :], OP.mult, OP.add), inc="dve")
                d3 = P.emit("vector", lambda v, r=r, oc=oc, gc=gc: v.scalar_tensor_tensor(
                    vt[:, :], r[:, oc * NTH + 4: oc * NTH + 4 + NT], cw[3][:, gc:gc + 1],
                    vt[:, :], OP.mult, OP.add), inc="dve")
                a = P.emit("scalar", lambda s, gc=gc, up=up: s.activation(
                    ut[up][:, :], vt[:, :], AF.Silu, bias=cb[:, gc:gc + 1]),
                    waits=[("dve", d3), (f"D_u{up}", ut_free.get(up, 0))], inc="act")
                st = P.dma("sync", t["u_s"].ap()[gc * 128:(gc + 1) * 128, :],
                           ut[up][:, :], f"D_u{up}", waits=[("act", a)])
                ut_free[up] = st
        wslot_free[wpar] = P.counts["pe"]
    P.barrier()
    if upto == "m1":
        return

    A = Carve(P.sbufs["arA"], ARENA_A)
    B_ = Carve(P.sbufs["arB"], ARENA_B)
    H = Carve(P.sbufs["arH"], ARENA_H)
    xpT = t["L0_attn_x_proj_wT"]
    wsl = A.take(32 * TT)
    w_sl = [wsl, wsl]
    uc = [B_.take(32 * 128), B_.take(32 * 128)]
    xst = [B_.take(TT), B_.take(TT)]
    wslot_free = {}
    uc_free = {}
    xst_free = {}
    bank_free = {}
    for on in range(8):
        wpar = 0
        dsem = f"D_xw{wpar}"
        ld = P.dma("gpsimd", w_sl[wpar][:, :].rearrange("p (c w) -> p c w", c=32),
                   xpT.ap()[:, on * TT:(on + 1) * TT].rearrange("(c p) w -> p c w", p=128),
                   dsem, waits=[("pe", P.counts["pe"])])
        for tg in range(16):
            up = tg % 2
            usem = f"D_uc{up}"
            ul = P.dma("gpsimd", uc[up][:, :].rearrange("p (c w) -> p c w", c=32),
                       t["u_s"].ap()[:, tg * 128:(tg + 1) * 128].rearrange("(c p) w -> p c w", p=128),
                       usem, waits=[("pe", uc_free.get(up, 0))])
            bk = tg % 2
            wv = w_sl[wpar][:, :].rearrange("p (c w) -> p c w", c=32)
            uv = uc[up][:, :].rearrange("p (c w) -> p c w", c=32)
            pe_done = mm_group(P, P.bank(bk).ap()[:, :],
                               [(uv[:, c, :], wv[:, c, :]) for c in range(32)],
                               waits=[(dsem, ld), (usem, ul), bank_free.get(bk, ("dve", 0))])
            uc_free[up] = pe_done
            sp = tg % 2
            d = P.emit("vector", lambda v, bk=bk, sp=sp: v.tensor_copy(
                xst[sp][:, :], P.bank(bk).ap()[:, :]),
                waits=[("pe", pe_done), (f"D_xs{sp}", xst_free.get(sp, 0))], inc="dve")
            bank_free[bk] = ("dve", d)
            st = P.dma("sync", t["xtm_s"].ap()[tg * 128:(tg + 1) * 128, on * TT:(on + 1) * TT],
                       xst[sp][:, :], f"D_xs{sp}", waits=[("dve", d)])
            xst_free[sp] = st
        wslot_free[wpar] = P.counts["pe"]
    P.barrier()
    Hp = Carve(P.sbufs["arH"], ARENA_H)
    sl = {
        "bfm": Hp.take(NT),
        "cfm": Hp.take(NT),
        "dt2r": Hp.take(NT, F32),
        "acs": Hp.take(NT, F32),
        "hi": Hp.take(NT),
        "lo": Hp.take(NT),
        "tm_acs": Hp.take(NT, F32),
        "tm_dt": Hp.take(NT, F32),
        "tm_decay": Hp.take(NT, F32),
        "btm": Hp.take(2 * NT),
        "S": Hp.take(2 * NT, F32),
        "dbc": Hp.take(2 * NT),
    }
    bfm, cfm = sl["bfm"], sl["cfm"]
    B2 = Carve(P.sbufs["arB"], ARENA_B)
    A2 = Carve(P.sbufs["arA"], ARENA_A)
    w2 = B2.take(32 * 320)
    ut2 = A2.take(32 * TT)
    dsem2 = f"D_xw2"
    ld2 = P.dma("gpsimd", w2[:, :].rearrange("p (c w) -> p c w", c=32),
                xpT.ap()[:, D_INNER:D_INNER + 320].rearrange("(c p) w -> p c w", p=128),
                dsem2)
    ut2_free = {}
    for ti, (t0, W) in enumerate(TILES_MAIN):
        usem = f"D_u2"
        ul = P.dma("gpsimd", ut2[:, 0:32 * W].rearrange("p (c w) -> p c w", c=32),
                   t["u_s"].ap()[:, t0 - HALO:t0 - HALO + W].rearrange("(c p) w -> p c w", p=128),
                   usem, waits=[("pe", ut2_free.get(0, 0))])
        wv = w2[:, :].rearrange("p (c w) -> p c w", c=32)
        uv = ut2[:, 0:32 * W].rearrange("p (c w) -> p c w", c=32)
        m0 = t0 - HALO
        for (bk, o0, o1, dstf) in ((4, 0, 64, None), (5, 64, 192, bfm), (6, 192, 320, cfm)):
            pe_done = mm_group(P, P.bank(bk).ap()[0:(o1 - o0), 0:W],
                               [(wv[:, c, o0:o1], uv[:, c, :]) for c in range(32)],
                               waits=[(dsem2, ld2), (usem, ul), bank_free.get(bk, ("dve", 0))])
            if dstf is None:
                d = P.emit("vector", lambda v, m0=m0, W=W: v.tensor_copy(
                    sl["dt2r"][0:64, m0:m0 + W], P.bank(4).ap()[0:64, 0:W]),
                    waits=[("pe", pe_done)], inc="dve")
            else:
                d = P.emit("vector", lambda v, dstf=dstf, bk=bk, m0=m0, W=W: v.tensor_copy(
                    dstf[:, m0:m0 + W], P.bank(bk).ap()[0:128, 0:W]),
                    waits=[("pe", pe_done)], inc="dve")
            bank_free[bk] = ("dve", d)
        ut2_free[0] = P.counts["pe"]
    P.barrier()
    if upto == "m2":
        return
    emit_mamba_ssd(P, t, xin, xout, tag, sl, upto)


def emit_mamba_ssd(P, t, xin, xout, tag, sl, upto="m7"):
    nc = P.nc
    fc = P.sbufs["c_fconst"]
    id128f = P.sbufs["c_id128f"]
    id128b = P.sbufs["c_id128b"]
    onesrow_f = P.sbufs["c_onesrow_f"]
    rowsel = P.sbufs["c_rowsel"]
    triu = P.sbufs["c_triu64"]
    oddcol = P.sbufs["c_oddcol"]
    B3 = Carve(P.sbufs["arB"], ARENA_B)
    sp_v = B3.take(TT, F32)[0:64, :]
    sp_b = B3.take(TT, F32)[0:64, :]
    tmpb = B3.take(2 * NT)
    rbc4 = B3.take(4096, F32)
    dt_bf = rbc4.bitcast(BF16)[0:64, 0:NT]
    lrow = B3.take(NCHUNK * 64, F32)[0:1, :]
    zt64 = P.sbuf("m_zt64", [64, 64], F32)
    spd = P.sbuf("m_spd", [64, 64], F32)
    spe = P.sbuf("m_spe", [64, 64], F32)
    lbc = P.sbuf("m_lbc", [128, 64], F32)
    dtb_col = P.sbuf("m_dtb", [64, 1], F32)
    negA = P.sbuf("m_negA", [64, 1], F32)
    dtpT = P.sbuf("m_dtpT", [64, 64], BF16)
    drow = P.sbuf("m_drow", [1, 64], F32)
    gtm = P.sbuf("m_gtm", [64, 64], BF16)
    e1s = P.sbuf("m_e1", [64, 64], F32)
    e2s = P.sbuf("m_e2", [64, 64], F32)
    lrow_s = P.sbuf("m_lrs", [1, 64], F32)

    acs, dt2r = sl["acs"], sl["dt2r"]
    hi, lo = sl["hi"], sl["lo"]
    tm_acs, tm_dt = sl["tm_acs"], sl["tm_dt"]
    tm_decay = sl["tm_decay"]
    tm_eacs = P.sbuf("m_tme", [64, 64], F32)
    btm, S, dbc = sl["btm"], sl["S"], sl["dbc"]
    bfm, cfm = sl["bfm"], sl["cfm"]

    l1 = P.dma("gpsimd", dtpT[:, :], t["L0_attn_dt_proj_wT"].ap(), "D_m3")
    l2 = P.dma("gpsimd", dtb_col[:, :], t["L0_attn_dt_proj_b"].ap().rearrange("(a b) -> a b", b=1), "D_m3")
    l3 = P.dma("gpsimd", negA[:, :], t["L0_attn_A_log"].ap().rearrange("(a b) -> a b", b=1), "D_m3")
    l4 = P.dma("gpsimd", drow[:, :], t["L0_attn_D"].ap().rearrange("(a b) -> b a", b=1), "D_m3")
    a1 = P.emit("scalar", lambda s: s.activation(negA[:, :], negA[:, :], AF.Exp),
                waits=[("D_m3", l4)], inc="act")
    n1 = P.emit("vector", lambda v: v.tensor_scalar(negA[:, :], negA[:, :], -1.0, None, OP.mult),
                waits=[("act", a1)], inc="dve")
    P.emit("vector", lambda v: v.memset(zt64[:, :], 0.0), inc="dve")
    pe1 = P.emit("tensor", lambda tt: tt.matmul(
        P.bank(7).ap()[:, 0:64], onesrow_f[0:1, :], drow[0:1, :],
        start=True, stop=True), waits=[("D_m3", l4)], inc="pe")
    dsm = P.sbuf("m_dsm", [128, 64], F32)
    dd1 = P.emit("vector", lambda v: v.tensor_copy(dsm[:, :], P.bank(7).ap()[:, 0:64]),
                 waits=[("pe", pe1)], inc="dve")
    P.emit("vector", lambda v: v.tensor_copy(
        dbc[:, :].rearrange("p (h q) -> p h q", h=64),
        dsm[:, :].unsqueeze(2).broadcast_to([128, 64, 64])), inc="dve")
    c1 = P.emit("vector", lambda v: v.tensor_copy(dt_bf[:, :], dt2r[0:64, 0:NT]), inc="dve")
    for ti in range(4):
        t0 = ti * TT
        pe_done = P.emit("tensor", lambda tt, t0=t0: tt.matmul(
            P.bank(0).ap()[0:64, :], dtpT[:, :], dt_bf[:, t0:t0 + TT], start=True, stop=True),
            waits=[("D_m3", l2), ("dve", c1), P.now("dve")], inc="pe")
        v1 = P.emit("vector", lambda v, t0=t0: v.tensor_scalar(
            sp_v[:, :], P.bank(0).ap()[0:64, :], dtb_col[:, 0:1], None, OP.add),
            waits=[("pe", pe_done), P.now("act")], inc="dve")
        a2 = P.emit("scalar", lambda s: s.activation(sp_b[:, :], sp_v[:, :], AF.Abs),
                    waits=[("dve", v1)], inc="act")
        a3 = P.emit("scalar", lambda s: s.activation(sp_b[:, :], sp_b[:, :], AF.Exp, scale=-1.0),
                    inc="act")
        a4 = P.emit("scalar", lambda s: s.activation(sp_b[:, :], sp_b[:, :], AF.Ln, bias=fc[0:64, 2:3]),
                    inc="act")
        v2 = P.emit("vector", lambda v: v.tensor_scalar(sp_v[:, :], sp_v[:, :], 0.0, None, OP.max),
                    waits=[("act", a4)], inc="dve")
        v3 = P.emit("vector", lambda v, t0=t0: v.tensor_add(dt2r[0:64, t0:t0 + TT], sp_v[:, :], sp_b[:, :]),
                    inc="dve")
        v4 = P.emit("vector", lambda v, t0=t0: v.tensor_scalar(
            sp_v[:, :], dt2r[0:64, t0:t0 + TT], negA[:, 0:1], None, OP.mult),
            waits=[("dve", n1)], inc="dve")
        for cc in range(TT // CHUNK):
            P.emit("vector", lambda v, t0=t0, cc=cc: v.tensor_tensor_scan(
                acs[0:64, t0 + cc * CHUNK: t0 + (cc + 1) * CHUNK],
                sp_v[:, cc * CHUNK:(cc + 1) * CHUNK], zt64[:, :], 0.0, OP.add, OP.add),
                inc="dve")
    for c in range(NCHUNK):
        cs = slice(c * CHUNK, (c + 1) * CHUNK)
        d0 = P.emit("vector", lambda v, cs=cs, c=c: v.tensor_scalar(
            spd[:, :], acs[0:64, cs], acs[0:64, c * CHUNK + 63:c * CHUNK + 64], None, OP.subtract),
            waits=[P.now("pe")], inc="dve")
        a5 = P.emit("scalar", lambda s: s.activation(spd[:, :], spd[:, :], AF.Exp, scale=-1.0),
                    waits=[("dve", d0)], inc="act")
        a6 = P.emit("scalar", lambda s, cs=cs: s.activation(spe[:, :], acs[0:64, cs], AF.Exp),
                    waits=[P.now("pe")], inc="act")
        pw = [("act", a6), ("dve", P.counts["dve"])]
        tps = [(acs[0:64, cs], tm_acs, 2), (dt2r[0:64, cs], tm_dt, 3),
               (spd[:, :], tm_decay, 4)]
        for (src_ap, dst, bk) in tps:
            pe2 = P.emit("tensor", lambda tt, src_ap=src_ap, bk=bk: tt.transpose(
                P.bank(bk).ap()[0:64, 0:64], src_ap, id128f[0:64, 0:64]),
                waits=(pw if bk == 2 else ()), inc="pe")
            P.emit("vector", lambda v, dst=dst, c=c, bk=bk: v.tensor_copy(
                dst[0:64, c * 64:(c + 1) * 64], P.bank(bk).ap()[0:64, 0:64]),
                waits=[("pe", pe2)], inc="dve")
        pe3 = P.emit("tensor", lambda tt: tt.transpose(
            P.bank(6).ap()[0:1, 0:64], spe[:, 63:64], id128f[0:64, 0:64]),
            inc="pe")
        P.emit("vector", lambda v, c=c: v.tensor_copy(
            lrow[0:1, c * 64:(c + 1) * 64], P.bank(6).ap()[0:1, 0:64]),
            waits=[("pe", pe3)], inc="dve")
        pe4 = P.emit("tensor", lambda tt, cs=cs: tt.transpose(
            P.bank_bf(7)[0:64, 0:128], bfm[:, cs], id128b[:, :]),
            waits=[P.now("dve")], inc="pe")
        P.emit("vector", lambda v, c=c: v.tensor_copy(
            btm[0:64, c * 128:(c + 1) * 128], P.bank_bf(7)[0:64, 0:128]),
            waits=[("pe", pe4)], inc="dve")
    P.barrier()
    if upto == "m3":
        return

    A = Carve(P.sbufs["arA"], ARENA_A)
    xt = A.take(2 * NT)
    xs = A.take(2 * NT)
    ysb = [A.take(2 * NT), A.take(2 * NT)]
    sbf_all = A.take(2 * NT)
    rbc = rbc4
    em_all = B3.take(4096)
    tmp512 = B3.take(512)
    xda = tmpb
    rowd = t["rowd"]

    zS = P.emit("vector", lambda v: v.memset(S[:, :], 0.0), inc="dve")
    P.cm = {}

    def chunk_loop2(main, c):
        cs = slice(c * CHUNK, (c + 1) * CHUNK)
        xsem = f"D_xt{int(main)}"
        xl = P.dma("gpsimd", xt[0:64, :], t["xtm_s"].ap()[c * 64:(c + 1) * 64, :],
                   xsem, waits=[("dve", P.cm.get(("xtc", main), 0)),
                                ("pe", P.cm.get(("xtp", main), 0))])
        pe_l = P.emit("tensor", lambda tt, c=c: tt.matmul(
            P.bank(1).ap()[:, 0:64], onesrow_f[0:1, :], lrow[0:1, c * 64:(c + 1) * 64],
            start=True, stop=True), waits=[P.now("dve")], inc="pe")
        dl = P.emit("vector", lambda v: v.tensor_copy(lbc[:, :], P.bank(1).ap()[:, 0:64]),
                    waits=[("pe", pe_l)], inc="dve")
        dxs = P.emit("vector", lambda v, c=c: v.tensor_mul(
            xs[0:64, :].rearrange("p (h q) -> p h q", h=64),
            xt[0:64, :].rearrange("p (h q) -> p h q", h=64),
            tm_dt[0:64, c * 64:(c + 1) * 64].unsqueeze(2).broadcast_to([64, 64, 64])),
            waits=[(xsem, xl), ("pe", P.cm.get(("xsp", main), 0))], inc="dve")
        dxd = P.emit("vector", lambda v, c=c: v.tensor_mul(
            xda[0:64, :].rearrange("p (h q) -> p h q", h=64),
            xs[0:64, :].rearrange("p (h q) -> p h q", h=64),
            tm_decay[0:64, c * 64:(c + 1) * 64].unsqueeze(2).broadcast_to([64, 64, 64])),
            waits=[("pe", P.cm.get(("xdp",), 0))], inc="dve")
        if main:
            pe_g = P.emit("tensor", lambda tt, cs=cs: tt.matmul(
                P.bank(0).ap()[0:64, 0:64], bfm[:, cs], cfm[:, cs], start=True, stop=True),
                waits=[("dve", P.cm.get(("gtb",), 0))], inc="pe")
            dg = P.emit("vector", lambda v: v.tensor_mul(gtm[:, :], P.bank(0).ap()[0:64, 0:64],
                                                         triu[:, :]),
                        waits=[("pe", pe_g)], inc="dve")
            P.cm[("gtb",)] = dg
            ate = P.emit("scalar", lambda s, c=c: s.activation(
                tm_eacs[:, :], tm_acs[0:64, c * 64:(c + 1) * 64], AF.Exp),
                waits=[P.now("dve")], inc="act")
            rd = rowd.ap()[c % 2, :]
            g1 = P.dma("gpsimd", rd, acs[0:64, cs], "D_rb",
                       waits=[("act", P.cm.get(("rbc",), 0))])
            g2 = P.dma("gpsimd", rbc[0:64, :], bass.AP(t["rowd"], (c % 2) * 4096, [[0, 64], [1, 4096]]),
                       "D_rb", waits=[("D_rb", g1)])
            e1a = P.emit("vector", lambda v, c=c: v.tensor_sub(
                rbc[0:64, :].rearrange("p (h q) -> p h q", h=64),
                rbc[0:64, :].rearrange("p (h q) -> p h q", h=64),
                tm_acs[0:64, c * 64:(c + 1) * 64].unsqueeze(2).broadcast_to([64, 64, 64])),
                waits=[("D_rb", g2)], inc="dve")
            e1b = P.emit("vector", lambda v: v.tensor_scalar(
                rbc[0:64, :], rbc[0:64, :], 0.0, None, OP.min), inc="dve")
            eex = P.emit("scalar", lambda s: s.activation(em_all[0:64, :], rbc[0:64, :], AF.Exp),
                         waits=[("dve", e1b)], inc="act")
            P.cm[("rbc",)] = eex
            emg = P.emit("vector", lambda v: v.tensor_mul(
                em_all[0:64, :].rearrange("p (h q) -> p h q", h=64),
                em_all[0:64, :].rearrange("p (h q) -> p h q", h=64),
                gtm[:, :].unsqueeze(1).broadcast_to([64, 64, 64])),
                waits=[("act", eex), ("dve", dg)], inc="dve")
            dsb = P.emit("vector", lambda v: v.tensor_copy(sbf_all[:, :], S[:, :]),
                         waits=[("pe", P.cm.get(("sbp",), 0))], inc="dve")
            P.cm[("sbp",)] = 0
        for h8 in range(8):
            sb_ = 6 + h8 % 2
            if main:
                yb = 2 + h8 % 2
                ob = 4 + h8 % 2
                pe_o = P.emit("tensor", lambda tt, cs=cs, h8=h8, ob=ob: tt.matmul(
                    P.bank(ob).ap()[0:64, :], cfm[:, cs], sbf_all[:, h8 * 512:(h8 + 1) * 512],
                    start=True, stop=True),
                    waits=[("dve", dsb), ("dve", P.cm.get(("obk", ob), 0))], inc="pe")
                for hh in range(8):
                    h = h8 * 8 + hh
                    w = [("dve", emg)]
                    if hh == 0:
                        w.append(("dve", P.cm.get(("ybk", yb), 0)))
                    P.emit("tensor", lambda tt, h=h, yb=yb, hh=hh: tt.matmul(
                        P.bank(yb).ap()[0:64, hh * 64:(hh + 1) * 64],
                        em_all[0:64, h * 64:(h + 1) * 64], xs[0:64, h * 64:(h + 1) * 64],
                        start=True, stop=True),
                        waits=w, inc=("pe" if hh == 7 else None))
                pyd = P.counts["pe"]
            pe_s = P.emit("tensor", lambda tt, c=c, h8=h8, sb_=sb_: tt.matmul(
                P.bank(sb_).ap()[:, :], btm[0:64, c * 128:(c + 1) * 128],
                xda[0:64, h8 * 512:(h8 + 1) * 512],
                start=True, stop=True),
                waits=[("dve", dxd), ("dve", P.cm.get(("stb", sb_), 0))], inc="pe")
            P.emit("vector", lambda v, h8=h8: v.tensor_mul(
                S[:, h8 * 512:(h8 + 1) * 512].rearrange("p (h q) -> p h q", h=8),
                S[:, h8 * 512:(h8 + 1) * 512].rearrange("p (h q) -> p h q", h=8),
                lbc[:, h8 * 8:(h8 + 1) * 8].unsqueeze(2).broadcast_to([128, 8, 64])),
                waits=[("pe", pe_s), ("dve", dl)], inc="dve")
            d2 = P.emit("vector", lambda v, h8=h8, sb_=sb_: v.tensor_add(
                S[:, h8 * 512:(h8 + 1) * 512], S[:, h8 * 512:(h8 + 1) * 512],
                P.bank(sb_).ap()[:, :]), inc="dve")
            P.cm[("stb", sb_)] = d2
            if main:
                w = [("pe", pyd)]
                if h8 < 2:
                    w.append((f"D_y{c % 2}", P.cm.get(("ysb", c % 2), 0)))
                dv1 = P.emit("vector", lambda v, h8=h8, ob=ob, c=c: v.tensor_mul(
                    tmp512[0:64, :].rearrange("p (h q) -> p h q", h=8),
                    P.bank(ob).ap()[0:64, :].rearrange("p (h q) -> p h q", h=8),
                    tm_eacs[:, h8 * 8:(h8 + 1) * 8].unsqueeze(2).broadcast_to([64, 8, 64])),
                    waits=w, inc="dve")
                dv2 = P.emit("vector", lambda v, h8=h8, yb=yb, c=c: v.tensor_add(
                    ysb[c % 2][0:64, h8 * 512:(h8 + 1) * 512], tmp512[0:64, :],
                    P.bank(yb).ap()[0:64, :]), inc="dve")
                P.cm[("ybk", yb)] = dv2
                P.cm[("obk", ob)] = dv2
        P.cm[("sbp",)] = P.counts["pe"]
        P.cm[("xdp",)] = P.counts["pe"]
        if not main:
            P.cm[("xsp", main)] = P.counts["pe"]
        else:
            P.cm[("xsp", main)] = P.counts["pe"]
        if main:
            dt1 = P.emit("vector", lambda v: v.tensor_mul(
                xda[0:64, :], xt[0:64, :], dbc[0:64, :]),
                waits=[("pe", P.cm.get(("xdp",), 0))], inc="dve")
            dt2_ = P.emit("vector", lambda v, c=c: v.tensor_add(
                ysb[c % 2][0:64, :], ysb[c % 2][0:64, :], xda[0:64, :]), inc="dve")
            st = P.dma("sync", t["y_s"].ap()[c * 64:(c + 1) * 64, :], ysb[c % 2][0:64, :],
                       f"D_y{c % 2}", waits=[("dve", dt2_)])
            P.cm[("ysb", c % 2)] = st
            P.cm[("xtc", main)] = dt2_
        else:
            P.cm[("xtc", main)] = dxd
        P.cm[("xtp", main)] = P.counts["pe"]

    for c in range(NCHUNK):
        chunk_loop2(False, c)
    P.barrier()
    if upto == "m4":
        return
    e1 = P.dma("gpsimd", t["ccin"].ap(), S[:, :], "D_cc")
    cc = P.emit("gpsimd", lambda g: g.collective_compute(
        "AllGather", OP.bypass, replica_groups=[[0, 1], [2, 3], [4, 5], [6, 7]],
        ins=[t["ccin"].ap().opt()], outs=[t["ccout"].ap().opt()]),
        waits=[("D_cc", e1)], inc="cc", is_dma=True)
    e2 = P.dma("gpsimd", S[:, :], t["ccout"].ap()[0:128, :], "D_cc", waits=[("cc", cc)])
    P.emit("vector", lambda v: v.tensor_scalar(S[:, :], S[:, :], oddcol[:, 0:1], None, OP.mult),
           waits=[("D_cc", e2)], inc="dve")
    P.barrier()
    if upto == "m5":
        return
    import os
    nmain = int(os.environ.get("MAMBA_MAIN_CHUNKS", NCHUNK))
    for c in range(nmain):
        chunk_loop2(True, c)
    P.barrier()
    if upto == "m6":
        return
    emit_mamba_out(P, t, xin, xout, tag)


def emit_mamba_out(P, t, xin, xout, tag):
    A = Carve(P.sbufs["arA"], ARENA_A)
    B_ = Carve(P.sbufs["arB"], ARENA_B)
    H = Carve(P.sbufs["arH"], ARENA_H)
    id128b = P.sbufs["c_id128b"]
    ygf = A.take(32 * TT)
    yt = [A.take(2 * NT), A.take(2 * NT)]
    zt = [B_.take(32 * 128), B_.take(32 * 128)]
    zs = B_.take(32 * 128)
    w_sl = [H.take(32 * 256), H.take(32 * 256)]
    xi = [H.take(2 * TT, F32), H.take(2 * TT, F32)]
    xo = [H.take(2 * TT, F32), H.take(2 * TT, F32)]
    woT = t["L0_attn_out_proj_wT"]
    cm = {}
    for tg in range(4):
        for tc in range(4):
            gtok = tg * 4 + tc
            ypar = tc % 2
            ysem = f"D_yt{ypar}"
            yl = P.dma("gpsimd", yt[ypar][0:128, :], t["y_s"].ap()[gtok * 128:(gtok + 1) * 128, :],
                       ysem, waits=[("pe", cm.get(("yt", ypar), 0))])
            zsem = f"D_zt{ypar}"
            zl = P.dma("sync", zt[ypar][:, :].rearrange("p (c w) -> p c w", c=32),
                       t["z_s"].ap()[:, gtok * 128:(gtok + 1) * 128].rearrange("(c p) w -> p c w", p=128),
                       zsem, waits=[("act", cm.get(("zt", ypar), 0))])
            az = P.emit("scalar", lambda s, ypar=ypar: s.activation(
                zs[:, :], zt[ypar][:, :], AF.Silu),
                waits=[(zsem, zl), ("dve", cm.get(("zs",), 0))], inc="act")
            cm[("zt", ypar)] = az
            for fck in range(32):
                pe1 = P.emit("tensor", lambda tt, ypar=ypar, fck=fck: tt.transpose(
                    P.bank_bf(fck % 2)[0:128, 0:128], yt[ypar][0:128, fck * 128:(fck + 1) * 128],
                    id128b[:, :]),
                    waits=([(ysem, yl), ("dve", cm.get(("tb", fck % 2), 0))] if True else ()),
                    inc="pe")
                d1 = P.emit("vector", lambda v, fck=fck, tc=tc, ypar=ypar: v.scalar_tensor_tensor(
                    ygf[:, fck * TT + tc * 128: fck * TT + (tc + 1) * 128],
                    P.bank_bf(fck % 2)[0:128, 0:128], 1.0, zs[:, fck * 128:(fck + 1) * 128],
                    OP.mult, OP.mult),
                    waits=[("pe", pe1), ("act", az)], inc="dve")
                cm[("tb", fck % 2)] = d1
            cm[("yt", ypar)] = P.counts["pe"]
            cm[("zs",)] = P.counts["dve"]
        dve_ygf = P.counts["dve"]
        for ob in range(8):
            wpar = ob % 2
            wsem = f"D_ow{wpar}"
            wl = P.dma("gpsimd", w_sl[wpar][:, :].rearrange("p (c w) -> p c w", c=32),
                       woT.ap()[:, ob * 256:(ob + 1) * 256].rearrange("(c p) w -> p c w", p=128),
                       wsem, waits=[("pe", cm.get(("ow", wpar), 0))])
            xsem = f"D_oxi{wpar}"
            xl = P.dma("gpsimd", xi[wpar][:, :].rearrange("p (c w) -> p c w", c=2),
                       xin.ap()[ob * 256:(ob + 1) * 256, HALO + tg * TT:HALO + (tg + 1) * TT]
                       .rearrange("(c p) w -> p c w", p=128),
                       xsem, waits=[("dve", cm.get(("oxi", wpar), 0))])
            wv = w_sl[wpar][:, :].rearrange("p (c w) -> p c w", c=32)
            yv = ygf[:, :].rearrange("p (c w) -> p c w", c=32)
            for oc in range(2):
                bk = 4 + (ob % 2) * 2 + oc
                pe_done = mm_group(P, P.bank(bk).ap()[:, :],
                                   [(wv[:, ch, oc * 128:(oc + 1) * 128], yv[:, ch, :]) for ch in range(32)],
                                   waits=[(wsem, wl), ("dve", dve_ygf), cm.get(("bnk", bk), ("dve", 0))])
                d2 = P.emit("vector", lambda v, oc=oc, wpar=wpar, bk=bk: v.tensor_add(
                    xo[wpar][:, oc * TT:(oc + 1) * TT], P.bank(bk).ap()[:, :],
                    xi[wpar][:, oc * TT:(oc + 1) * TT]),
                    waits=[("pe", pe_done), (xsem, xl),
                           (f"D_oxo{wpar}", cm.get(("oxo", wpar), 0))], inc="dve")
                cm[("bnk", bk)] = ("dve", d2)
            cm[("oxi", wpar)] = P.counts["dve"]
            st = P.dma("sync", xout.ap()[ob * 256:(ob + 1) * 256, HALO + tg * TT:HALO + (tg + 1) * TT]
                       .rearrange("(c p) w -> p c w", p=128),
                       xo[wpar][:, :].rearrange("p (c w) -> p c w", c=2),
                       f"D_oxo{wpar}", waits=[P.now("dve")])
            cm[("oxo", wpar)] = st
            cm[("ow", wpar)] = P.counts["pe"]
    P.barrier()


def declare_mamba_tensors(nc, kind_x="ExternalInput", kind_y="ExternalOutput"):
    t = {}
    for nm, shape, dt in CONST_SPECS:
        t[nm] = nc.dram_tensor(nm, shape, dt, kind="ExternalInput")
    t["L0_attn_in_proj_wT"] = nc.dram_tensor("L0_attn_in_proj_wT", [DM, 2 * D_INNER], F32, kind="ExternalInput")
    t["L0_attn_x_proj_wT"] = nc.dram_tensor("L0_attn_x_proj_wT", [D_INNER, 4416], F32, kind="ExternalInput")
    t["L0_attn_dt_proj_wT"] = nc.dram_tensor("L0_attn_dt_proj_wT", [64, 64], F32, kind="ExternalInput")
    t["L0_attn_out_proj_wT"] = nc.dram_tensor("L0_attn_out_proj_wT", [D_INNER, DM], F32, kind="ExternalInput")
    t["L0_attn_conv_wT"] = nc.dram_tensor("L0_attn_conv_wT", [4, D_INNER], F32, kind="ExternalInput")
    t["L0_attn_conv_b"] = nc.dram_tensor("L0_attn_conv_b", [D_INNER], F32, kind="ExternalInput")
    t["L0_attn_dt_proj_b"] = nc.dram_tensor("L0_attn_dt_proj_b", [64], F32, kind="ExternalInput")
    t["L0_attn_A_log"] = nc.dram_tensor("L0_attn_A_log", [64], F32, kind="ExternalInput")
    t["L0_attn_D"] = nc.dram_tensor("L0_attn_D", [64], F32, kind="ExternalInput")
    t["xin"] = nc.dram_tensor("xin", [DM, NTH], F32, kind=kind_x)
    t["xout"] = nc.dram_tensor("xout", [DM, NTH], F32, kind=kind_y)
    t["hln2"] = nc.dram_tensor("hln2", [DM, NTH], BF16)
    t["z_s"] = nc.dram_tensor("z_s", [D_INNER, NT], BF16)
    t["u_s"] = nc.dram_tensor("u_s", [D_INNER, NT], BF16)
    t["xtm_s"] = nc.dram_tensor("xtm_s", [NT, D_INNER], BF16)
    t["y_s"] = nc.dram_tensor("y_s", [NT, D_INNER], BF16)
    t["ccin"] = nc.dram_tensor("ccin", [128, 4096], F32)
    t["ccout"] = nc.dram_tensor("ccout", [256, 4096], F32)
    t["rowd"] = nc.dram_tensor("rowd", [2, 4096], F32)
    return t


def emit_mamba_cols(P, t):
    for k in range(4):
        build_colset(P, f"cw{k}", t["L0_attn_conv_wT"].ap()[k, :], D_INNER)
    build_colset(P, "cb", t["L0_attn_conv_b"], D_INNER)


def build_mamba_program(upto="m7", dbg=()):
    from contextlib import ExitStack
    nc = bass.Bass("TRN2", target_bir_lowering=False)
    ctx = ExitStack()
    t = declare_mamba_tensors(nc)
    P = Prog(nc, ctx)
    with ctx:
        alloc_arenas(P)
        emit_consts(P, t)
        P.barrier()
        emit_mamba_cols(P, t)
        P.barrier()
        emit_mamba(P, t, t["xin"], t["xout"], "m", upto=upto)
        for nm in dbg:
            shp = {"hln2": [DM, NTH], "z_s": [D_INNER, NT], "u_s": [D_INNER, NT],
                   "xtm_s": [NT, D_INNER], "y_s": [NT, D_INNER],
                   "ccout": [256, 4096]}[nm]
            dtp = F32 if nm == "ccout" else BF16
            o = nc.dram_tensor("dump_" + nm, shp, dtp, kind="ExternalOutput")
            st = P.dma("gpsimd", o.ap(), t[nm].ap(), "D_dump")
            P.emit("gpsimd", lambda g: None, waits=[("D_dump", st)], inc="pe")
        P.finish()
    return nc


def host_prep_mamba(inputs):
    d = {}
    d["L0_attn_in_proj_wT"] = np.ascontiguousarray(inputs["L0_attn_in_proj_w"].T)
    d["L0_attn_x_proj_wT"] = np.ascontiguousarray(inputs["L0_attn_x_proj_w"].T)
    d["L0_attn_dt_proj_wT"] = np.ascontiguousarray(inputs["L0_attn_dt_proj_w"].T)
    d["L0_attn_out_proj_wT"] = np.ascontiguousarray(inputs["L0_attn_out_proj_w"].T)
    d["L0_attn_conv_wT"] = np.ascontiguousarray(inputs["L0_attn_conv_w"].T)
    for nm in ["L0_attn_conv_b", "L0_attn_dt_proj_b", "L0_attn_A_log", "L0_attn_D"]:
        d[nm] = inputs[nm]
    return d


def mamba_in_map(prep, prepm, xshard, core):
    m = {nm: np.asarray(prep[nm]) for nm, _, _ in CONST_SPECS if nm != "oddcol"}
    m["oddcol"] = prep["oddcol"][core]
    m.update(prepm)
    m["xin"] = xshard
    return m


def emit_attn1(P, t, xin, xout, tag="a"):
    A = Carve(P.sbufs["arA"], ARENA_A)
    B_ = Carve(P.sbufs["arB"], ARENA_B)
    H = Carve(P.sbufs["arH"], ARENA_H)
    hln = H.take(DC * NT)
    kv = H.take(4 * NT)
    bq = P.sbufs["col_l1bq"]
    bkv = P.sbufs["col_l1bkv"]

    def hw(c, t0, W):
        return hln[:, c * NT + (t0 - HALO): c * NT + (t0 - HALO) + W]

    emit_ln(P, xin, hw, tiles=TILES_MAIN, tag=f"{tag}0", ar1=A, ar2=B_)
    P.barrier()

    A = Carve(P.sbufs["arA"], ARENA_A)
    B_ = Carve(P.sbufs["arB"], ARENA_B)
    w_sl = A.take(DC * TT)
    qg = A.take(4 * NT)
    kdup = B_.take(4 * NT)
    prod = B_.take(NT)
    srow = B_.take(TT, F32)
    zrow = B_.take(TT, F32)
    erow = B_.take(TT, F32)
    rrow = B_.take(TT, F32)
    arow = B_.take(TT, F32)
    tv = B_.take(TT, F32)
    xi = [B_.take(TT, F32), B_.take(TT, F32)]
    xo = [B_.take(TT, F32), B_.take(TT, F32)]
    onescol_b = P.sbufs["c_onescol_b"]
    onesrow_f = P.sbufs["c_onesrow_f"]
    hv = hln.rearrange("p (c w) -> p c w", c=DC)
    kvv = kv.rearrange("p (c w) -> p c w", c=4)
    qv = qg.rearrange("p (c w) -> p c w", c=4)
    Wq, Wkv = t["L1_attn_Wq"], t["L1_attn_Wkv"]
    cm = {}

    def proj_block(wt, c0, dst_view, bias, kind):
        dsem = f"D_w"
        ld = P.dma("gpsimd", w_sl[:, :].rearrange("p (c w) -> p c w", c=DC),
                   wt.ap()[:, c0:c0 + TT].rearrange("(c p) w -> p c w", p=128),
                   dsem, waits=[("pe", P.counts["pe"])])
        wv = w_sl[:, :].rearrange("p (c w) -> p c w", c=DC)
        for ti in range(4):
            t0 = ti * TT
            for oc in range(4):
                bk = (ti * 4 + oc) % 4
                pe_done = mm_group(P, P.bank(bk).ap()[:, :],
                                   [(wv[:, c, oc * 128:(oc + 1) * 128],
                                     hv[:, c, t0:t0 + TT]) for c in range(DC)],
                                   waits=[(dsem, ld), cm.get(("bk", bk), ("dve", 0))])
                d = P.emit("vector", lambda v, bk=bk, oc=oc, t0=t0: v.tensor_scalar(
                    dst_view[:, oc, t0:t0 + TT], P.bank(bk).ap()[:, :],
                    bias[:, (c0 // 128 + oc) if kind == "q" else oc:
                         (c0 // 128 + oc) if kind == "q" else oc][:, 0:0] if False else
                    bias[:, (c0 // 128 + oc) if kind == "q" else oc:
                         ((c0 // 128 + oc) if kind == "q" else oc) + 1],
                    None, OP.add),
                    waits=[("pe", pe_done)], inc="dve")
                cm[("bk", bk)] = ("dve", d)

    proj_block(Wkv, 0, kvv, bkv, "kv")
    kddone = P.dma("gpsimd", kdup.rearrange("p (c w) -> p c w", c=4).__getitem__((slice(64, 128), slice(None), slice(None))),
                   kv.rearrange("p (c w) -> p c w", c=4)[0:64, :, :],
                   f"D_kd", waits=[P.now("dve")])
    kdv = kdup.rearrange("p (c w) -> p c w", c=4)
    for g in range(4):
        proj_block(Wq, g * TT, qv, bq, "q")
        for h in range(8):
            f0 = g * 512 + h * 64
            cq = (h * 64) // 128
            pq = (h * 64) % 128
            ksrc = (kvv[0:64, g, :] if pq == 0 else kdv[64:128, g, :])
            ps = slice(pq, pq + 64)
            dp = P.emit("vector", lambda v, cq=cq, ps=ps, ksrc=ksrc: v.tensor_mul(
                prod[ps, :], qv[ps.start:ps.stop, cq, :], ksrc),
                waits=[("pe", cm.get(("pr",), 0)), (f"D_kd", kddone)], inc="dve")
            for ti in range(4):
                t0 = ti * TT
                bk = 4 + ti % 2
                pe1 = P.emit("tensor", lambda tt, t0=t0, bk=bk, ps=ps: tt.matmul(
                    P.bank(bk).ap()[0:1, :], onescol_b[ps, 0:1], prod[ps, t0:t0 + TT],
                    start=True, stop=True),
                    waits=[("dve", dp), ("dve", cm.get(("sb", bk), 0))], inc="pe")
                d1 = P.emit("vector", lambda v, bk=bk: v.tensor_copy(srow[0:1, :], P.bank(bk).ap()[0:1, :]),
                            waits=[("pe", pe1)], inc="dve")
                cm[("sb", bk)] = d1
                P.emit("vector", lambda v: v.tensor_sub(zrow[0:1, :], srow[0:1, :], srow[0:1, :]),
                       waits=[P.now("act")], inc="dve")
                ae = P.emit("scalar", lambda s: s.activation(erow[0:1, :], zrow[0:1, :], AF.Exp),
                            waits=[P.now("dve")], inc="act")
                dr = P.emit("vector", lambda v: v.reciprocal(rrow[0:1, :], erow[0:1, :]),
                            waits=[("act", ae)], inc="dve")
                da = P.emit("vector", lambda v: v.tensor_mul(arow[0:1, :], erow[0:1, :], rrow[0:1, :]),
                            inc="dve")
                bk2 = 6 + ti % 2
                pe2 = P.emit("tensor", lambda tt, bk2=bk2: tt.matmul(
                    P.bank(bk2).ap()[:, :], onesrow_f[0:1, :], arow[0:1, :],
                    start=True, stop=True),
                    waits=[("dve", da), ("dve", cm.get(("ab", bk2), 0))], inc="pe")
                xp = ti % 2
                xsem = f"D_xi{xp}"
                xl = P.dma("gpsimd", xi[xp][64:128, :],
                           xin.ap()[f0:f0 + 64, HALO + t0:HALO + t0 + TT],
                           xsem, waits=[("dve", cm.get(("xi", xp), 0))])
                d2 = P.emit("vector", lambda v, g=g, bk2=bk2, t0=t0: v.tensor_mul(
                    tv[64:128, :], kvv[64:128, g, t0:t0 + TT], P.bank(bk2).ap()[64:128, :]),
                    waits=[("pe", pe2)], inc="dve")
                cm[("ab", bk2)] = d2
                d3 = P.emit("vector", lambda v, xp=xp: v.tensor_add(
                    xo[xp][64:128, :], tv[64:128, :], xi[xp][64:128, :]),
                    waits=[(xsem, xl), (f"D_xo{xp}", cm.get(("xo", xp), 0))], inc="dve")
                cm[("xi", xp)] = d3
                st = P.dma("sync", xout.ap()[f0:f0 + 64, HALO + t0:HALO + t0 + TT],
                           xo[xp][64:128, :], f"D_xo{xp}", waits=[("dve", d3)])
                cm[("xo", xp)] = st
            cm[("pr",)] = P.counts["pe"]
    P.barrier()


def emit_ffn1(P, t, xin, xout, tag, fold_wd=False):
    A = Carve(P.sbufs["arA"], ARENA_A)
    B_ = Carve(P.sbufs["arB"], ARENA_B)
    H = Carve(P.sbufs["arH"], ARENA_H)
    had = P.sbufs["c_had128"]
    idb, nidb = P.sbufs["c_id128b"], P.sbufs["c_negid128b"]
    su = P.sbufs["col_l1su"]
    wsg = P.sbufs["col_l1wsg"]
    combo = P.sbufs["col_l1combo"]
    wsd = P.sbufs["col_l1wsd"]

    if fold_wd:
        fbA = A.take(64 * 256)
        fbB = B_.take(64 * 256)
        fv = [fbA.rearrange("p (c w) -> p c w", c=64), fbB.rearrange("p (c w) -> p c w", c=64)]
        wdT = t["L1_ffn_W_downT"]
        cm = {}
        for ot in range(8):
            ld = P.dma("gpsimd", fv[0],
                       wdT.ap()[:, ot * 256:(ot + 1) * 256].rearrange("(c p) w -> p c w", p=128),
                       f"D_fw", waits=[("pe", cm.get(("lda",), 0)), P.now("dve")])
            cur = 0
            first_w = [(f"D_fw", ld), P.now("dve")]
            for c in range(64):
                bk = c % 4
                pe1 = P.emit("tensor", lambda tt, src_ap=fv[cur][:, c, :], bk=bk: tt.matmul(
                    P.bank(bk).ap()[:, 0:256], had[:, :], src_ap,
                    start=True, stop=True),
                    waits=(first_w if c == 0 else [P.now("dve")] if c < 4 else
                           [("dve", cm.get(("fb", bk), 0))]), inc="pe")
                d1 = P.emit("vector", lambda v, dst_ap=fv[1 - cur][:, c, :], bk=bk: v.tensor_copy(
                    dst_ap, P.bank(bk).ap()[:, 0:256]),
                    waits=[("pe", pe1)], inc="dve")
                cm[("fb", bk)] = d1
            cur = 1 - cur
            for sbit in range(6):
                bit = 1 << sbit
                for c in range(64):
                    bk = c % 4
                    pa = c & ~bit
                    pb = c | bit
                    sgn = nidb if (c & bit) else idb
                    pe1 = mm_group(P, P.bank(bk).ap()[:, 0:256],
                                   [(idb[:, :], fv[cur][:, pa, :]),
                                    (sgn[:, :], fv[cur][:, pb, :])],
                                   waits=([P.now("dve")] if c < 4 else [("dve", cm.get(("fb", bk), 0))]))
                    d1 = P.emit("vector", lambda v, dst_ap=fv[1 - cur][:, c, :], bk=bk: v.tensor_copy(
                        dst_ap, P.bank(bk).ap()[:, 0:256]),
                        waits=[("pe", pe1)], inc="dve")
                    cm[("fb", bk)] = d1
                cur = 1 - cur
            st = P.dma("sync", t["wdp_s"].ap()[:, ot * 256:(ot + 1) * 256].rearrange("(c p) w -> p c w", p=128),
                       fv[cur], f"D_fo", waits=[P.now("dve")])
            cm[("lda",)] = P.counts["pe"]
        P.barrier()
        A = Carve(P.sbufs["arA"], ARENA_A)
        B_ = Carve(P.sbufs["arB"], ARENA_B)

    hln = H.take(DC * NT)

    def hw(c, t0, W):
        return hln[:, c * NT + (t0 - HALO): c * NT + (t0 - HALO) + W]

    emit_ln(P, xin, hw, su_col=su, tiles=TILES_MAIN, tag=f"{tag}0", ar1=A, ar2=B_)
    P.barrier()
    A = Carve(P.sbufs["arA"], ARENA_A)
    B_ = Carve(P.sbufs["arB"], ARENA_B)
    wbA = A.take(DC * TT)
    wbB = B_.take(DC * TT)
    wv2 = [wbA.rearrange("p (c w) -> p c w", c=DC), wbB.rearrange("p (c w) -> p c w", c=DC)]
    hv = hln.rearrange("p (c w) -> p c w", c=DC)
    cm = {}
    for ti in range(4):
        t0 = ti * TT
        cur = 0
        for c in range(DC):
            bk = c % 4
            pe1 = P.emit("tensor", lambda tt, c=c, t0=t0, bk=bk: tt.matmul(
                P.bank(bk).ap()[:, :], had[:, :], hv[:, c, t0:t0 + TT],
                start=True, stop=True),
                waits=([P.now("dve")] if c < 4 else [("dve", cm.get(("fb", bk), 0))]), inc="pe")
            d1 = P.emit("vector", lambda v, dst_ap=wv2[0][:, c, :], bk=bk: v.tensor_copy(
                dst_ap, P.bank(bk).ap()[:, :]),
                waits=[("pe", pe1)], inc="dve")
            cm[("fb", bk)] = d1
        for sbit in range(4):
            bit = 1 << sbit
            last = sbit == 3
            for c in range(DC):
                bk = c % 4
                pa = c & ~bit
                pb = c | bit
                sgn = nidb if (c & bit) else idb
                pe1 = mm_group(P, P.bank(bk).ap()[:, :],
                               [(idb[:, :], wv2[cur][:, pa, :]),
                                (sgn[:, :], wv2[cur][:, pb, :])],
                               waits=[("dve", cm.get(("fb", bk), 0))])
                dst = (hv[:, c, t0:t0 + TT] if last else wv2[1 - cur][:, c, :])
                d1 = P.emit("vector", lambda v, dst=dst, bk=bk: v.tensor_copy(
                    dst, P.bank(bk).ap()[:, :]),
                    waits=[("pe", pe1)], inc="dve")
                cm[("fb", bk)] = d1
            cur = 1 - cur
    P.barrier()

    A = Carve(P.sbufs["arA"], ARENA_A)
    B_ = Carve(P.sbufs["arB"], ARENA_B)
    wg = [A.take(DC * TT), B_.take(DC * TT)]
    wu = [A.take(DC * TT), B_.take(DC * TT)]
    sg_sb = [A.take(TT), A.take(TT)]
    h_sb = [A.take(TT) for _ in range(4)]
    hmid = t["hmid"]
    wgT, wuT = t["L1_ffn_W_gateT"], t["L1_ffn_W_upT"]
    bank_free = {}
    slot_free = {}
    hs_free = {}
    sg_free = {}
    for ob in range(16):
        wpar = ob % 2
        dsem = f"D_w{wpar}"
        for (wt, dst) in ((wgT, wg[wpar]), (wuT, wu[wpar])):
            ld = P.dma("gpsimd", dst[:, :].rearrange("p (c w) -> p c w", c=DC),
                       wt.ap()[:, ob * TT:(ob + 1) * TT].rearrange("(c p) w -> p c w", p=128),
                       dsem, waits=[("pe", slot_free.get(wpar, 0))])
        for ti in range(4):
            t0 = ti * TT
            for oc in range(4):
                bsel = (ti * 4 + oc) % 2
                bg, bu = P.bank(bsel).ap(), P.bank(2 + bsel).ap()
                og = ob * 4 + oc
                gw = wg[wpar][:, :].rearrange("p (c w) -> p c w", c=DC)
                uw = wu[wpar][:, :].rearrange("p (c w) -> p c w", c=DC)
                mm_group(P, bg[:, :],
                         [(gw[:, c, oc * 128:(oc + 1) * 128], hv[:, c, t0:t0 + TT]) for c in range(DC)],
                         waits=[(dsem, ld), bank_free.get(bsel, ("pe", 0))])
                pe_done = mm_group(P, bu[:, :],
                                   [(uw[:, c, oc * 128:(oc + 1) * 128], hv[:, c, t0:t0 + TT]) for c in range(DC)],
                                   waits=[bank_free.get(2 + bsel, ("pe", 0))])
                sgp = (ti * 4 + oc) % 2
                a = P.emit("scalar", lambda s, bg=bg, og=og, sgp=sgp: s.activation(
                    sg_sb[sgp][:, :], bg[:, :], AF.Silu, scale=wsg[:, og:og + 1]),
                    waits=[("pe", pe_done), ("dve", sg_free.get(sgp, 0))], inc="act")
                bank_free[bsel] = ("act", a)
                hp = (ti * 4 + oc) % 4
                d = P.emit("vector", lambda v, bu=bu, og=og, sgp=sgp, hp=hp: v.scalar_tensor_tensor(
                    h_sb[hp][:, :], sg_sb[sgp][:, :], combo[:, og:og + 1], bu[:, :],
                    OP.mult, OP.mult),
                    waits=[("act", a), (f"D_h{hp}", hs_free.get(hp, 0))], inc="dve")
                bank_free[2 + bsel] = ("dve", d)
                sg_free[sgp] = d
                st = P.dma("sync", hmid.ap()[og * 128:(og + 1) * 128, t0:t0 + TT],
                           h_sb[hp][:, :], f"D_h{hp}", waits=[("dve", d)])
                hs_free[hp] = st
        slot_free[wpar] = P.counts["pe"]
    P.barrier()

    A = Carve(P.sbufs["arA"], ARENA_A)
    B_ = Carve(P.sbufs["arB"], ARENA_B)
    H2 = Carve(P.sbufs["arH"], ARENA_H)
    OBW = 256
    wd = [A.take(64 * OBW), B_.take(64 * OBW)]
    hq = [H2.take(DC * TT), H2.take(DC * TT)]
    xi = [H2.take(4 * TT, F32)]
    xo = [H2.take(4 * TT, F32)]
    wdp = t["wdp_s"]
    bank_free = {}
    slot_free = {}
    xo_free = {}
    xi_free = {}
    hq_free = {}
    for obp in range(4):
        lds = [0, 0]
        for half in range(2):
            ob = obp * 2 + half
            dsem = f"D_wd{half}"
            lds[half] = P.dma("gpsimd", wd[half][:, :].rearrange("p (c w) -> p c w", c=64),
                              wdp.ap()[:, ob * OBW:(ob + 1) * OBW].rearrange("(c p) w -> p c w", p=128),
                              dsem, waits=[("pe", slot_free.get(half, 0))])
        for ti in range(4):
            t0 = HALO + ti * TT
            tb = (ti % 2) * 4
            xsem = f"D_xi0"
            xl = P.dma("gpsimd", xi[0][:, :].rearrange("p (c w) -> p c w", c=4),
                       xin.ap()[obp * 512:(obp + 1) * 512, t0:t0 + TT].rearrange("(c p) w -> p c w", p=128),
                       xsem, waits=[("dve", xi_free.get(0, 0))])
            for kq in range(4):
                hb = kq % 2
                hsem = f"D_hq{hb}"
                hl = P.dma("gpsimd", hq[hb][:, :].rearrange("p (c w) -> p c w", c=DC),
                           hmid.ap()[kq * 2048:(kq + 1) * 2048, t0 - HALO:t0 - HALO + TT]
                           .rearrange("(c p) w -> p c w", p=128),
                           hsem, waits=[("pe", hq_free.get(hb, 0))])
                hqv = hq[hb][:, :].rearrange("p (c w) -> p c w", c=DC)
                for oc in range(4):
                    wdv = wd[oc // 2][:, :].rearrange("p (c w) -> p c w", c=64)
                    occ = oc % 2
                    wts = [(hsem, hl), (f"D_wd{oc // 2}", lds[oc // 2])]
                    if kq == 0:
                        wts += [bank_free.get(tb + oc, ("dve", 0))]
                    for c in range(DC):
                        gk = kq * DC + c
                        P.emit("tensor", lambda t_, gk=gk, occ=occ, c=c, wdv=wdv, hqv=hqv, tb=tb, oc=oc:
                               t_.matmul(P.bank(tb + oc).ap()[:, :],
                                         wdv[:, gk, occ * 128:(occ + 1) * 128], hqv[:, c, :],
                                         start=(gk == 0), stop=(gk == 63)),
                               waits=(wts if c == 0 else ()),
                               inc=("pe" if c == DC - 1 else None))
                hq_free[hb] = P.counts["pe"]
            pe_done = P.counts["pe"]
            for oc in range(4):
                og = obp * 4 + oc
                d = P.emit("vector", lambda v, oc=oc, og=og, tb=tb: v.scalar_tensor_tensor(
                    xo[0][:, oc * TT:(oc + 1) * TT], P.bank(tb + oc).ap()[:, :], wsd[:, og:og + 1],
                    xi[0][:, oc * TT:(oc + 1) * TT], OP.mult, OP.add),
                    waits=[("pe", pe_done), (xsem, xl), (f"D_xo0", xo_free.get(0, 0))],
                    inc="dve")
                bank_free[tb + oc] = ("dve", d)
            xi_free[0] = P.counts["dve"]
            st = P.dma("sync", xout.ap()[obp * 512:(obp + 1) * 512, t0:t0 + TT].rearrange("(c p) w -> p c w", p=128),
                       xo[0][:, :].rearrange("p (c w) -> p c w", c=4),
                       f"D_xo0", waits=[P.now("dve")])
            xo_free[0] = st
        slot_free[0] = P.counts["pe"]
        slot_free[1] = P.counts["pe"]
    P.barrier()


def declare_ffn1_tensors(nc, kind_x="ExternalInput", kind_y="ExternalOutput"):
    t = {}
    for nm, shape, dt in CONST_SPECS:
        t[nm] = nc.dram_tensor(nm, shape, dt, kind="ExternalInput")
    t["L1_ffn_W_upT"] = nc.dram_tensor("L1_ffn_W_upT", [DM, INTER], F32, kind="ExternalInput")
    t["L1_ffn_W_gateT"] = nc.dram_tensor("L1_ffn_W_gateT", [DM, INTER], F32, kind="ExternalInput")
    t["L1_ffn_W_downT"] = nc.dram_tensor("L1_ffn_W_downT", [INTER, DM], F32, kind="ExternalInput")
    t["L1_ffn_SU_ug"] = nc.dram_tensor("L1_ffn_SU_ug", [DM], F32, kind="ExternalInput")
    t["L1_ffn_SU_dp"] = nc.dram_tensor("L1_ffn_SU_dp", [INTER], F32, kind="ExternalInput")
    t["L1_ffn_Wscale_ug"] = nc.dram_tensor("L1_ffn_Wscale_ug", [2 * INTER], F32, kind="ExternalInput")
    t["L1_ffn_Wscale_dp"] = nc.dram_tensor("L1_ffn_Wscale_dp", [DM], F32, kind="ExternalInput")
    t["xin"] = nc.dram_tensor("xin", [DM, NTH], F32, kind=kind_x)
    t["xout"] = nc.dram_tensor("xout", [DM, NTH], F32, kind=kind_y)
    t["hmid"] = nc.dram_tensor("hmid", [INTER, NT], BF16)
    t["wdp_s"] = nc.dram_tensor("wdp_s", [INTER, DM], BF16)
    return t


def emit_ffn1_cols(P, t):
    import math
    s2 = 1.0 / math.sqrt(2048.0)
    s8 = 1.0 / math.sqrt(8192.0)
    build_colset(P, "l1su", t["L1_ffn_SU_ug"], DM)
    build_colset(P, "l1wsg", t["L1_ffn_Wscale_ug"].ap()[INTER:2 * INTER], INTER, post_scale=s2)
    wsu = build_colset(P, "l1wsu", t["L1_ffn_Wscale_ug"].ap()[0:INTER], INTER, post_scale=s2)
    sdp = build_colset(P, "l1sdp", t["L1_ffn_SU_dp"], INTER)
    build_colset(P, "l1wsd", t["L1_ffn_Wscale_dp"], DM, post_scale=0.5 * s8)
    combo = P.sbuf("col_l1combo", [128, 64], F32)
    P.emit("vector", lambda v: v.tensor_mul(combo[:, :], wsu[:, :], sdp[:, :]),
           waits=[P.now("dve"), P.now("pe")], inc="dve")


def declare_attn1_tensors(nc, kind_x="ExternalInput", kind_y="ExternalOutput"):
    t = {}
    for nm, shape, dt in CONST_SPECS:
        t[nm] = nc.dram_tensor(nm, shape, dt, kind="ExternalInput")
    t["L1_attn_Wq"] = nc.dram_tensor("L1_attn_Wq", [DM, DM], F32, kind="ExternalInput")
    t["L1_attn_bq"] = nc.dram_tensor("L1_attn_bq", [DM], F32, kind="ExternalInput")
    t["L1_attn_Wkv"] = nc.dram_tensor("L1_attn_Wkv", [DM, 512], F32, kind="ExternalInput")
    t["L1_attn_bkv"] = nc.dram_tensor("L1_attn_bkv", [512], F32, kind="ExternalInput")
    t["xin"] = nc.dram_tensor("xin", [DM, NTH], F32, kind=kind_x)
    t["xout"] = nc.dram_tensor("xout", [DM, NTH], F32, kind=kind_y)
    return t


def emit_attn1_cols(P, t):
    build_colset(P, "l1bq", t["L1_attn_bq"], DM)
    build_colset(P, "l1bkv", t["L1_attn_bkv"], 512)


def build_ffn1_program(fold=True):
    from contextlib import ExitStack
    nc = bass.Bass("TRN2", target_bir_lowering=False)
    ctx = ExitStack()
    t = declare_ffn1_tensors(nc)
    P = Prog(nc, ctx)
    with ctx:
        alloc_arenas(P)
        emit_consts(P, t)
        P.barrier()
        emit_ffn1_cols(P, t)
        P.barrier()
        emit_ffn1(P, t, t["xin"], t["xout"], "f1", fold_wd=fold)
        P.finish()
    return nc


def build_attn1_program():
    from contextlib import ExitStack
    nc = bass.Bass("TRN2", target_bir_lowering=False)
    ctx = ExitStack()
    t = declare_attn1_tensors(nc)
    P = Prog(nc, ctx)
    with ctx:
        alloc_arenas(P)
        emit_consts(P, t)
        P.barrier()
        emit_attn1_cols(P, t)
        P.barrier()
        emit_attn1(P, t, t["xin"], t["xout"], "a")
        P.finish()
    return nc


def host_prep_ffn1(inputs):
    d = {}
    d["L1_ffn_W_upT"] = np.ascontiguousarray(inputs["L1_ffn_W_up"].T)
    d["L1_ffn_W_gateT"] = np.ascontiguousarray(inputs["L1_ffn_W_gate"].T)
    d["L1_ffn_W_downT"] = np.ascontiguousarray(inputs["L1_ffn_W_down"].T)
    for nm in ["L1_ffn_SU_ug", "L1_ffn_SU_dp", "L1_ffn_Wscale_ug", "L1_ffn_Wscale_dp"]:
        d[nm] = inputs[nm]
    return d


def host_prep_attn1(inputs):
    return {nm: inputs[nm] for nm in ["L1_attn_Wq", "L1_attn_bq", "L1_attn_Wkv", "L1_attn_bkv"]}


def generic_in_map(prep, extra, xshard, core):
    m = {nm: np.asarray(prep[nm]) for nm, _, _ in CONST_SPECS if nm != "oddcol"}
    m["oddcol"] = prep["oddcol"][core]
    m.update(extra)
    m["xin"] = xshard
    return m


def declare_all_tensors(nc):
    t = {}
    for nm, shape, dt in CONST_SPECS:
        t[nm] = nc.dram_tensor(nm, shape, dt, kind="ExternalInput")
    t["L0_ffn_gate_wT"] = nc.dram_tensor("L0_ffn_gate_wT", [DM, INTER], I32, kind="ExternalInput")
    t["L0_ffn_up_wT"] = nc.dram_tensor("L0_ffn_up_wT", [DM, INTER], I32, kind="ExternalInput")
    t["L0_ffn_down_wT"] = nc.dram_tensor("L0_ffn_down_wT", [INTER, DM], I32, kind="ExternalInput")
    t["L0_ffn_gate_scale"] = nc.dram_tensor("L0_ffn_gate_scale", [INTER], F32, kind="ExternalInput")
    t["L0_ffn_up_scale"] = nc.dram_tensor("L0_ffn_up_scale", [INTER], F32, kind="ExternalInput")
    t["L0_ffn_down_scale"] = nc.dram_tensor("L0_ffn_down_scale", [DM], F32, kind="ExternalInput")
    t["L0_attn_in_proj_wT"] = nc.dram_tensor("L0_attn_in_proj_wT", [DM, 2 * D_INNER], F32, kind="ExternalInput")
    t["L0_attn_x_proj_wT"] = nc.dram_tensor("L0_attn_x_proj_wT", [D_INNER, 4416], F32, kind="ExternalInput")
    t["L0_attn_dt_proj_wT"] = nc.dram_tensor("L0_attn_dt_proj_wT", [64, 64], F32, kind="ExternalInput")
    t["L0_attn_out_proj_wT"] = nc.dram_tensor("L0_attn_out_proj_wT", [D_INNER, DM], F32, kind="ExternalInput")
    t["L0_attn_conv_wT"] = nc.dram_tensor("L0_attn_conv_wT", [4, D_INNER], F32, kind="ExternalInput")
    t["L0_attn_conv_b"] = nc.dram_tensor("L0_attn_conv_b", [D_INNER], F32, kind="ExternalInput")
    t["L0_attn_dt_proj_b"] = nc.dram_tensor("L0_attn_dt_proj_b", [64], F32, kind="ExternalInput")
    t["L0_attn_A_log"] = nc.dram_tensor("L0_attn_A_log", [64], F32, kind="ExternalInput")
    t["L0_attn_D"] = nc.dram_tensor("L0_attn_D", [64], F32, kind="ExternalInput")
    t["L1_attn_Wq"] = nc.dram_tensor("L1_attn_Wq", [DM, DM], F32, kind="ExternalInput")
    t["L1_attn_bq"] = nc.dram_tensor("L1_attn_bq", [DM], F32, kind="ExternalInput")
    t["L1_attn_Wkv"] = nc.dram_tensor("L1_attn_Wkv", [DM, 512], F32, kind="ExternalInput")
    t["L1_attn_bkv"] = nc.dram_tensor("L1_attn_bkv", [512], F32, kind="ExternalInput")
    t["L1_ffn_W_upT"] = nc.dram_tensor("L1_ffn_W_upT", [DM, INTER], F32, kind="ExternalInput")
    t["L1_ffn_W_gateT"] = nc.dram_tensor("L1_ffn_W_gateT", [DM, INTER], F32, kind="ExternalInput")
    t["L1_ffn_W_downT"] = nc.dram_tensor("L1_ffn_W_downT", [INTER, DM], F32, kind="ExternalInput")
    t["L1_ffn_SU_ug"] = nc.dram_tensor("L1_ffn_SU_ug", [DM], F32, kind="ExternalInput")
    t["L1_ffn_SU_dp"] = nc.dram_tensor("L1_ffn_SU_dp", [INTER], F32, kind="ExternalInput")
    t["L1_ffn_Wscale_ug"] = nc.dram_tensor("L1_ffn_Wscale_ug", [2 * INTER], F32, kind="ExternalInput")
    t["L1_ffn_Wscale_dp"] = nc.dram_tensor("L1_ffn_Wscale_dp", [DM], F32, kind="ExternalInput")
    t["xin"] = nc.dram_tensor("xin", [DM, NTH], F32, kind="ExternalInput")
    t["xout"] = nc.dram_tensor("xout", [DM, NTH], F32, kind="ExternalOutput")
    t["xA"] = nc.dram_tensor("xA", [DM, NTH], F32)
    t["xB"] = nc.dram_tensor("xB", [DM, NTH], F32)
    t["hmid"] = nc.dram_tensor("hmid", [INTER, NTH], BF16)
    t["wdp_s"] = nc.dram_tensor("wdp_s", [INTER, DM], BF16)
    t["hln2"] = nc.dram_tensor("hln2", [DM, NTH], BF16)
    t["z_s"] = nc.dram_tensor("z_s", [D_INNER, NT], BF16)
    t["u_s"] = nc.dram_tensor("u_s", [D_INNER, NT], BF16)
    t["xtm_s"] = nc.dram_tensor("xtm_s", [NT, D_INNER], BF16)
    t["y_s"] = nc.dram_tensor("y_s", [NT, D_INNER], BF16)
    t["ccin"] = nc.dram_tensor("ccin", [128, 4096], F32)
    t["ccout"] = nc.dram_tensor("ccout", [256, 4096], F32)
    t["rowd"] = nc.dram_tensor("rowd", [2, 4096], F32)
    return t


def build_fused_program():
    from contextlib import ExitStack
    nc = bass.Bass("TRN2", target_bir_lowering=False)
    ctx = ExitStack()
    t = declare_all_tensors(nc)
    P = Prog(nc, ctx)
    with ctx:
        alloc_arenas(P)
        emit_consts(P, t)
        P.barrier()
        build_colset(P, "l0g", t["L0_ffn_gate_scale"], INTER)
        build_colset(P, "l0u", t["L0_ffn_up_scale"], INTER)
        build_colset(P, "l0d", t["L0_ffn_down_scale"], DM, post_scale=0.5)
        emit_mamba_cols(P, t)
        emit_ffn1_cols(P, t)
        emit_attn1_cols(P, t)
        P.barrier()
        emit_ffn0(P, t, t["xin"], t["xA"], "s1")
        emit_mamba(P, t, t["xA"], t["xB"], "s2")
        emit_ffn0(P, t, t["xB"], t["xA"], "s3")
        emit_ffn1(P, t, t["xA"], t["xB"], "s4", fold_wd=True)
        emit_attn1(P, t, t["xB"], t["xA"], "s5")
        emit_ffn1(P, t, t["xA"], t["xout"], "s6", fold_wd=False)
        P.barrier()
        P.finish()
    return nc


def fused_in_map(inputs, prep, prepm, prepf1, prepa1, xshard, core):
    m = {nm: np.asarray(prep[nm]) for nm, _, _ in CONST_SPECS if nm != "oddcol"}
    m["oddcol"] = prep["oddcol"][core]
    for nm in ["L0_ffn_gate_wT", "L0_ffn_up_wT", "L0_ffn_down_wT",
               "L0_ffn_gate_scale", "L0_ffn_up_scale", "L0_ffn_down_scale"]:
        m[nm] = prep[nm]
    m.update(prepm)
    m.update(prepf1)
    m.update(prepa1)
    m["xin"] = xshard
    return m


def kernel_fused(**inputs):
    x = inputs["L0_attn_x"]
    prep = host_prep_common(inputs)
    prepm = host_prep_mamba(inputs)
    prepf1 = host_prep_ffn1(inputs)
    prepa1 = host_prep_attn1(inputs)
    nc = build_fused_program()
    shards = shard_x(x)
    maps = [fused_in_map(inputs, prep, prepm, prepf1, prepa1, shards[k], k)
            for k in range(NCORES)]
    res = run_bass_kernel_spmd(nc, maps, core_ids=list(range(NCORES)))
    return _gather(res)


def kernel(**inputs):
    return kernel_fused(**inputs)
